# version 06
# baseline (speedup 1.0000x reference)
import numpy as np
import ml_dtypes
import concourse.bass as bass
import concourse.mybir as mybir
from concourse.bass_utils import run_bass_kernel_spmd

F32 = mybir.dt.float32
BF16 = mybir.dt.bfloat16
I8 = mybir.dt.int8
AF = mybir.ActivationFunctionType
ALU = mybir.AluOpType
AX = mybir.AxisListType.X

P = 128
C = 2048
NCC = 16
FF = 8192
NFC = 64
T = 1024
HALF = 512
QUAR = 256
NCORES = 8
EPS = 1e-5
CINV = 1.0 / C
UPTO = 4

ENGS = ("sync", "tensor", "vector", "scalar", "gpsimd")

VECS = ["ln1w", "ln1b", "ln2w", "ln2b", "tmk", "tmv", "tmr", "cmk", "cmr",
        "td", "tf", "sdown"]
VID = {v: i for i, v in enumerate(VECS)}

SEM_NAMES = ["qS", "qG", "qA", "qB", "qC", "qD", "qE", "qF", "mm", "ev", "ac", "pl", "r0", "r1", "r2",
             "sA", "sB", "sC", "sD", "sE", "sF"]

NF32 = 21100
NBF = 63800


class Sch:

    def __init__(self):
        self.ops = {e: [] for e in ENGS}
        self.cnt = {}
        self.ctx = None

    def inc(self, n, by=1):
        self.cnt[n] = self.cnt.get(n, 0) + by
        return self.cnt[n]

    def cur(self, n):
        return self.cnt.get(n, 0)

    def on(self, eng, fn, chain=True):
        if chain and eng in ("vector", "scalar"):
            sname = "ev" if eng == "vector" else "ac"
            idx = self.inc(sname)
            sem = self.ctx.sem[sname]

            def wrapped(e, fn=fn, idx=idx, sem=sem):
                if idx > 1:
                    e.wait_ge(sem, idx - 1)
                inst = fn(e)
                inst.then_inc(sem, 1)
                return inst
            self.ops[eng].append(wrapped)
        else:
            self.ops[eng].append(fn)


class Arena:
    def __init__(self, tensor, nelem, name):
        self.t = tensor
        self.n = nelem
        self.off = 0
        self.marks = []
        self.hi = 0
        self.name = name

    def alloc(self, nelem, name=""):
        assert self.off + nelem <= self.n, \
            f"{self.name} arena overflow: {name} needs {nelem} at {self.off}/{self.n}"
        sl = self.t[:, self.off:self.off + nelem]
        self.off += nelem
        self.hi = max(self.hi, self.off)
        return sl

    def push(self):
        self.marks.append(self.off)

    def pop(self):
        self.off = self.marks.pop()


class Ctx:
    pass


def W(ctx, eng, sem_name, val):
    if val > 0:
        ctx.sch.on(eng, lambda e, s=ctx.sem[sem_name], v=val: e.wait_ge(s, v), chain=False)


def Wcur(ctx, eng, names):
    for n in names:
        W(ctx, eng, n, ctx.sch.cur(n))


def milestone(ctx, eng, sem_name):
    t = ctx.sch.inc(sem_name)
    ctx.sch.on(eng, lambda e, s=ctx.sem[sem_name]: e.nop().then_inc(s, 1), chain=False)
    return t


def dma(ctx, eng, qsem, out_ap, in_ap, **kw):
    tgt = ctx.sch.inc(qsem, 16)
    ctx.sch.on(eng, lambda e, o=out_ap, i=in_ap, s=ctx.sem[qsem], kk=kw:
               e.dma_start(out=o, in_=i, **kk).then_inc(s, 16), chain=False)
    return tgt


class Banks:

    def __init__(self, ctx):
        self.ctx = ctx
        self.idxs = [0, 1, 2, 3, 4, 5]
        self.rr = 0

    def acquire(self):
        b = self.idxs[self.rr]
        self.rr = (self.rr + 1) % len(self.idxs)
        bank_wait(self.ctx, b)
        return b

    def free(self, eng, b):
        bank_rec(self.ctx, eng, b)


def bank_wait(ctx, b, eng="tensor"):
    rec = ctx.bankrec.get(b)
    if rec is not None:
        W(ctx, eng, rec[0], rec[1])


def bank_rec(ctx, eng, b):
    s = {"vector": "ev", "scalar": "ac", "tensor": "mm"}[eng]
    ctx.bankrec[b] = (s, ctx.sch.cur(s))


def bank_wait_all(ctx):
    for b in range(8):
        bank_wait(ctx, b)


def bank_free(ctx, eng, b):
    bank_rec(ctx, eng, b)


STORE_SLOTS = ("sA", "sB", "sC", "sD", "sE", "sF")


def arena_fence(ctx):
    for s in ("ev", "ac", "mm"):
        W(ctx, "gpsimd", s, ctx.sch.cur(s))
    for e in ("vector", "scalar"):
        for s in STORE_SLOTS:
            W(ctx, e, s, ctx.sch.cur(s))


def store_barrier(ctx, eng):
    for s in STORE_SLOTS:
        W(ctx, eng, s, ctx.sch.cur(s))


def slab_dma(ctx, slot, out_ap, in_ap):
    marks = ctx.streams.setdefault(slot, [])
    if marks:
        for s, v in marks[-1]:
            W(ctx, "gpsimd", s, v)
    return dma(ctx, "gpsimd", slot, out_ap, in_ap)


def slab_mark(ctx, slot, extra=()):
    ctx.streams.setdefault(slot, []).append((("mm", ctx.sch.cur("mm")),) + tuple(extra))



def build_nc(upto=UPTO):
    nc = bass.Bass("TRN2", target_bir_lowering=False, debug=False)
    ctx = Ctx()
    ctx.nc = nc
    ctx.sch = Sch()
    ctx.sch.ctx = ctx

    dr = {}
    dr["x"] = nc.dram_tensor("x", [C, T], F32, kind="ExternalInput").ap()
    dr["vecbank"] = nc.dram_tensor("vecbank", [P, 16 * len(VECS)], F32, kind="ExternalInput").ap()
    dr["fbank"] = nc.dram_tensor("fbank", [P, 2 * NFC], F32, kind="ExternalInput").ap()
    dr["ident"] = nc.dram_tensor("ident", [P, P], F32, kind="ExternalInput").ap()
    dr["lag"] = nc.dram_tensor("lag", [16, NCORES * P], F32, kind="ExternalInput").ap()
    dr["bmask"] = nc.dram_tensor("bmask", [P, 1], F32, kind="ExternalInput").ap()
    dr["pmask"] = nc.dram_tensor("pmask", [P, 8], F32, kind="ExternalInput").ap()
    dr["wgate"] = nc.dram_tensor("wgate", [NFC, P, NCC, 128], I8, kind="ExternalInput").ap()
    dr["wup"] = nc.dram_tensor("wup", [NFC, P, NCC, 128], I8, kind="ExternalInput").ap()
    dr["wdown"] = nc.dram_tensor("wdown", [FF, C], I8, kind="ExternalInput").ap()
    for nm in ("wkT", "wvT", "wrT", "woT", "wcrT"):
        dr[nm] = nc.dram_tensor(nm, [8, P, NCC, 256], F32, kind="ExternalInput").ap()
    dr["wckT"] = nc.dram_tensor("wckT", [NFC, P, NCC, 128], F32, kind="ExternalInput").ap()
    dr["wcvT"] = nc.dram_tensor("wcvT", [FF, C], F32, kind="ExternalInput").ap()
    dr["y"] = nc.dram_tensor("y", [C, T], F32, kind="ExternalOutput").ap()

    dr["x1sp"] = nc.dram_tensor("x1sp", [C, T], F32).ap()
    dr["z1sp"] = nc.dram_tensor("z1sp", [C, T], F32).ap()
    dr["x2sp"] = nc.dram_tensor("x2sp", [C, T], F32).ap()
    dr["numsp"] = nc.dram_tensor("numsp", [C, T], F32).ap()
    dr["densp"] = nc.dram_tensor("densp", [C, T], F32).ap()
    dr["sigsp"] = nc.dram_tensor("sigsp", [C, T], BF16).ap()
    dr["wcvbf"] = nc.dram_tensor("wcvbf", [FF, C], BF16).ap()
    dr["pay0"] = nc.dram_tensor("pay0", [P, 16], F32).ap()
    dr["gat0"] = nc.dram_tensor("gat0", [NCORES * P, 16], F32).ap()
    dr["pay1"] = nc.dram_tensor("pay1", [64, P], F32).ap()
    dr["gat1"] = nc.dram_tensor("gat1", [NCORES * 64, P], F32).ap()
    dr["pay2"] = nc.dram_tensor("pay2", [P, 16], F32).ap()
    dr["gat2"] = nc.dram_tensor("gat2", [NCORES * P, 16], F32).ap()
    ctx.dr = dr

    import contextlib
    with contextlib.ExitStack() as es:
        AF32 = es.enter_context(nc.sbuf_tensor("af32", [P, NF32], F32))
        ABF = es.enter_context(nc.sbuf_tensor("abf", [P, NBF], BF16))
        ctx.PB = [es.enter_context(nc.psum_tensor(f"pb{i}", [P, 512], F32)) for i in range(8)]
        ctx.sem = {n: es.enter_context(nc.semaphore(n)) for n in SEM_NAMES}
        block = es.enter_context(nc.Block())

        ctx.f32 = Arena(AF32, NF32, "f32")
        ctx.bf = Arena(ABF, NBF, "bf16")
        ctx.bankrec = {}
        ctx.streams = {}
        ctx.banks = Banks(ctx)

        build_program(ctx, upto)

        sch = ctx.sch
        for eng in ENGS:
            getattr(block, eng)(lambda e, ops=tuple(sch.ops[eng]): [f(e) for f in ops])

    import os
    if os.environ.get("KARENA"):
        print(f"arena peaks: f32={ctx.f32.hi} bf={ctx.bf.hi} "
              f"bytes={ctx.f32.hi*4 + ctx.bf.hi*2}")
    return nc



def build_program(ctx, upto):
    sch, sem, dr = ctx.sch, ctx.sem, ctx.dr
    f32a, bfa = ctx.f32, ctx.bf

    ctx.vb = f32a.alloc(16 * len(VECS), "vecbank")
    ctx.fb = f32a.alloc(2 * NFC, "fbank")
    ctx.dv = f32a.alloc(16 * 6, "derived")
    ctx.ramp = f32a.alloc(T, "ramp")
    ctx.rows = f32a.alloc(4 * 1024, "rows")
    ctx.bcA = f32a.alloc(2 * T, "bcastA")
    ctx.bcB = f32a.alloc(3 * T, "bcastB")
    ctx.bmask = f32a.alloc(1, "bmask")
    ctx.pmask = f32a.alloc(8, "pmask")
    ctx.mall = f32a.alloc(16, "m_all")
    ctx.cafm = f32a.alloc(16, "ca_fm")
    ctx.cbfm = f32a.alloc(16, "cb_fm")
    ctx.sumstg = f32a.alloc(64, "sumstg")
    ctx.identb = bfa.alloc(P, "identb")
    onescol = bfa.alloc(1, "ones_col")
    onesrow_r = bfa.alloc(P, "ones_row")
    ctx.ones_bf = onescol
    ctx.onesrow_bf = onesrow_r[0:1, :]
    ctx.hilo = bfa.alloc(2 * 1024, "hilo")

    def VB(name):
        return ctx.vb[:, VID[name] * 16:(VID[name] + 1) * 16]
    ctx.VB = VB
    DV = {n: ctx.dv[:, i * 16:(i + 1) * 16] for i, n in
          enumerate(["w", "d", "eu", "lw", "shalf", "scr"])}
    ctx.DV = DV

    creg = f32a.alloc(2, "consts")
    ctx.nc.const_aps.register(F32, EPS, creg[:, 0:1])
    ctx.nc.const_aps.register(F32, 0.0, creg[:, 1:2])
    sch.on("vector", lambda v, o=creg[:, 0:1]: v.memset(o, EPS))
    sch.on("vector", lambda v, o=creg[:, 1:2]: v.memset(o, 0.0))
    dma(ctx, "sync", "qS", ctx.vb, dr["vecbank"])
    dma(ctx, "sync", "qS", ctx.fb, dr["fbank"])
    dma(ctx, "sync", "qS", ctx.bmask, dr["bmask"])
    dma(ctx, "sync", "qS", ctx.pmask, dr["pmask"])
    dma(ctx, "gpsimd", "qG", ctx.identb, dr["ident"])
    Wcur(ctx, "scalar", ["qS"])
    Wcur(ctx, "scalar", ["ev"])
    sch.on("scalar", lambda s: s.activation(DV["scr"], VB("td"), AF.Exp))
    sch.on("scalar", lambda s: s.mul(DV["w"], DV["scr"], -1.0))
    sch.on("scalar", lambda s: s.activation(DV["d"], DV["w"], AF.Exp))
    sch.on("scalar", lambda s: s.activation(DV["eu"], VB("tf"), AF.Exp))
    sch.on("scalar", lambda s: s.mul(DV["lw"], DV["w"], float(T)))
    sch.on("scalar", lambda s: s.mul(DV["shalf"], VB("sdown"), 0.5))
    sch.on("vector", lambda v: v.memset(ctx.ones_bf, 1.0))
    sch.on("vector", lambda v: v.memset(ctx.onesrow_bf, 1.0))
    sch.inc("pl")
    sch.on("gpsimd", lambda g: g.iota(ctx.ramp, pattern=[[1, T]], base=0, channel_multiplier=0,
                                      allow_small_or_imprecise_dtypes=True).then_inc(sem["pl"], 1))

    ffn_stage(ctx, src=dr["x"], dst=dr["x1sp"], res=dr["x"], do_stats=True,
              debug_y=(upto == 1))
    if upto == 1:
        return finish(ctx)
    attn_mix_stage(ctx, debug_y=(upto == 2))
    if upto == 2:
        return finish(ctx)
    chmix_stage(ctx, debug_y=(upto == 3))
    if upto == 3:
        return finish(ctx)
    ffn_stage(ctx, src=dr["x2sp"], dst=dr["y"], res=dr["x2sp"], do_stats=False,
              debug_y=False)
    return finish(ctx)


def finish(ctx):
    for s in STORE_SLOTS + ("qS", "qG", "qA", "qB", "qC", "qD"):
        W(ctx, "gpsimd", s, ctx.sch.cur(s))



def fmv(dram, cc=None, cols=None):
    v = dram.rearrange("(cc p) t -> p cc t", p=P)
    if cc is None:
        return v
    if cols is None:
        return v[:, cc]
    return v[:, cc, cols]


def ln_stats(ctx, xin_fn, bc, double_ln=False, halves=(0, 1), bcast=True):
    sch, sem = ctx.sch, ctx.sem
    bfa = ctx.bf
    bfa.push()
    xbf = [bfa.alloc(512, "xbf") for _ in range(2)]
    xsq = [bfa.alloc(512, "xsq") for _ in range(2)]
    rows = ctx.rows
    mu_row = rows[0:1, 0:1024]
    msq_row = rows[0:1, 1024:2048]
    var_row = rows[0:1, 2048:3072]
    r_row = rows[0:1, 3072:4096]

    for g in halves:
        cs = slice(g * 512, (g + 1) * 512)
        bank_wait(ctx, 6)
        bank_wait(ctx, 7)
        mmcc = {}
        for cc in range(NCC):
            xa = xin_fn(cc, g)
            sl = cc % 2
            if cc >= 2:
                W(ctx, "scalar", "mm", mmcc[cc - 2])
                W(ctx, "vector", "mm", mmcc[cc - 2])
            a1 = sch.cur("ac") + 1
            sch.on("scalar", lambda s, o=xbf[sl], i=xa: s.copy(o, i))
            e1 = sch.cur("ev") + 1
            sch.on("vector", lambda v, o=xsq[sl], i=xa: v.tensor_mul(o, i, i))
            W(ctx, "tensor", "ac", a1)
            W(ctx, "tensor", "ev", e1)
            sch.on("tensor", lambda t, o=ctx.PB[6][0:1, 0:512], l=ctx.ones_bf, r=xbf[sl],
                   st=(cc == 0), sp=(cc == NCC - 1): t.matmul(o, l, r, start=st, stop=sp))
            mmcc[cc] = sch.inc("mm")
            sch.on("tensor", lambda t, o=ctx.PB[7][0:1, 0:512], l=ctx.ones_bf, r=xsq[sl],
                   st=(cc == 0), sp=(cc == NCC - 1), s=sem["mm"]:
                   t.matmul(o, l, r, start=st, stop=sp).then_inc(s, 1))
        mdone = sch.cur("mm")
        W(ctx, "vector", "mm", mdone)
        sch.on("vector", lambda v, o=mu_row[:, cs], i=ctx.PB[6][0:1, 0:512]:
               v.tensor_scalar(o, i, CINV, None, ALU.mult))
        sch.on("vector", lambda v, o=msq_row[:, cs], i=ctx.PB[7][0:1, 0:512]:
               v.tensor_scalar(o, i, CINV, None, ALU.mult))
        bank_free(ctx, "vector", 6)
        bank_free(ctx, "vector", 7)
        sch.on("vector", lambda v, o=var_row[:, cs], a=mu_row[:, cs]: v.tensor_mul(o, a, a))
        e2 = sch.cur("ev") + 1
        sch.on("vector", lambda v, o=var_row[:, cs], a=msq_row[:, cs], b=var_row[:, cs]:
               v.tensor_sub(o, a, b))
        W(ctx, "scalar", "ev", e2)
        a2 = sch.cur("ac") + 1
        sch.on("scalar", lambda s, o=r_row[:, cs], i=var_row[:, cs]:
               s.activation(o, i, AF.Sqrt, bias=EPS))
        W(ctx, "vector", "ac", a2)
        sch.on("vector", lambda v, o=r_row[:, cs], i=r_row[:, cs]:
               v.reciprocal(o, i))
    bfa.pop()
    if not bcast:
        return
    bcast_row(ctx, mu_row, bc[:, 0:T])
    bcast_row(ctx, r_row, bc[:, T:2 * T])
    if double_ln:
        t1 = msq_row
        sch.on("vector", lambda v, o=t1, a=r_row, b=r_row: v.tensor_mul(o, a, b))
        e3 = sch.cur("ev") + 1
        sch.on("vector", lambda v, o=t1, a=var_row, b=t1: v.tensor_mul(o, a, b))
        W(ctx, "scalar", "ev", e3)
        a3 = sch.cur("ac") + 1
        sch.on("scalar", lambda s, o=t1, i=t1: s.activation(o, i, AF.Sqrt, bias=EPS))
        W(ctx, "vector", "ac", a3)
        sch.on("vector", lambda v, o=t1, i=t1: v.reciprocal(o, i))
        sch.on("vector", lambda v, o=t1, a=r_row, b=t1: v.tensor_mul(o, a, b))
        bcast_row(ctx, t1, bc[:, 2 * T:3 * T])


def bcast_row(ctx, row_f32, dst_f32, n=T):
    sch, sem = ctx.sch, ctx.sem
    hi = ctx.hilo[0:1, 0:1024]
    lo = ctx.hilo[0:1, 1024:2048]
    Wcur(ctx, "scalar", ["ev", "mm"])
    Wcur(ctx, "vector", ["mm"])
    a1 = sch.cur("ac") + 1
    sch.on("scalar", lambda s, o=hi[:, 0:n], i=row_f32[:, 0:n]: s.copy(o, i))
    W(ctx, "vector", "ac", a1)
    e1 = sch.cur("ev") + 1
    sch.on("vector", lambda v, o=lo[:, 0:n], a=row_f32[:, 0:n], b=hi[:, 0:n]:
           v.tensor_sub(o, a, b))
    W(ctx, "tensor", "ev", e1)
    for g in range((n + 511) // 512):
        lo_c = g * 512
        hi_c = min(n, (g + 1) * 512)
        wdt = hi_c - lo_c
        bank_wait(ctx, 7)
        sch.on("tensor", lambda t, o=ctx.PB[7][:, 0:wdt], l=ctx.onesrow_bf, r=hi[:, lo_c:hi_c]:
               t.matmul(o, l, r, start=True, stop=False))
        sch.inc("mm")
        m1 = sch.cur("mm")
        sch.on("tensor", lambda t, o=ctx.PB[7][:, 0:wdt], l=ctx.onesrow_bf, r=lo[:, lo_c:hi_c],
               s=sem["mm"]: t.matmul(o, l, r, start=False, stop=True).then_inc(s, 1))
        W(ctx, "vector", "mm", m1)
        sch.on("vector", lambda v, o=dst_f32[:, lo_c:hi_c], i=ctx.PB[7][:, 0:wdt]:
               v.tensor_copy(o, i))
        bank_free(ctx, "vector", 7)


def proj_stage(ctx, wdram, rhs3, out_emit, windows=4, win=512, sfn="sf3", slabs=None,
               rhs_cols=(0, 1)):
    sch, sem = ctx.sch, ctx.sem
    per = win // 128
    for wi in range(windows):
        buf = slabs[wi % 2]
        slot = "qA" if wi % 2 == 0 else "qB"
        qt = slab_dma(ctx, slot,
                      buf.rearrange("p (kc m) -> p kc m", kc=NCC), wdram[wi])
        W(ctx, "tensor", slot, qt)
        b3 = buf.rearrange("p (kc m) -> p kc m", kc=NCC)
        for mi in range(per):
            mc = wi * per + mi
            for g in rhs_cols:
                b = ctx.banks.acquire()
                for kc in range(NCC):
                    last = (kc == NCC - 1)
                    if last:
                        sch.inc("mm")
                    sch.on("tensor", lambda t, o=ctx.PB[b][:, 0:512],
                           l=b3[:, kc, mi * 128:(mi + 1) * 128],
                           r=rhs3[:, kc, g * 512:(g + 1) * 512],
                           st=(kc == 0), sp=last, fin=last, s=sem["mm"]:
                           (t.matmul(o, l, r, start=st, stop=sp).then_inc(s, 1) if fin
                            else t.matmul(o, l, r, start=st, stop=sp)))
                out_emit(mc, g, b, sch.cur("mm"))
        slab_mark(ctx, slot)



def ffn_stage(ctx, src, dst, res, do_stats, debug_y):
    sch, sem, dr = ctx.sch, ctx.sem, ctx.dr
    f32a, bfa = ctx.f32, ctx.bf
    f32a.push(); bfa.push()
    xhat = bfa.alloc(NCC * HALF, "xhat")
    xh3 = xhat.rearrange("p (cc t) -> p cc t", cc=NCC)

    f32a.push()
    xin = f32a.alloc(NCC * 512, "xin")
    tmp = f32a.alloc(512, "t0")

    def load_half(g):
        Wcur(ctx, "sync", ["ev", "ac"])
        for cc in range(NCC):
            dma(ctx, "sync", "qS", xin[:, cc * 512:(cc + 1) * 512],
                fmv(src, cc, slice(g * 512, (g + 1) * 512)))
        q = sch.cur("qS")
        W(ctx, "scalar", "qS", q)
        W(ctx, "vector", "qS", q)

    store_barrier(ctx, "sync")
    for g in range(2):
        load_half(g)
        ln_stats(ctx, lambda cc, _g: xin[:, cc * 512:(cc + 1) * 512], ctx.bcA, halves=(g,),
                 bcast=(g == 1))

    f32a.pop()

    f32a.push(); bfa.push()
    arena_fence(ctx)
    fuse = bfa.alloc(NFC * HALF, "fuse")
    f3 = fuse.rearrange("p (fc t) -> p fc t", fc=NFC)
    wslabg = [bfa.alloc(NCC * 128, "wslabg") for _ in range(3)]
    wslabu = [bfa.alloc(NCC * 128, "wslabu") for _ in range(3)]
    gsc = [f32a.alloc(HALF, "gsc") for _ in range(2)]
    upsc = [f32a.alloc(HALF, "upsc") for _ in range(2)]
    dslab = [bfa.alloc(2 * C, "dslab") for _ in range(2)]
    xres = [f32a.alloc(HALF, "xres") for _ in range(2)]
    xout = [f32a.alloc(HALF, "xout") for _ in range(2)]

    for h in range(2):
        hs = slice(h * HALF, (h + 1) * HALF)
        W(ctx, "tensor", "mm", sch.cur("mm"))
        Wcur(ctx, "sync", ["ev", "ac"])
        for cc in range(NCC):
            Wcur(ctx, "sync", ["ev"])
            qx = dma(ctx, "sync", "qS", xres[cc % 2], fmv(src, cc, hs))
            W(ctx, "vector", "qS", qx)
            if cc < 2:
                W(ctx, "vector", "mm", sch.cur("mm"))
            sch.on("vector", lambda v, o=gsc[cc % 2], a=xres[cc % 2], b=ctx.bcA[:, 0:T][:, hs]:
                   v.tensor_sub(o, a, b))
            sch.on("vector", lambda v, o=xh3[:, cc, 0:HALF], a=gsc[cc % 2], b=ctx.bcA[:, T:2 * T][:, hs]:
                   v.tensor_mul(o, a, b))
        xhat_done = sch.cur("ev")
        W(ctx, "tensor", "ev", xhat_done)
        for fc in range(NFC):
            bufg = wslabg[fc % 3]
            bufu = wslabu[fc % 3]
            slotg = ("qA", "qB", "qE")[fc % 3]
            slotu = ("qC", "qD", "qF")[fc % 3]
            qtg = slab_dma(ctx, slotg,
                           bufg.rearrange("p (kc m) -> p kc m", kc=NCC), dr["wgate"][fc])
            qtu = slab_dma(ctx, slotu,
                           bufu.rearrange("p (kc m) -> p kc m", kc=NCC), dr["wup"][fc])
            for wsel in ("g", "u"):
                W(ctx, "tensor", slotg if wsel == "g" else slotu,
                  qtg if wsel == "g" else qtu)
                b3 = (bufg if wsel == "g" else bufu).rearrange("p (kc m) -> p kc m", kc=NCC)
                b = ctx.banks.acquire()
                for kc in range(NCC):
                    last = (kc == NCC - 1)
                    if last:
                        sch.inc("mm")
                    sch.on("tensor", lambda t, o=ctx.PB[b][:, 0:512],
                           l=b3[:, kc, 0:128],
                           r=xh3[:, kc, 0:HALF], st=(kc == 0), sp=last, fin=last, s=sem["mm"]:
                           (t.matmul(o, l, r, start=st, stop=sp).then_inc(s, 1) if fin
                            else t.matmul(o, l, r, start=st, stop=sp)))
                last_mm = sch.cur("mm")
                if wsel == "g":
                    W(ctx, "scalar", "mm", last_mm)
                    W(ctx, "scalar", "ev", sch.cur("ev"))
                    sch.on("scalar", lambda s, o=gsc[fc % 2], i=ctx.PB[b][:, 0:512],
                           sc=ctx.fb[:, fc:fc + 1]:
                           s.activation(o, i, AF.Silu, scale=sc))
                    ctx.banks.free("scalar", b)
                else:
                    W(ctx, "vector", "mm", last_mm)
                    sch.on("vector", lambda v, o=upsc[fc % 2], i=ctx.PB[b][:, 0:512],
                           sc=ctx.fb[:, NFC + fc:NFC + fc + 1]:
                           v.tensor_scalar(o, i, sc, None, ALU.mult))
                    ctx.banks.free("vector", b)
                    W(ctx, "vector", "ac", sch.cur("ac"))
                    sch.on("vector", lambda v, o=f3[:, fc, 0:HALF],
                           a=gsc[fc % 2], b2=upsc[fc % 2]:
                           v.tensor_mul(o, a, b2))
            slab_mark(ctx, slotg)
            slab_mark(ctx, slotu)

        for cg in range(2):
            ccs = range(cg * 8, (cg + 1) * 8)
            gq = h * HALF
            bank_wait_all(ctx)
            Wcur(ctx, "tensor", ["ev", "ac"])
            for fp2 in range(NFC // 2):
                buf = dslab[fp2 % 2]
                slot = "qA" if fp2 % 2 == 0 else "qB"
                qt = slab_dma(ctx, slot,
                              buf.rearrange("p (f c) -> p f c", f=2),
                              fmv(dr["wdown"])[:, 2 * fp2:2 * fp2 + 2])
                W(ctx, "tensor", slot, qt)
                for fi in range(2):
                    fc = 2 * fp2 + fi
                    for cc in ccs:
                        pb = ctx.PB[cc % 8][:, 0:512]
                        last = (cc == ccs[-1])
                        if last:
                            sch.inc("mm")
                        sch.on("tensor", lambda t, o=pb, l=buf[:, (fi * NCC + cc) * 128:(fi * NCC + cc + 1) * 128],
                               r=f3[:, fc, 0:HALF], st=(fc == 0), sp=(fc == NFC - 1), fin=last, s=sem["mm"]:
                               (t.matmul(o, l, r, start=st, stop=sp).then_inc(s, 1) if fin
                                else t.matmul(o, l, r, start=st, stop=sp)))
                slab_mark(ctx, slot)
            mdone = sch.cur("mm")
            for cc in ccs:
                xr = xres[cc % 2]
                ot = xout[cc % 2]
                Wcur(ctx, "sync", ["ev"])
                qx = dma(ctx, "sync", "qS", xr, fmv(res, cc, hs))
                W(ctx, "vector", "qS", qx)
                W(ctx, "vector", "mm", mdone)
                sslot = "sA" if cc % 2 == 0 else "sB"
                W(ctx, "vector", sslot, sch.cur(sslot))
                pb = ctx.PB[cc % 8][:, 0:512]
                sch.on("vector", lambda v, o=ot, i=pb, sc=ctx.DV["shalf"][:, cc:cc + 1], x=xr:
                       v.scalar_tensor_tensor(o, i, sc, x, ALU.mult, ALU.add))
                bank_free(ctx, "vector", cc % 8)
                W(ctx, "sync", "ev", sch.cur("ev"))
                dma(ctx, "sync", sslot, fmv(dst, cc, hs), ot)
                if debug_y:
                    dma(ctx, "sync", sslot, fmv(dr["y"], cc, hs), ot)
    f32a.pop(); bfa.pop()
    f32a.pop(); bfa.pop()

    if do_stats:
        f32a.push()
        xin2 = f32a.alloc(NCC * 512, "xin2")
        colb = f32a.alloc(16, "colb")
        store_barrier(ctx, "sync")
        for g in range(2):
            Wcur(ctx, "sync", ["ev", "ac"])
            for cc in range(NCC):
                dma(ctx, "sync", "qS", xin2[:, cc * 512:(cc + 1) * 512],
                    fmv(dst, cc, slice(g * 512, (g + 1) * 512)))
            q = sch.cur("qS")
            W(ctx, "scalar", "qS", q)
            W(ctx, "vector", "qS", q)
            if g == 1:
                for cc in range(NCC):
                    if cc == NCC - 1:
                        sch.on("vector", lambda v, o=colb[:, cc:cc + 1],
                               i=xin2[:, cc * 512 + 511:cc * 512 + 512]:
                               v.tensor_copy(o, i))
                    else:
                        sch.on("vector", lambda v, o=colb[:, cc:cc + 1],
                               i=xin2[:, cc * 512 + 511:cc * 512 + 512]: v.tensor_copy(o, i))
                send_col_round(ctx, colb, "pay0", "gat0", "r0")
            ln_stats(ctx, lambda cc, _g: xin2[:, cc * 512:(cc + 1) * 512], ctx.bcB,
                     double_ln=(g == 1), halves=(g,), bcast=(g == 1))
        f32a.pop()


def send_col_round(ctx, colb, pay, gat, rsem):
    sch, sem, dr = ctx.sch, ctx.sem, ctx.dr
    Wcur(ctx, "gpsimd", ["ev"])
    qt2 = dma(ctx, "gpsimd", "qG", dr[pay], colb)
    W(ctx, "gpsimd", "qG", qt2)
    sch.inc(rsem)
    sch.on("gpsimd", lambda g, i=dr[pay], o=dr[gat], s=sem[rsem]:
           g.collective_compute("AllGather", ALU.bypass,
                                replica_groups=[list(range(NCORES))],
                                ins=[i], outs=[o]).then_inc(s, 1))
    W(ctx, "vector", "qG", ctx.sch.cur("qG"))
    W(ctx, "scalar", "qG", ctx.sch.cur("qG"))


def recv_col(ctx, gat, rsem, dst_col, tmp16):
    sch, sem, dr = ctx.sch, ctx.sem, ctx.dr
    f32a = ctx.f32
    f32a.push()
    arena_fence(ctx)
    allb = f32a.alloc(8 * 16, "allb")
    a3 = allb.rearrange("p (j cc) -> p j cc", j=8)
    W(ctx, "gpsimd", rsem, sch.cur(rsem))
    qg = dma(ctx, "gpsimd", "qG", a3, dr[gat].rearrange("(j p) cc -> p j cc", j=8))
    W(ctx, "vector", "qG", qg)
    sch.on("vector", lambda v, o=dst_col, a=a3[:, 0], b=ctx.pmask[:, 0:1]:
           v.tensor_scalar(o, a, b, None, ALU.mult))
    for j in range(1, 8):
        sch.on("vector", lambda v, o=tmp16, a=a3[:, j], b=ctx.pmask[:, j:j + 1]:
               v.tensor_scalar(o, a, b, None, ALU.mult))
        sch.on("vector", lambda v, o=dst_col, a=dst_col, b=tmp16: v.tensor_add(o, a, b))
    f32a.pop()


def boundary_ln(ctx, xb, out_col, wname, bname, double_ln):
    sch, sem = ctx.sch, ctx.sem
    f32a, bfa = ctx.f32, ctx.bf
    f32a.push(); bfa.push()
    xbb = bfa.alloc(16, "xbb")
    xbs = bfa.alloc(16, "xbs")
    r0 = ctx.rows[0:1, 0:8]
    rb = f32a.alloc(4, "rbb")
    tmpc = f32a.alloc(16, "tmpc")
    Wcur(ctx, "scalar", ["ev"])
    a1 = sch.cur("ac") + 1
    sch.on("scalar", lambda s, o=xbb, i=xb: s.copy(o, i))
    e1 = sch.cur("ev") + 1
    sch.on("vector", lambda v, o=xbs, i=xb: v.tensor_mul(o, i, i))
    W(ctx, "tensor", "ac", a1)
    W(ctx, "tensor", "ev", e1)
    bank_wait(ctx, 6)
    sch.on("tensor", lambda t, o=ctx.PB[6][0:1, 0:16], l=ctx.ones_bf, r=xbb:
           t.matmul(o, l, r, start=True, stop=True))
    sch.inc("mm")
    m1 = sch.cur("mm")
    sch.on("tensor", lambda t, o=ctx.PB[6][0:1, 16:32], l=ctx.ones_bf, r=xbs, s=sem["mm"]:
           t.matmul(o, l, r, start=True, stop=True).then_inc(s, 1))
    W(ctx, "vector", "mm", m1)
    sch.on("vector", lambda v, o=r0[:, 0:1], i=ctx.PB[6][0:1, 0:16]: v.reduce_sum(o, i, axis=AX))
    sch.on("vector", lambda v, o=r0[:, 1:2], i=ctx.PB[6][0:1, 16:32]: v.reduce_sum(o, i, axis=AX))
    bank_free(ctx, "vector", 6)
    sch.on("vector", lambda v, o=r0[:, 0:1], i=r0[:, 0:1]: v.tensor_scalar(o, i, CINV, None, ALU.mult))
    sch.on("vector", lambda v, o=r0[:, 1:2], i=r0[:, 1:2]: v.tensor_scalar(o, i, CINV, None, ALU.mult))
    sch.on("vector", lambda v, o=r0[:, 2:3], a=r0[:, 0:1], b=r0[:, 0:1]: v.tensor_mul(o, a, b))
    e2 = sch.cur("ev") + 1
    sch.on("vector", lambda v, o=r0[:, 1:2], a=r0[:, 1:2], b=r0[:, 2:3]:
           v.tensor_sub(o, a, b))
    W(ctx, "scalar", "ev", e2)
    a2 = sch.cur("ac") + 1
    sch.on("scalar", lambda s, o=r0[:, 2:3], i=r0[:, 1:2]:
           s.activation(o, i, AF.Sqrt, bias=EPS))
    W(ctx, "vector", "ac", a2)
    sch.on("vector", lambda v, o=r0[:, 2:3], i=r0[:, 2:3]: v.reciprocal(o, i))
    if double_ln:
        sch.on("vector", lambda v, o=r0[:, 3:4], a=r0[:, 2:3], b=r0[:, 2:3]: v.tensor_mul(o, a, b))
        e3 = sch.cur("ev") + 1
        sch.on("vector", lambda v, o=r0[:, 3:4], a=r0[:, 1:2], b=r0[:, 3:4]:
               v.tensor_mul(o, a, b))
        W(ctx, "scalar", "ev", e3)
        a3 = sch.cur("ac") + 1
        sch.on("scalar", lambda s, o=r0[:, 3:4], i=r0[:, 3:4]:
               s.activation(o, i, AF.Sqrt, bias=EPS))
        W(ctx, "vector", "ac", a3)
        sch.on("vector", lambda v, o=r0[:, 3:4], i=r0[:, 3:4]: v.reciprocal(o, i))
        sch.on("vector", lambda v, o=r0[:, 2:3], a=r0[:, 2:3], b=r0[:, 3:4]: v.tensor_mul(o, a, b))
    e4 = sch.cur("ev")
    hi = ctx.hilo[0:1, 0:4]
    lo = ctx.hilo[0:1, 4:8]
    W(ctx, "scalar", "mm", sch.cur("mm"))
    W(ctx, "vector", "mm", sch.cur("mm"))
    W(ctx, "scalar", "ev", e4)
    a4 = sch.cur("ac") + 1
    sch.on("scalar", lambda s, o=hi, i=r0[:, 0:4]: s.copy(o, i))
    W(ctx, "vector", "ac", a4)
    e5 = sch.cur("ev") + 1
    sch.on("vector", lambda v, o=lo, a=r0[:, 0:4], b=hi: v.tensor_sub(o, a, b))
    W(ctx, "tensor", "ev", e5)
    bank_wait(ctx, 6)
    sch.on("tensor", lambda t, o=ctx.PB[6][:, 0:4], l=ctx.onesrow_bf, r=hi:
           t.matmul(o, l, r, start=True, stop=False))
    sch.inc("mm")
    m2 = sch.cur("mm")
    sch.on("tensor", lambda t, o=ctx.PB[6][:, 0:4], l=ctx.onesrow_bf, r=lo, s=sem["mm"]:
           t.matmul(o, l, r, start=False, stop=True).then_inc(s, 1))
    W(ctx, "vector", "mm", m2)
    sch.on("vector", lambda v, o=rb, i=ctx.PB[6][:, 0:4]: v.tensor_copy(o, i))
    bank_free(ctx, "vector", 6)
    sch.on("vector", lambda v, o=tmpc, i=xb, s1=rb[:, 0:1], s2=rb[:, 2:3]:
           v.tensor_scalar(o, i, s1, s2, ALU.subtract, ALU.mult))
    sch.on("vector", lambda v, o=tmpc, a=tmpc, b=ctx.VB(wname): v.tensor_mul(o, a, b))
    sch.on("vector", lambda v, o=tmpc, a=tmpc, b=ctx.VB(bname): v.tensor_add(o, a, b))
    sch.on("vector", lambda v, o=out_col, i=tmpc, s1=ctx.bmask[:, 0:1]:
           v.tensor_scalar(o, i, s1, None, ALU.mult))
    f32a.pop(); bfa.pop()



def attn_mix_stage(ctx, debug_y):
    sch, sem, dr = ctx.sch, ctx.sem, ctx.dr
    f32a, bfa = ctx.f32, ctx.bf
    f32a.push(); bfa.push()
    Hb = bfa.alloc(NCC * 1025, "H")
    h3 = Hb.rearrange("p (cc t) -> p cc t", cc=NCC)
    bfa.push()
    XK = bfa.alloc(NCC * T, "XK")
    XV = bfa.alloc(NCC * T, "XV")
    xk3 = XK.rearrange("p (cc t) -> p cc t", cc=NCC)
    xv3 = XV.rearrange("p (cc t) -> p cc t", cc=NCC)

    mu1 = ctx.bcB[:, 0:T]
    r1b = ctx.bcB[:, T:2 * T]
    rr1 = ctx.bcB[:, 2 * T:3 * T]

    f32a.push()
    x1h = f32a.alloc(NCC * 512, "x1h")
    tmp = f32a.alloc(1024, "tmpA")
    store_barrier(ctx, "sync")
    for g in range(2):
        cs = slice(g * 512, (g + 1) * 512)
        Wcur(ctx, "sync", ["ev"])
        for cc in range(NCC):
            dma(ctx, "sync", "qS", x1h[:, cc * 512:(cc + 1) * 512],
                fmv(dr["x1sp"], cc, cs))
        W(ctx, "vector", "qS", sch.cur("qS"))
        for cc in range(NCC):
            sch.on("vector", lambda v, o=tmp[:, 0:512], a=x1h[:, cc * 512:(cc + 1) * 512],
                   b=mu1[:, cs]: v.tensor_sub(o, a, b))
            sch.on("vector", lambda v, o=tmp[:, 0:512], a=tmp[:, 0:512], b=rr1[:, cs]:
                   v.tensor_mul(o, a, b))
            sch.on("vector", lambda v, o=h3[:, cc, 1 + g * 512:1 + (g + 1) * 512], i=tmp[:, 0:512],
                   s1=ctx.VB("ln1w")[:, cc:cc + 1], s2=ctx.VB("ln1b")[:, cc:cc + 1]:
                   v.tensor_scalar(o, i, s1, s2, ALU.mult, ALU.add))
    dlt = ctx.hilo[:, 0:1024]
    Wcur(ctx, "vector", ["ev"])
    for cc in range(NCC):
        for buf3, vec in ((xk3, "tmk"), (xv3, "tmv")):
            sch.on("vector", lambda v, o=dlt[:, 0:1023], a=h3[:, cc, 2:1025], b=h3[:, cc, 1:1024]:
                   v.tensor_sub(o, a, b))
            sch.on("vector", lambda v, o=dlt[:, 0:1023], i=dlt[:, 0:1023],
                   s1=ctx.VB(vec)[:, cc:cc + 1]: v.tensor_scalar(o, i, s1, None, ALU.mult))
            sch.on("vector", lambda v, o=buf3[:, cc, 1:1024], a=dlt[:, 0:1023],
                   b=h3[:, cc, 1:1024]: v.tensor_add(o, a, b))
    lerpA_done = sch.cur("ev")
    xb = f32a.alloc(16, "xb")
    hb = f32a.alloc(16, "hb")
    recv_col(ctx, "gat0", "r0", xb, tmp[:, 0:16])
    boundary_ln(ctx, xb, hb, "ln1w", "ln1b", double_ln=True)
    for cc in range(NCC):
        sch.on("vector", lambda v, o=h3[:, cc, 0:1], i=hb[:, cc:cc + 1]:
               v.tensor_copy(o, i))
    for cc in range(NCC):
        for buf3, vec in ((xk3, "tmk"), (xv3, "tmv")):
            sch.on("vector", lambda v, o=dlt[:, 0:1], a=h3[:, cc, 1:2], b=h3[:, cc, 0:1]:
                   v.tensor_sub(o, a, b))
            sch.on("vector", lambda v, o=dlt[:, 0:1], i=dlt[:, 0:1],
                   s1=ctx.VB(vec)[:, cc:cc + 1]: v.tensor_scalar(o, i, s1, None, ALU.mult))
            sch.on("vector", lambda v, o=buf3[:, cc, 0:1], a=dlt[:, 0:1], b=h3[:, cc, 0:1]:
                   v.tensor_add(o, a, b))
    lerps_done = sch.cur("ev")
    f32a.pop()

    bfa.push(); f32a.push()
    arena_fence(ctx)
    kvr = [bfa.alloc(NCC * 256, "kv_slab") for _ in range(2)]
    Kd = f32a.alloc(1024, "Kd")
    Vd = f32a.alloc(1024, "Vd")
    EB = f32a.alloc(1024, "EBs")
    EA = f32a.alloc(1024, "EAs")
    PAs = f32a.alloc(1025, "PAs")
    PBs = f32a.alloc(1025, "PBs")
    NUM = f32a.alloc(1024, "NUMs")
    DEN = f32a.alloc(1024, "DENs")
    Dts = f32a.alloc(1024, "Dts")
    negm = f32a.alloc(1, "negm")
    for dw in range(8):
        qk = slab_dma(ctx, "qA",
                      kvr[0].rearrange("p (kc m) -> p kc m", kc=NCC), dr["wkT"][dw])
        qv = slab_dma(ctx, "qB",
                      kvr[1].rearrange("p (kc m) -> p kc m", kc=NCC), dr["wvT"][dw])
        W(ctx, "tensor", "qA", qk)
        W(ctx, "tensor", "qB", qv)
        b3k = kvr[0].rearrange("p (kc m) -> p kc m", kc=NCC)
        b3v = kvr[1].rearrange("p (kc m) -> p kc m", kc=NCC)
        for di in range(2):
            dc = dw * 2 + di
            for b3, src3, dst in ((b3k, xk3, Kd), (b3v, xv3, Vd)):
                for g in (1, 0):
                    W(ctx, "tensor", "ev", lerpA_done if g == 1 else lerps_done)
                    b = ctx.banks.acquire()
                    for kc in range(NCC):
                        last = (kc == NCC - 1)
                        if last:
                            sch.inc("mm")
                        sch.on("tensor", lambda t, o=ctx.PB[b][:, 0:512],
                               l=b3[:, kc, di * 128:(di + 1) * 128],
                               r=src3[:, kc, g * 512:(g + 1) * 512],
                               st=(kc == 0), sp=last, fin=last, s=sem["mm"]:
                               (t.matmul(o, l, r, start=st, stop=sp).then_inc(s, 1) if fin
                                else t.matmul(o, l, r, start=st, stop=sp)))
                    mt = sch.cur("mm")
                    W(ctx, "vector", "mm", mt)
                    W(ctx, "vector", "ac", sch.cur("ac"))
                    sch.on("vector", lambda v, o=dst[:, g * 512:(g + 1) * 512], i=ctx.PB[b][:, 0:512]:
                           v.tensor_copy(o, i))
                    ctx.banks.free("vector", b)
            sch.on("vector", lambda v, o=ctx.mall[:, dc:dc + 1], i=Kd: v.reduce_max(o, i, axis=AX))
            e1 = sch.cur("ev") + 1
            sch.on("vector", lambda v, o=negm, i=ctx.mall[:, dc:dc + 1]:
                   v.tensor_scalar(o, i, -1.0, None, ALU.mult))
            W(ctx, "scalar", "ev", e1)
            a2 = sch.cur("ac") + 1
            sch.on("scalar", lambda s, o=EB, i=Kd, bi=negm:
                   s.activation(o, i, AF.Exp, bias=bi))
            W(ctx, "vector", "ac", a2)
            sch.on("vector", lambda v, o=Dts, i=EB, s1=ctx.DV["d"][:, dc:dc + 1]:
                   v.tensor_scalar(o, i, 0.0, s1, ALU.mult, ALU.add))
            sch.on("vector", lambda v, o=EA, a=EB, b=Vd: v.tensor_mul(o, a, b))
            sch.on("vector", lambda v, o=PAs[:, 0:1]: v.memset(o, 0.0))
            sch.on("vector", lambda v, o=PBs[:, 0:1]: v.memset(o, 0.0))
            sch.on("vector", lambda v, o=PAs[:, 1:1025], d0=Dts, d1=EA:
                   v.tensor_tensor_scan(o, d0, d1, 0.0, ALU.mult, ALU.add))
            sch.on("vector", lambda v, o=PBs[:, 1:1025], d0=Dts, d1=EB:
                   v.tensor_tensor_scan(o, d0, d1, 0.0, ALU.mult, ALU.add))
            sch.on("vector", lambda v, o=NUM, i=EA, s1=ctx.DV["eu"][:, dc:dc + 1], x=PAs[:, 0:1024]:
                   v.scalar_tensor_tensor(o, i, s1, x, ALU.mult, ALU.add))
            sch.on("vector", lambda v, o=DEN, i=EB, s1=ctx.DV["eu"][:, dc:dc + 1], x=PBs[:, 0:1024]:
                   v.scalar_tensor_tensor(o, i, s1, x, ALU.mult, ALU.add))
            sch.on("vector", lambda v, o=ctx.sumstg[:, dc:dc + 1], i=PAs[:, 1024:1025]:
                   v.tensor_copy(o, i))
            sch.on("vector", lambda v, o=ctx.sumstg[:, 16 + dc:17 + dc], i=PBs[:, 1024:1025]:
                   v.tensor_copy(o, i))
            evn = sch.cur("ev")
            W(ctx, "sync", "ev", evn)
            dma(ctx, "sync", "sC", dr["numsp"].rearrange("(dc p) t -> p dc t", p=P)[:, dc], NUM)
            dma(ctx, "sync", "sD", dr["densp"].rearrange("(dc p) t -> p dc t", p=P)[:, dc], DEN)
            W(ctx, "vector", "sC", sch.cur("sC"))
            W(ctx, "vector", "sD", sch.cur("sD"))
        slab_mark(ctx, "qA")
        slab_mark(ctx, "qB")

    sch.on("vector", lambda v, o=ctx.sumstg[:, 32:48], i=ctx.mall[:, 0:16]: v.tensor_copy(o, i))
    sch.on("vector", lambda v, o=ctx.sumstg[:, 48:64], i=ctx.DV["lw"]:
           v.tensor_copy(o, i))

    payt = f32a.alloc(128, "payt")
    hi = ctx.hilo[:, 0:64]
    lo = ctx.hilo[:, 64:128]
    Wcur(ctx, "scalar", ["ev", "mm"])
    Wcur(ctx, "vector", ["mm"])
    a3 = sch.cur("ac") + 1
    sch.on("scalar", lambda s, o=hi, i=ctx.sumstg: s.copy(o, i))
    W(ctx, "vector", "ac", a3)
    e5 = sch.cur("ev") + 1
    sch.on("vector", lambda v, o=lo, a=ctx.sumstg, b=hi: v.tensor_sub(o, a, b))
    W(ctx, "tensor", "ev", e5)
    bank_wait(ctx, 6)
    sch.on("tensor", lambda t, o=ctx.PB[6][0:64, 0:128], l=hi, r=ctx.identb:
           t.matmul(o, l, r, start=True, stop=False))
    sch.inc("mm")
    m3 = sch.cur("mm")
    sch.on("tensor", lambda t, o=ctx.PB[6][0:64, 0:128], l=lo, r=ctx.identb, s=sem["mm"]:
           t.matmul(o, l, r, start=False, stop=True).then_inc(s, 1))
    W(ctx, "vector", "mm", m3)
    e6 = sch.cur("ev") + 1
    sch.on("vector", lambda v, o=payt[0:64, :], i=ctx.PB[6][0:64, 0:128]:
           v.tensor_copy(o, i))
    bank_free(ctx, "vector", 6)
    W(ctx, "sync", "ev", e6)
    qp = dma(ctx, "sync", "qS", dr["pay1"], payt[0:64, :])
    W(ctx, "gpsimd", "qS", qp)
    sch.inc("r1")
    sch.on("gpsimd", lambda g, i=dr["pay1"], o=dr["gat1"], s=sem["r1"]:
           g.collective_compute("AllGather", ALU.bypass,
                                replica_groups=[list(range(NCORES))],
                                ins=[i], outs=[o]).then_inc(s, 1))
    f32a.pop()
    bfa.pop()
    bfa.pop()

    bfa.push()
    arena_fence(ctx)
    XR = bfa.alloc(NCC * T, "XR")
    xr3 = XR.rearrange("p (cc t) -> p cc t", cc=NCC)
    SR = bfa.alloc(NCC * T, "SR")
    rslab = [bfa.alloc(NCC * 256, "r_slab") for _ in range(2)]
    dlt2 = ctx.hilo[:, 0:1024]
    Wcur(ctx, "vector", ["ev"])
    for cc in range(NCC):
        sch.on("vector", lambda v, o=dlt2, a=h3[:, cc, 1:1025], b=h3[:, cc, 0:1024]:
               v.tensor_sub(o, a, b))
        sch.on("vector", lambda v, o=dlt2, i=dlt2, s1=ctx.VB("tmr")[:, cc:cc + 1]:
               v.tensor_scalar(o, i, s1, None, ALU.mult))
        sch.on("vector", lambda v, o=xr3[:, cc], a=dlt2, b=h3[:, cc, 0:1024]:
               v.tensor_add(o, a, b))
    W(ctx, "tensor", "ev", sch.cur("ev"))

    f32a.push()
    arena_fence(ctx)
    SAt = f32a.alloc(8 * 128, "SAt")
    SBt = f32a.alloc(8 * 128, "SBt")
    Mt = f32a.alloc(8 * 128, "Mt")
    LWt = f32a.alloc(8 * 128, "LWt")
    CF = f32a.alloc(8 * 128, "CF")
    lagT = f32a.alloc(8 * 128, "lagT")
    W(ctx, "gpsimd", "r1", sch.cur("r1"))
    dma(ctx, "gpsimd", "qG", lagT[0:16, :], dr["lag"])
    g1 = dr["gat1"].rearrange("(j r dc) q -> r dc j q", j=8, r=4)
    for r, dstt in ((0, SAt), (1, SBt), (2, Mt), (3, LWt)):
        dma(ctx, "gpsimd", "qG", dstt[0:16, :].rearrange("d (j q) -> d j q", j=8), g1[r])
    W(ctx, "vector", "qG", sch.cur("qG"))
    sch.on("vector", lambda v, o=CF[0:16, :], a=LWt[0:16, :], b=lagT[0:16, :]: v.tensor_mul(o, a, b))
    e7 = sch.cur("ev") + 1
    sch.on("vector", lambda v, o=CF[0:16, :], a=CF[0:16, :], b=Mt[0:16, :]:
           v.tensor_add(o, a, b))
    W(ctx, "scalar", "ev", e7)
    a5 = sch.cur("ac") + 1
    sch.on("scalar", lambda s, o=CF[0:16, :], i=CF[0:16, :]:
           s.activation(o, i, AF.Exp))
    W(ctx, "vector", "ac", a5)
    sch.on("vector", lambda v, o=SAt[0:16, :], a=CF[0:16, :], b=SAt[0:16, :]: v.tensor_mul(o, a, b))
    sch.on("vector", lambda v, o=SBt[0:16, :], a=CF[0:16, :], b=SBt[0:16, :]: v.tensor_mul(o, a, b))
    for dstt in (SAt, SBt):
        for j in range(1, 8):
            sch.on("vector", lambda v, o=dstt[0:16, 0:128], a=dstt[0:16, 0:128],
                   b=dstt[0:16, j * 128:(j + 1) * 128]: v.tensor_add(o, a, b))
    e8 = sch.cur("ev")
    for srcr, dstf in ((SAt, ctx.cafm), (SBt, ctx.cbfm)):
        hi2 = ctx.hilo[0:16, 128:256]
        lo2 = ctx.hilo[0:16, 256:384]
        W(ctx, "scalar", "mm", sch.cur("mm"))
        W(ctx, "vector", "mm", sch.cur("mm"))
        W(ctx, "scalar", "ev", e8)
        a6 = sch.cur("ac") + 1
        sch.on("scalar", lambda s, o=hi2, i=srcr[0:16, 0:128]: s.copy(o, i))
        W(ctx, "vector", "ac", a6)
        e9 = sch.cur("ev") + 1
        sch.on("vector", lambda v, o=lo2, a=srcr[0:16, 0:128], b=hi2:
               v.tensor_sub(o, a, b))
        W(ctx, "tensor", "ev", e9)
        bank_wait(ctx, 6)
        sch.on("tensor", lambda t, o=ctx.PB[6][:, 0:16], l=hi2, r=ctx.identb[0:16, 0:16]:
               t.matmul(o, l, r, start=True, stop=False))
        sch.inc("mm")
        m4 = sch.cur("mm")
        sch.on("tensor", lambda t, o=ctx.PB[6][:, 0:16], l=lo2, r=ctx.identb[0:16, 0:16], s=sem["mm"]:
               t.matmul(o, l, r, start=False, stop=True).then_inc(s, 1))
        W(ctx, "vector", "mm", m4)
        sch.on("vector", lambda v, o=dstf, i=ctx.PB[6][:, 0:16]:
               v.tensor_copy(o, i))
        bank_free(ctx, "vector", 6)
    f32a.pop()

    f32a.push()
    nrc = [f32a.alloc(1024, "nrc") for _ in range(2)]
    drc = [f32a.alloc(1024, "drc") for _ in range(2)]
    ALPH = f32a.alloc(1024, "ALPH")
    store_barrier(ctx, "sync")
    comb_ev = sch.cur("ev")
    comb_ac = sch.cur("ac")
    Wcur(ctx, "vector", ["pl"])
    for wi in range(8):
        buf = rslab[wi % 2]
        slot = "qA" if wi % 2 == 0 else "qB"
        qt = slab_dma(ctx, slot,
                      buf.rearrange("p (kc m) -> p kc m", kc=NCC), dr["wrT"][wi])
        W(ctx, "tensor", slot, qt)
        b3 = buf.rearrange("p (kc m) -> p kc m", kc=NCC)
        for mi in range(2):
            mc = wi * 2 + mi
            for g in range(2):
                b = ctx.banks.acquire()
                for kc in range(NCC):
                    last = (kc == NCC - 1)
                    if last:
                        sch.inc("mm")
                    sch.on("tensor", lambda t, o=ctx.PB[b][:, 0:512],
                           l=b3[:, kc, mi * 128:(mi + 1) * 128],
                           r=xr3[:, kc, g * 512:(g + 1) * 512],
                           st=(kc == 0), sp=last, fin=last, s=sem["mm"]:
                           (t.matmul(o, l, r, start=st, stop=sp).then_inc(s, 1) if fin
                            else t.matmul(o, l, r, start=st, stop=sp)))
                mt = sch.cur("mm")
                W(ctx, "scalar", "mm", mt)
                sch.on("scalar", lambda s, o=SR[:, mc * T + g * 512: mc * T + (g + 1) * 512],
                       i=ctx.PB[b][:, 0:512]:
                       s.activation(o, i, AF.Sigmoid))
                ctx.banks.free("scalar", b)
        slab_mark(ctx, slot)
        for dc in (wi * 2, wi * 2 + 1):
            W(ctx, "sync", "ev", sch.cur("ev"))
            dma(ctx, "sync", "qS", nrc[dc % 2], dr["numsp"].rearrange("(dc p) t -> p dc t", p=P)[:, dc])
            dma(ctx, "sync", "qS", drc[dc % 2], dr["densp"].rearrange("(dc p) t -> p dc t", p=P)[:, dc])
            qn = sch.cur("qS")
            W(ctx, "vector", "qS", qn)
            if dc == 0:
                W(ctx, "vector", "ev", comb_ev)
                W(ctx, "vector", "ac", comb_ac)
            sch.on("vector", lambda v, o=ALPH, i=ctx.ramp, s1=ctx.DV["w"][:, dc:dc + 1],
                   s2=ctx.mall[:, dc:dc + 1]: v.tensor_scalar(o, i, s1, s2, ALU.mult, ALU.subtract))
            e10 = sch.cur("ev")
            W(ctx, "scalar", "ev", e10)
            a7 = sch.cur("ac") + 1
            sch.on("scalar", lambda s, o=ALPH, i=ALPH: s.activation(o, i, AF.Exp))
            W(ctx, "vector", "ac", a7)
            sch.on("vector", lambda v, o=nrc[dc % 2], i=ALPH, s1=ctx.cafm[:, dc:dc + 1], x=nrc[dc % 2]:
                   v.scalar_tensor_tensor(o, i, s1, x, ALU.mult, ALU.add))
            sch.on("vector", lambda v, o=drc[dc % 2], i=ALPH, s1=ctx.cbfm[:, dc:dc + 1], x=drc[dc % 2]:
                   v.scalar_tensor_tensor(o, i, s1, x, ALU.mult, ALU.add))
            sch.on("vector", lambda v, o=drc[dc % 2], i=drc[dc % 2]: v.reciprocal(o, i))
            sch.on("vector", lambda v, o=nrc[dc % 2], a=nrc[dc % 2], b=drc[dc % 2]: v.tensor_mul(o, a, b))
            sch.on("vector", lambda v, o=SR[:, dc * T:(dc + 1) * T], a=nrc[dc % 2],
                   b=SR[:, dc * T:(dc + 1) * T]: v.tensor_mul(o, a, b))
    rwkv_done = sch.cur("ev")
    f32a.pop()

    f32a.push()
    arena_fence(ctx)
    rw3 = SR.rearrange("p (dc t) -> p dc t", dc=NCC)
    z1t = f32a.alloc(512, "z1t")
    x1q = f32a.alloc(512, "x1q")
    oslab = rslab
    W(ctx, "tensor", "ev", rwkv_done)

    def wo_emit(mc, g, b, mt):
        cs = slice(g * 512, (g + 1) * 512)
        Wcur(ctx, "sync", ["ev"])
        qx = dma(ctx, "sync", "qS", x1q, fmv(dr["x1sp"], mc, cs))
        W(ctx, "vector", "qS", qx)
        W(ctx, "vector", "mm", mt)
        W(ctx, "vector", "sC", sch.cur("sC"))
        sch.on("vector", lambda v, o=z1t, a=x1q, b=mu1[:, cs]: v.tensor_sub(o, a, b))
        sch.on("vector", lambda v, o=z1t, a=z1t, b=r1b[:, cs]: v.tensor_mul(o, a, b))
        sch.on("vector", lambda v, o=z1t, a=z1t, b=ctx.PB[b][:, 0:512]:
               v.tensor_add(o, a, b))
        ctx.banks.free("vector", b)
        W(ctx, "sync", "ev", sch.cur("ev"))
        dma(ctx, "sync", "sC", fmv(dr["z1sp"], mc, cs), z1t)
        if debug_y:
            dma(ctx, "sync", "sC", fmv(dr["y"], mc, cs), z1t)

    proj_stage(ctx, dr["woT"], rw3, wo_emit, windows=8, win=256, sfn="sf3", slabs=oslab)
    f32a.pop()
    bfa.pop()
    bfa.pop()

    f32a.push()
    xin2 = f32a.alloc(NCC * 512, "z1re")
    colb = f32a.alloc(16, "colb2")
    store_barrier(ctx, "sync")
    for g in range(2):
        Wcur(ctx, "sync", ["ev", "ac"])
        for cc in range(NCC):
            dma(ctx, "sync", "qS", xin2[:, cc * 512:(cc + 1) * 512],
                fmv(dr["z1sp"], cc, slice(g * 512, (g + 1) * 512)))
        q = sch.cur("qS")
        W(ctx, "scalar", "qS", q)
        W(ctx, "vector", "qS", q)
        if g == 1:
            for cc in range(NCC):
                if cc == NCC - 1:
                    sch.on("vector", lambda v, o=colb[:, cc:cc + 1],
                           i=xin2[:, cc * 512 + 511:cc * 512 + 512]:
                           v.tensor_copy(o, i))
                else:
                    sch.on("vector", lambda v, o=colb[:, cc:cc + 1],
                           i=xin2[:, cc * 512 + 511:cc * 512 + 512]: v.tensor_copy(o, i))
            send_col_round(ctx, colb, "pay2", "gat2", "r2")
        ln_stats(ctx, lambda cc, _g: xin2[:, cc * 512:(cc + 1) * 512], ctx.bcA, halves=(g,),
                 bcast=(g == 1))
    f32a.pop()
    f32a.pop()



def chmix_stage(ctx, debug_y):
    sch, sem, dr = ctx.sch, ctx.sem, ctx.dr
    f32a, bfa = ctx.f32, ctx.bf
    f32a.push(); bfa.push()
    XK2 = bfa.alloc(NCC * T, "XK2")
    xk23 = XK2.rearrange("p (cc t) -> p cc t", cc=NCC)
    bfa.push()
    XR2 = bfa.alloc(NCC * T, "XR2")
    xr23 = XR2.rearrange("p (cc t) -> p cc t", cc=NCC)

    mu2 = ctx.bcA[:, 0:T]
    r2b = ctx.bcA[:, T:2 * T]

    bfa.push()
    Gb = bfa.alloc(NCC * 1025, "G")
    g3 = Gb.rearrange("p (cc t) -> p cc t", cc=NCC)
    f32a.push()
    z1h = f32a.alloc(NCC * 512, "z1h")
    tmp = f32a.alloc(1024, "tmpC")
    store_barrier(ctx, "sync")
    for g in range(2):
        cs = slice(g * 512, (g + 1) * 512)
        Wcur(ctx, "sync", ["ev"])
        for cc in range(NCC):
            dma(ctx, "sync", "qS", z1h[:, cc * 512:(cc + 1) * 512], fmv(dr["z1sp"], cc, cs))
        W(ctx, "vector", "qS", sch.cur("qS"))
        for cc in range(NCC):
            sch.on("vector", lambda v, o=tmp[:, 0:512], a=z1h[:, cc * 512:(cc + 1) * 512],
                   b=mu2[:, cs]: v.tensor_sub(o, a, b))
            sch.on("vector", lambda v, o=tmp[:, 0:512], a=tmp[:, 0:512], b=r2b[:, cs]:
                   v.tensor_mul(o, a, b))
            sch.on("vector", lambda v, o=g3[:, cc, 1 + g * 512:1 + (g + 1) * 512], i=tmp[:, 0:512],
                   s1=ctx.VB("ln2w")[:, cc:cc + 1], s2=ctx.VB("ln2b")[:, cc:cc + 1]:
                   v.tensor_scalar(o, i, s1, s2, ALU.mult, ALU.add))
    zb = f32a.alloc(16, "zb")
    gb = f32a.alloc(16, "gb")
    recv_col(ctx, "gat2", "r2", zb, tmp[:, 0:16])
    boundary_ln(ctx, zb, gb, "ln2w", "ln2b", double_ln=False)
    W(ctx, "vector", "ev", sch.cur("ev"))
    for cc in range(NCC):
        sch.on("vector", lambda v, o=g3[:, cc, 0:1], i=gb[:, cc:cc + 1]:
               v.tensor_copy(o, i))
    dlt = ctx.hilo[:, 0:1024]
    Wcur(ctx, "vector", ["ev"])
    for cc in range(NCC):
        for buf3, vec in ((xk23, "cmk"), (xr23, "cmr")):
            sch.on("vector", lambda v, o=dlt, a=g3[:, cc, 1:1025], b=g3[:, cc, 0:1024]:
                   v.tensor_sub(o, a, b))
            sch.on("vector", lambda v, o=dlt, i=dlt, s1=ctx.VB(vec)[:, cc:cc + 1]:
                   v.tensor_scalar(o, i, s1, None, ALU.mult))
            sch.on("vector", lambda v, o=buf3[:, cc], a=dlt, b=g3[:, cc, 0:1024]:
                   v.tensor_add(o, a, b))
    lerp2 = sch.cur("ev")
    f32a.pop()
    bfa.pop()

    bfa.push()
    arena_fence(ctx)
    sgt = bfa.alloc(512, "sigt")
    crslab = [bfa.alloc(NCC * 256, "cr_slab") for _ in range(2)]
    W(ctx, "tensor", "ev", lerp2)

    def sig_emit(mc, g, b, mt):
        cs = slice(g * 512, (g + 1) * 512)
        W(ctx, "scalar", "mm", mt)
        W(ctx, "scalar", "sC", sch.cur("sC"))
        sch.on("scalar", lambda s, o=sgt, i=ctx.PB[b][:, 0:512]:
               s.activation(o, i, AF.Sigmoid))
        ctx.banks.free("scalar", b)
        W(ctx, "sync", "ac", sch.cur("ac"))
        dma(ctx, "sync", "sC", dr["sigsp"].rearrange("(cc p) t -> p cc t", p=P)[:, mc, cs], sgt)

    proj_stage(ctx, dr["wcrT"], xr23, sig_emit, windows=8, win=256, sfn="sf3", slabs=crslab)
    bfa.pop()
    bfa.pop()

    bfa.push()
    arena_fence(ctx)
    KF = bfa.alloc(NFC * HALF, "KF")
    kf3 = KF.rearrange("p (fc t) -> p fc t", fc=NFC)
    ckslab = [bfa.alloc(NCC * 128, "ck_slab") for _ in range(2)]
    rel = ctx.f32.alloc(512, "relscr")
    vslab = [bfa.alloc(C, "v_slab") for _ in range(2)]
    sgq = bfa.alloc(HALF, "sgq")
    f32a.push()
    x1q = [f32a.alloc(HALF, "x1q") for _ in range(2)]
    z1q = [f32a.alloc(HALF, "z1q") for _ in range(2)]
    x2t = [f32a.alloc(HALF, "x2t") for _ in range(2)]
    first_pass = [True]

    for h in range(2):
        hs = slice(h * HALF, (h + 1) * HALF)

        def kf_emit(mc, g_unused, b, mt, h=h):
            W(ctx, "scalar", "mm", mt)
            W(ctx, "scalar", "ev", sch.cur("ev"))
            a1 = sch.cur("ac") + 1
            sch.on("scalar", lambda s, o=rel, i=ctx.PB[b][:, 0:512]:
                   s.activation(o, i, AF.Relu))
            ctx.banks.free("scalar", b)
            W(ctx, "vector", "ac", a1)
            sch.on("vector", lambda v, o=kf3[:, mc, 0:HALF], a=rel, b=rel:
                   v.tensor_mul(o, a, b))

        xk_h = XK2.rearrange("p (cc t) -> p cc t", cc=NCC)

        def kf_rhs3(_):
            return None
        for fw in range(NFC):
            buf = ckslab[fw % 2]
            slot = "qA" if fw % 2 == 0 else "qB"
            qt = slab_dma(ctx, slot,
                          buf.rearrange("p (kc m) -> p kc m", kc=NCC), dr["wckT"][fw])
            W(ctx, "tensor", slot, qt)
            b3 = buf.rearrange("p (kc m) -> p kc m", kc=NCC)
            for fi in range(1):
                fc = fw
                b = ctx.banks.acquire()
                for kc in range(NCC):
                    last = (kc == NCC - 1)
                    if last:
                        sch.inc("mm")
                    sch.on("tensor", lambda t, o=ctx.PB[b][:, 0:512],
                           l=b3[:, kc, 0:128],
                           r=xk_h[:, kc, hs], st=(kc == 0), sp=last, fin=last, s=sem["mm"]:
                           (t.matmul(o, l, r, start=st, stop=sp).then_inc(s, 1) if fin
                            else t.matmul(o, l, r, start=st, stop=sp)))
                kf_emit(fc, 0, b, sch.cur("mm"))
            slab_mark(ctx, slot)
        kf_done = sch.cur("ev")

        for cg in range(2):
            ccs = range(cg * 8, (cg + 1) * 8)
            gq = h * HALF
            bank_wait_all(ctx)
            W(ctx, "tensor", "ev", kf_done)
            for fc in range(NFC):
                buf = vslab[fc % 2]
                slot = "qA" if fc % 2 == 0 else "qB"
                if first_pass[0]:
                    qt = slab_dma(ctx, slot, buf, fmv(dr["wcvT"], fc))
                    W(ctx, "sync", slot, qt)
                    cslot = "sE" if fc % 2 == 0 else "sF"
                    cw = (cslot, dma(ctx, "sync", cslot, fmv(dr["wcvbf"], fc), buf))
                else:
                    qt = slab_dma(ctx, slot, buf, fmv(dr["wcvbf"], fc))
                    cw = None
                W(ctx, "tensor", slot, qt)
                for cc in ccs:
                    pb = ctx.PB[cc % 8][:, 0:512]
                    last = (cc == ccs[-1])
                    if last:
                        sch.inc("mm")
                    sch.on("tensor", lambda t, o=pb, l=buf[:, cc * 128:(cc + 1) * 128],
                           r=kf3[:, fc, 0:HALF], st=(fc == 0), sp=(fc == NFC - 1), fin=last, s=sem["mm"]:
                           (t.matmul(o, l, r, start=st, stop=sp).then_inc(s, 1) if fin
                            else t.matmul(o, l, r, start=st, stop=sp)))
                slab_mark(ctx, slot, extra=((cw,) if cw else ()))
            first_pass[0] = False
            mdone = sch.cur("mm")
            for cc in ccs:
                Wcur(ctx, "sync", ["ev"])
                dma(ctx, "sync", "qS", x1q[cc % 2], fmv(dr["x1sp"], cc, hs))
                dma(ctx, "sync", "qS", z1q[cc % 2], fmv(dr["z1sp"], cc, hs))
                dma(ctx, "sync", "qS", sgq,
                    dr["sigsp"].rearrange("(cc p) t -> p cc t", p=P)[:, cc, hs])
                W(ctx, "vector", "qS", sch.cur("qS"))
                W(ctx, "vector", "mm", mdone)
                sslot = "sA" if cc % 2 == 0 else "sB"
                W(ctx, "vector", sslot, sch.cur(sslot))
                pb = ctx.PB[cc % 8][:, 0:512]
                ot = x2t[cc % 2]
                sch.on("vector", lambda v, o=ot, a=sgq, b=pb: v.tensor_mul(o, a, b))
                bank_free(ctx, "vector", cc % 8)
                sch.on("vector", lambda v, o=ot, a=ot, b=x1q[cc % 2]: v.tensor_add(o, a, b))
                sch.on("vector", lambda v, o=ot, a=ot, b=z1q[cc % 2]: v.tensor_add(o, a, b))
                W(ctx, "sync", "ev", sch.cur("ev"))
                dma(ctx, "sync", sslot, fmv(dr["x2sp"], cc, hs), ot)
                if debug_y:
                    dma(ctx, "sync", sslot, fmv(dr["y"], cc, hs), ot)
    f32a.pop()
    bfa.pop()
    bfa.pop()
    f32a.pop()



def kernel(**inputs):
    upto = inputs.pop("_upto", UPTO)
    d = inputs
    xT = np.ascontiguousarray(np.asarray(d["L0_attn_x"])[0].T.astype(np.float32))

    def tr(a):
        return np.ascontiguousarray(np.asarray(a, dtype=np.float32).T)

    def tile4(w2d, win):
        m = w2d.shape[1]
        return np.ascontiguousarray(
            w2d.reshape(NCC, P, m // win, win).transpose(2, 1, 0, 3))

    wkT = tile4(tr(d["L0_attn_Wk"]), 256)
    wvT = tile4(tr(d["L0_attn_Wv"]), 256)
    wrT = tile4(tr(d["L0_attn_Wr"]), 256)
    woT = tile4(tr(d["L0_attn_Wo"]), 256)
    wcrT = tile4(tr(d["L0_attn_Wcr"]), 256)
    wckT = tile4(tr(d["L0_attn_Wck"]), 128)
    wcvT = tr(d["L0_attn_Wcv"])
    wgate = tile4(np.ascontiguousarray(np.asarray(d["L0_ffn_w_gate"], np.int8)), 128)
    wup = tile4(np.ascontiguousarray(np.asarray(d["L0_ffn_w_up"], np.int8)), 128)
    wdown = np.ascontiguousarray(np.asarray(d["L0_ffn_w_down"], np.int8))

    def fm(v):
        return np.ascontiguousarray(np.asarray(v, np.float32).reshape(16, 128).T)

    vecs = [d["L0_attn_ln1_w"], d["L0_attn_ln1_b"], d["L0_attn_ln2_w"], d["L0_attn_ln2_b"],
            np.asarray(d["L0_attn_tm_k"]).reshape(-1), np.asarray(d["L0_attn_tm_v"]).reshape(-1),
            np.asarray(d["L0_attn_tm_r"]).reshape(-1),
            np.asarray(d["L0_attn_cm_k"]).reshape(-1), np.asarray(d["L0_attn_cm_r"]).reshape(-1),
            d["L0_attn_time_decay"], d["L0_attn_time_first"], d["L0_ffn_s_down"]]
    vecbank = np.hstack([fm(v) for v in vecs]).astype(np.float32)
    fbank = np.hstack([np.ascontiguousarray(np.asarray(v, np.float32).reshape(64, 128).T)
                       for v in (d["L0_ffn_s_gate"], d["L0_ffn_s_up"])]).astype(np.float32)
    ident = np.eye(P, dtype=np.float32)

    in_maps = []
    for c in range(NCORES):
        lag = np.full((16, NCORES * P), 1e6, dtype=np.float32)
        for j in range(c):
            lag[:, j * P:(j + 1) * P] = float(c - 1 - j)
        bm = np.full((P, 1), 0.0 if c == 0 else 1.0, dtype=np.float32)
        pm = np.zeros((P, 8), dtype=np.float32)
        if c > 0:
            pm[:, c - 1] = 1.0
        m = {"x": np.ascontiguousarray(xT[:, c * T:(c + 1) * T]),
             "vecbank": vecbank, "fbank": fbank, "ident": ident,
             "lag": lag, "bmask": bm, "pmask": pm,
             "wgate": wgate, "wup": wup, "wdown": wdown,
             "wkT": wkT, "wvT": wvT, "wrT": wrT, "woT": woT, "wcrT": wcrT,
             "wckT": wckT, "wcvT": wcvT}
        in_maps.append(m)

    nc = build_nc(upto=upto)
    res = run_bass_kernel_spmd(nc, in_maps, core_ids=list(range(NCORES)))
    out = np.empty((8192, 2048), dtype=np.float32)
    for c in range(NCORES):
        out[c * T:(c + 1) * T, :] = res.results[c]["y"].T
    return out.reshape(1, 8192, 2048)


# version 08
# speedup vs baseline: 1.1964x; 1.1964x over previous
import numpy as np
import ml_dtypes
import concourse.bass as bass
import concourse.mybir as mybir
from concourse.bass_utils import run_bass_kernel_spmd

F32 = mybir.dt.float32
BF16 = mybir.dt.bfloat16
I8 = mybir.dt.int8
AF = mybir.ActivationFunctionType
ALU = mybir.AluOpType
AX = mybir.AxisListType.X

P = 128
C = 2048
NCC = 16
FF = 8192
NFC = 64
T = 1024
HALF = 512
QUAR = 256
NCORES = 8
EPS = 1e-5
CINV = 1.0 / C
UPTO = 4

ENGS = ("sync", "tensor", "vector", "scalar", "gpsimd")

VECS = ["ln1w", "ln1b", "ln2w", "ln2b", "tmk", "tmv", "tmr", "cmk", "cmr",
        "td", "tf", "sdown"]
VID = {v: i for i, v in enumerate(VECS)}

SEM_NAMES = ["qS", "qG", "qA", "qB", "qC", "qD", "qE", "qF", "mm", "ev", "ac", "pl", "r0", "r1", "r2",
             "sA", "sB", "sC", "sD", "sE", "sF"]

NF32 = 16100
NBF = 74200


class Sch:

    def __init__(self):
        self.ops = {e: [] for e in ENGS}
        self.cnt = {}
        self.ctx = None

    def inc(self, n, by=1):
        self.cnt[n] = self.cnt.get(n, 0) + by
        return self.cnt[n]

    def cur(self, n):
        return self.cnt.get(n, 0)

    def on(self, eng, fn, chain=True):
        if chain and eng in ("vector", "scalar"):
            sname = "ev" if eng == "vector" else "ac"
            idx = self.inc(sname)
            sem = self.ctx.sem[sname]

            def wrapped(e, fn=fn, idx=idx, sem=sem):
                if idx > 1:
                    e.wait_ge(sem, idx - 1)
                inst = fn(e)
                inst.then_inc(sem, 1)
                return inst
            self.ops[eng].append(wrapped)
        else:
            self.ops[eng].append(fn)


class Arena:
    def __init__(self, tensor, nelem, name):
        self.t = tensor
        self.n = nelem
        self.off = 0
        self.marks = []
        self.hi = 0
        self.name = name

    def alloc(self, nelem, name=""):
        assert self.off + nelem <= self.n, \
            f"{self.name} arena overflow: {name} needs {nelem} at {self.off}/{self.n}"
        sl = self.t[:, self.off:self.off + nelem]
        self.off += nelem
        self.hi = max(self.hi, self.off)
        return sl

    def push(self):
        self.marks.append(self.off)

    def pop(self):
        self.off = self.marks.pop()


class Ctx:
    pass


def W(ctx, eng, sem_name, val):
    if val > 0:
        ctx.sch.on(eng, lambda e, s=ctx.sem[sem_name], v=val: e.wait_ge(s, v), chain=False)


def Wcur(ctx, eng, names):
    for n in names:
        W(ctx, eng, n, ctx.sch.cur(n))


def milestone(ctx, eng, sem_name):
    t = ctx.sch.inc(sem_name)
    ctx.sch.on(eng, lambda e, s=ctx.sem[sem_name]: e.nop().then_inc(s, 1), chain=False)
    return t


def dma(ctx, eng, qsem, out_ap, in_ap, **kw):
    tgt = ctx.sch.inc(qsem, 16)
    ctx.sch.on(eng, lambda e, o=out_ap, i=in_ap, s=ctx.sem[qsem], kk=kw:
               e.dma_start(out=o, in_=i, **kk).then_inc(s, 16), chain=False)
    return tgt


class Banks:

    def __init__(self, ctx):
        self.ctx = ctx
        self.idxs = [0, 1, 2, 3, 4, 5]
        self.rr = 0

    def acquire(self):
        b = self.idxs[self.rr]
        self.rr = (self.rr + 1) % len(self.idxs)
        bank_wait(self.ctx, b)
        return b

    def free(self, eng, b):
        bank_rec(self.ctx, eng, b)


def bank_wait(ctx, b, eng="tensor"):
    rec = ctx.bankrec.get(b)
    if rec is not None:
        W(ctx, eng, rec[0], rec[1])


def bank_rec(ctx, eng, b):
    s = {"vector": "ev", "scalar": "ac", "tensor": "mm"}[eng]
    ctx.bankrec[b] = (s, ctx.sch.cur(s))


def bank_wait_all(ctx):
    for b in range(8):
        bank_wait(ctx, b)


def bank_free(ctx, eng, b):
    bank_rec(ctx, eng, b)


STORE_SLOTS = ("sA", "sB", "sC", "sD", "sE", "sF")


def arena_fence(ctx):
    for s in ("ev", "ac", "mm"):
        W(ctx, "sync", s, ctx.sch.cur(s))
    for e in ("vector", "scalar"):
        for s in STORE_SLOTS:
            W(ctx, e, s, ctx.sch.cur(s))


def store_barrier(ctx, eng):
    for s in STORE_SLOTS:
        W(ctx, eng, s, ctx.sch.cur(s))


def slab_dma(ctx, slot, out_ap, in_ap):
    marks = ctx.streams.setdefault(slot, [])
    if marks:
        for s, v in marks[-1]:
            W(ctx, "gpsimd", s, v)
    return dma(ctx, "gpsimd", slot, out_ap, in_ap)


def slab_mark(ctx, slot, extra=()):
    ctx.streams.setdefault(slot, []).append((("mm", ctx.sch.cur("mm")),) + tuple(extra))



def build_nc(upto=UPTO):
    nc = bass.Bass("TRN2", target_bir_lowering=False, debug=False)
    ctx = Ctx()
    ctx.nc = nc
    ctx.sch = Sch()
    ctx.sch.ctx = ctx

    dr = {}
    dr["x"] = nc.dram_tensor("x", [C, T], F32, kind="ExternalInput").ap()
    dr["vecbank"] = nc.dram_tensor("vecbank", [P, 16 * len(VECS)], F32, kind="ExternalInput").ap()
    dr["fbank"] = nc.dram_tensor("fbank", [P, 2 * NFC], F32, kind="ExternalInput").ap()
    dr["ident"] = nc.dram_tensor("ident", [P, P], F32, kind="ExternalInput").ap()
    dr["lag"] = nc.dram_tensor("lag", [16, NCORES * P], F32, kind="ExternalInput").ap()
    dr["bmask"] = nc.dram_tensor("bmask", [P, 1], F32, kind="ExternalInput").ap()
    dr["pmask"] = nc.dram_tensor("pmask", [P, 8], F32, kind="ExternalInput").ap()
    dr["wgate"] = nc.dram_tensor("wgate", [NFC, P, NCC, 128], I8, kind="ExternalInput").ap()
    dr["wup"] = nc.dram_tensor("wup", [NFC, P, NCC, 128], I8, kind="ExternalInput").ap()
    dr["wdown"] = nc.dram_tensor("wdown", [FF, C], I8, kind="ExternalInput").ap()
    for nm in ("wkT", "wvT", "wrT", "woT", "wcrT"):
        dr[nm] = nc.dram_tensor(nm, [8, P, NCC, 256], F32, kind="ExternalInput").ap()
    dr["wckT"] = nc.dram_tensor("wckT", [NFC, P, NCC, 128], F32, kind="ExternalInput").ap()
    dr["wcvT"] = nc.dram_tensor("wcvT", [FF, C], F32, kind="ExternalInput").ap()
    dr["y"] = nc.dram_tensor("y", [C, T], F32, kind="ExternalOutput").ap()

    dr["x1sp"] = nc.dram_tensor("x1sp", [C, T], F32).ap()
    dr["z1sp"] = nc.dram_tensor("z1sp", [C, T], F32).ap()
    dr["x2sp"] = nc.dram_tensor("x2sp", [C, T], F32).ap()
    dr["numsp"] = nc.dram_tensor("numsp", [C, T], F32).ap()
    dr["densp"] = nc.dram_tensor("densp", [C, T], F32).ap()
    dr["sigsp"] = nc.dram_tensor("sigsp", [C, T], BF16).ap()
    dr["pay0"] = nc.dram_tensor("pay0", [P, 16], F32).ap()
    dr["gat0"] = nc.dram_tensor("gat0", [NCORES * P, 16], F32).ap()
    dr["pay1"] = nc.dram_tensor("pay1", [64, P], F32).ap()
    dr["gat1"] = nc.dram_tensor("gat1", [NCORES * 64, P], F32).ap()
    dr["pay2"] = nc.dram_tensor("pay2", [P, 16], F32).ap()
    dr["gat2"] = nc.dram_tensor("gat2", [NCORES * P, 16], F32).ap()
    ctx.dr = dr

    import contextlib
    with contextlib.ExitStack() as es:
        AF32 = es.enter_context(nc.sbuf_tensor("af32", [P, NF32], F32))
        ABF = es.enter_context(nc.sbuf_tensor("abf", [P, NBF], BF16))
        ctx.PB = [es.enter_context(nc.psum_tensor(f"pb{i}", [P, 512], F32)) for i in range(8)]
        ctx.sem = {n: es.enter_context(nc.semaphore(n)) for n in SEM_NAMES}
        block = es.enter_context(nc.Block())

        ctx.f32 = Arena(AF32, NF32, "f32")
        ctx.bf = Arena(ABF, NBF, "bf16")
        ctx.bankrec = {}
        ctx.streams = {}
        ctx.banks = Banks(ctx)

        build_program(ctx, upto)

        sch = ctx.sch
        for eng in ENGS:
            getattr(block, eng)(lambda e, ops=tuple(sch.ops[eng]): [f(e) for f in ops])

    import os
    if os.environ.get("KARENA"):
        print(f"arena peaks: f32={ctx.f32.hi} bf={ctx.bf.hi} "
              f"bytes={ctx.f32.hi*4 + ctx.bf.hi*2}")
    return nc



def build_program(ctx, upto):
    sch, sem, dr = ctx.sch, ctx.sem, ctx.dr
    f32a, bfa = ctx.f32, ctx.bf

    ctx.vb = f32a.alloc(16 * len(VECS), "vecbank")
    ctx.fb = f32a.alloc(2 * NFC, "fbank")
    ctx.dv = f32a.alloc(16 * 6, "derived")
    ctx.ramp = f32a.alloc(T, "ramp")
    ctx.r8 = f32a.alloc(8, "r8")
    ctx.bcA = f32a.alloc(2 * T, "bcastA")
    ctx.bcB = f32a.alloc(3 * T, "bcastB")
    ctx.bmask = f32a.alloc(1, "bmask")
    ctx.pmask = f32a.alloc(8, "pmask")
    ctx.mall = f32a.alloc(16, "m_all")
    ctx.cafm = f32a.alloc(16, "ca_fm")
    ctx.cbfm = f32a.alloc(16, "cb_fm")
    ctx.sumstg = f32a.alloc(64, "sumstg")
    ctx.identb = bfa.alloc(P, "identb")
    onescol = bfa.alloc(1, "ones_col")
    onesrow_r = bfa.alloc(P, "ones_row")
    ctx.ones_bf = onescol
    ctx.onesrow_bf = onesrow_r[0:1, :]
    ctx.hilo = bfa.alloc(2 * 1024, "hilo")
    ctx.pool = {"qA": bfa.alloc(4096, "poolA"), "qB": bfa.alloc(4096, "poolB"),
                "qC": bfa.alloc(4096, "poolC"), "qD": bfa.alloc(4096, "poolD"),
                "qE": bfa.alloc(2048, "poolE"), "qF": bfa.alloc(2048, "poolF")}

    def VB(name):
        return ctx.vb[:, VID[name] * 16:(VID[name] + 1) * 16]
    ctx.VB = VB
    DV = {n: ctx.dv[:, i * 16:(i + 1) * 16] for i, n in
          enumerate(["w", "d", "eu", "lw", "shalf", "scr"])}
    ctx.DV = DV

    creg = f32a.alloc(2, "consts")
    ctx.nc.const_aps.register(F32, EPS, creg[:, 0:1])
    ctx.nc.const_aps.register(F32, 0.0, creg[:, 1:2])
    sch.on("vector", lambda v, o=creg[:, 0:1]: v.memset(o, EPS))
    sch.on("vector", lambda v, o=creg[:, 1:2]: v.memset(o, 0.0))
    dma(ctx, "sync", "qS", ctx.vb, dr["vecbank"])
    dma(ctx, "sync", "qS", ctx.fb, dr["fbank"])
    dma(ctx, "sync", "qS", ctx.bmask, dr["bmask"])
    dma(ctx, "sync", "qS", ctx.pmask, dr["pmask"])
    dma(ctx, "gpsimd", "qG", ctx.identb, dr["ident"])
    Wcur(ctx, "scalar", ["qS"])
    Wcur(ctx, "scalar", ["ev"])
    sch.on("scalar", lambda s: s.activation(DV["scr"], VB("td"), AF.Exp))
    sch.on("scalar", lambda s: s.mul(DV["w"], DV["scr"], -1.0))
    sch.on("scalar", lambda s: s.activation(DV["d"], DV["w"], AF.Exp))
    sch.on("scalar", lambda s: s.activation(DV["eu"], VB("tf"), AF.Exp))
    sch.on("scalar", lambda s: s.mul(DV["lw"], DV["w"], float(T)))
    sch.on("scalar", lambda s: s.mul(DV["shalf"], VB("sdown"), 0.5))
    sch.on("vector", lambda v: v.memset(ctx.ones_bf, 1.0))
    sch.on("vector", lambda v: v.memset(ctx.onesrow_bf, 1.0))
    sch.inc("pl")
    sch.on("gpsimd", lambda g: g.iota(ctx.ramp, pattern=[[1, T]], base=0, channel_multiplier=0,
                                      allow_small_or_imprecise_dtypes=True).then_inc(sem["pl"], 1))

    ffn_stage(ctx, src=dr["x"], dst=dr["x1sp"], res=dr["x"], do_stats=True,
              debug_y=(upto == 1))
    if upto == 1:
        return finish(ctx)
    attn_mix_stage(ctx, debug_y=(upto == 2))
    if upto == 2:
        return finish(ctx)
    chmix_stage(ctx, debug_y=(upto == 3))
    if upto == 3:
        return finish(ctx)
    ffn_stage(ctx, src=dr["x2sp"], dst=dr["y"], res=dr["x2sp"], do_stats=False,
              debug_y=False)
    return finish(ctx)


def finish(ctx):
    for s in STORE_SLOTS + ("qS", "qG", "qA", "qB", "qC", "qD", "qE", "qF"):
        W(ctx, "gpsimd", s, ctx.sch.cur(s))



class Ring:

    def __init__(self, ctx, bufs):
        self.ctx = ctx
        self.bufs = bufs
        self.cons = [(0, 0)] * len(bufs)
        self.i = 0

    def load(self, src, engs=("vector",)):
        ctx = self.ctx
        j = self.i % len(self.bufs)
        self.i += 1
        ev, ac = self.cons[j]
        W(ctx, "sync", "ev", ev)
        W(ctx, "sync", "ac", ac)
        qt = dma(ctx, "sync", "qS", self.bufs[j], src)
        for e in engs:
            W(ctx, e, "qS", qt)
        return self.bufs[j], j

    def consumed(self, j):
        self.cons[j] = (self.ctx.sch.cur("ev"), self.ctx.sch.cur("ac"))


def f32_scratch(ctx, n, name="f32scr"):
    if ctx.bf.off % 2:
        ctx.bf.alloc(1, "align")
    return ctx.bf.alloc(2 * n, name).bitcast(F32)


def fmv(dram, cc=None, cols=None):
    v = dram.rearrange("(cc p) t -> p cc t", p=P)
    if cc is None:
        return v
    if cols is None:
        return v[:, cc]
    return v[:, cc, cols]


def ln_stats(ctx, xin_fn, bc, double_ln=False, halves=(0, 1), bcast=True):
    sch, sem = ctx.sch, ctx.sem
    bfa = ctx.bf
    bfa.push()
    xbf = [bfa.alloc(512, "xbf") for _ in range(2)]
    xsq = [bfa.alloc(512, "xsq") for _ in range(2)]
    rows = ctx.rows
    mu_row = rows[0:1, 0:1024]
    msq_row = rows[0:1, 1024:2048]
    var_row = rows[0:1, 2048:3072]
    r_row = rows[0:1, 3072:4096]

    for g in halves:
        cs = slice(g * 512, (g + 1) * 512)
        bank_wait(ctx, 6)
        bank_wait(ctx, 7)
        mmcc = {}
        for cc in range(NCC):
            xa = xin_fn(cc, g)
            sl = cc % 2
            if cc >= 2:
                W(ctx, "scalar", "mm", mmcc[cc - 2])
                W(ctx, "vector", "mm", mmcc[cc - 2])
            a1 = sch.cur("ac") + 1
            sch.on("scalar", lambda s, o=xbf[sl], i=xa: s.copy(o, i))
            e1 = sch.cur("ev") + 1
            sch.on("vector", lambda v, o=xsq[sl], i=xa: v.tensor_mul(o, i, i))
            W(ctx, "tensor", "ac", a1)
            W(ctx, "tensor", "ev", e1)
            sch.on("tensor", lambda t, o=ctx.PB[6][0:1, 0:512], l=ctx.ones_bf, r=xbf[sl],
                   st=(cc == 0), sp=(cc == NCC - 1): t.matmul(o, l, r, start=st, stop=sp))
            mmcc[cc] = sch.inc("mm")
            sch.on("tensor", lambda t, o=ctx.PB[7][0:1, 0:512], l=ctx.ones_bf, r=xsq[sl],
                   st=(cc == 0), sp=(cc == NCC - 1), s=sem["mm"]:
                   t.matmul(o, l, r, start=st, stop=sp).then_inc(s, 1))
        mdone = sch.cur("mm")
        W(ctx, "vector", "mm", mdone)
        sch.on("vector", lambda v, o=mu_row[:, cs], i=ctx.PB[6][0:1, 0:512]:
               v.tensor_scalar(o, i, CINV, None, ALU.mult))
        sch.on("vector", lambda v, o=msq_row[:, cs], i=ctx.PB[7][0:1, 0:512]:
               v.tensor_scalar(o, i, CINV, None, ALU.mult))
        bank_free(ctx, "vector", 6)
        bank_free(ctx, "vector", 7)
        sch.on("vector", lambda v, o=var_row[:, cs], a=mu_row[:, cs]: v.tensor_mul(o, a, a))
        e2 = sch.cur("ev") + 1
        sch.on("vector", lambda v, o=var_row[:, cs], a=msq_row[:, cs], b=var_row[:, cs]:
               v.tensor_sub(o, a, b))
        W(ctx, "scalar", "ev", e2)
        a2 = sch.cur("ac") + 1
        sch.on("scalar", lambda s, o=r_row[:, cs], i=var_row[:, cs]:
               s.activation(o, i, AF.Sqrt, bias=EPS))
        W(ctx, "vector", "ac", a2)
        sch.on("vector", lambda v, o=r_row[:, cs], i=r_row[:, cs]:
               v.reciprocal(o, i))
    bfa.pop()
    if not bcast:
        return
    bcast_row(ctx, mu_row, bc[:, 0:T])
    bcast_row(ctx, r_row, bc[:, T:2 * T])
    if double_ln:
        t1 = msq_row
        sch.on("vector", lambda v, o=t1, a=r_row, b=r_row: v.tensor_mul(o, a, b))
        e3 = sch.cur("ev") + 1
        sch.on("vector", lambda v, o=t1, a=var_row, b=t1: v.tensor_mul(o, a, b))
        W(ctx, "scalar", "ev", e3)
        a3 = sch.cur("ac") + 1
        sch.on("scalar", lambda s, o=t1, i=t1: s.activation(o, i, AF.Sqrt, bias=EPS))
        W(ctx, "vector", "ac", a3)
        sch.on("vector", lambda v, o=t1, i=t1: v.reciprocal(o, i))
        sch.on("vector", lambda v, o=t1, a=r_row, b=t1: v.tensor_mul(o, a, b))
        bcast_row(ctx, t1, bc[:, 2 * T:3 * T])


def bcast_row(ctx, row_f32, dst_f32, n=T):
    sch, sem = ctx.sch, ctx.sem
    hi = ctx.hilo[0:1, 0:1024]
    lo = ctx.hilo[0:1, 1024:2048]
    Wcur(ctx, "scalar", ["ev", "mm"])
    Wcur(ctx, "vector", ["mm"])
    a1 = sch.cur("ac") + 1
    sch.on("scalar", lambda s, o=hi[:, 0:n], i=row_f32[:, 0:n]: s.copy(o, i))
    W(ctx, "vector", "ac", a1)
    e1 = sch.cur("ev") + 1
    sch.on("vector", lambda v, o=lo[:, 0:n], a=row_f32[:, 0:n], b=hi[:, 0:n]:
           v.tensor_sub(o, a, b))
    W(ctx, "tensor", "ev", e1)
    for g in range((n + 511) // 512):
        lo_c = g * 512
        hi_c = min(n, (g + 1) * 512)
        wdt = hi_c - lo_c
        bank_wait(ctx, 7)
        sch.on("tensor", lambda t, o=ctx.PB[7][:, 0:wdt], l=ctx.onesrow_bf, r=hi[:, lo_c:hi_c]:
               t.matmul(o, l, r, start=True, stop=False))
        sch.inc("mm")
        m1 = sch.cur("mm")
        sch.on("tensor", lambda t, o=ctx.PB[7][:, 0:wdt], l=ctx.onesrow_bf, r=lo[:, lo_c:hi_c],
               s=sem["mm"]: t.matmul(o, l, r, start=False, stop=True).then_inc(s, 1))
        W(ctx, "vector", "mm", m1)
        sch.on("vector", lambda v, o=dst_f32[:, lo_c:hi_c], i=ctx.PB[7][:, 0:wdt]:
               v.tensor_copy(o, i))
        bank_free(ctx, "vector", 7)


def proj_stage(ctx, wdram, rhs3, out_emit, windows=4, win=512, sfn="sf3",
               slots=("qA", "qB", "qC"), rhs_cols=(0, 1)):
    sch, sem = ctx.sch, ctx.sem
    per = win // 128
    qts = {}

    def issue(wi):
        slot = slots[wi % len(slots)]
        buf = ctx.pool[slot][:, 0:NCC * win]
        qts[wi] = (slot, buf, slab_dma(ctx, slot,
                   buf.rearrange("p (kc m) -> p kc m", kc=NCC), wdram[wi]))

    for wi in range(min(len(slots), windows)):
        issue(wi)
    for wi in range(windows):
        slot, buf, qt = qts[wi]
        W(ctx, "tensor", slot, qt)
        b3 = buf.rearrange("p (kc m) -> p kc m", kc=NCC)
        for mi in range(per):
            mc = wi * per + mi
            for g in rhs_cols:
                b = ctx.banks.acquire()
                for kc in range(NCC):
                    last = (kc == NCC - 1)
                    if last:
                        sch.inc("mm")
                    sch.on("tensor", lambda t, o=ctx.PB[b][:, 0:512],
                           l=b3[:, kc, mi * 128:(mi + 1) * 128],
                           r=rhs3[:, kc, g * 512:(g + 1) * 512],
                           st=(kc == 0), sp=last, fin=last, s=sem["mm"]:
                           (t.matmul(o, l, r, start=st, stop=sp).then_inc(s, 1) if fin
                            else t.matmul(o, l, r, start=st, stop=sp)))
                out_emit(mc, g, b, sch.cur("mm"))
        slab_mark(ctx, slot)
        if wi + len(slots) < windows:
            issue(wi + len(slots))



def ffn_stage(ctx, src, dst, res, do_stats, debug_y):
    sch, sem, dr = ctx.sch, ctx.sem, ctx.dr
    f32a, bfa = ctx.f32, ctx.bf
    f32a.push(); bfa.push()
    xhat = bfa.alloc(NCC * HALF, "xhat")
    xh3 = xhat.rearrange("p (cc t) -> p cc t", cc=NCC)

    f32a.push(); bfa.push()
    xin = f32_scratch(ctx, NCC * 512, "xin")
    ctx.rows = f32a.alloc(4 * 1024, "rows")
    tmp = f32a.alloc(512, "t0")

    def load_half(g):
        Wcur(ctx, "sync", ["ev", "ac"])
        for cc in range(NCC):
            dma(ctx, "sync", "qS", xin[:, cc * 512:(cc + 1) * 512],
                fmv(src, cc, slice(g * 512, (g + 1) * 512)))
        q = sch.cur("qS")
        W(ctx, "scalar", "qS", q)
        W(ctx, "vector", "qS", q)

    store_barrier(ctx, "sync")
    for g in range(2):
        load_half(g)
        ln_stats(ctx, lambda cc, _g: xin[:, cc * 512:(cc + 1) * 512], ctx.bcA, halves=(g,),
                 bcast=(g == 1))

    f32a.pop(); bfa.pop()

    f32a.push(); bfa.push()
    arena_fence(ctx)
    fuse = bfa.alloc(NFC * HALF, "fuse")
    f3 = fuse.rearrange("p (fc t) -> p fc t", fc=NFC)
    gsc = [f32a.alloc(HALF, "gsc") for _ in range(2)]
    upsc = [f32a.alloc(HALF, "upsc") for _ in range(2)]
    xring = Ring(ctx, [f32a.alloc(HALF, "xres") for _ in range(4)])
    xout = [f32a.alloc(HALF, "xout") for _ in range(2)]

    for h in range(2):
        hs = slice(h * HALF, (h + 1) * HALF)
        W(ctx, "tensor", "mm", sch.cur("mm"))
        for cc in range(NCC):
            xr, jr = xring.load(fmv(src, cc, hs))
            if cc < 2:
                W(ctx, "vector", "mm", sch.cur("mm"))
            sch.on("vector", lambda v, o=gsc[cc % 2], a=xr, b=ctx.bcA[:, 0:T][:, hs]:
                   v.tensor_sub(o, a, b))
            sch.on("vector", lambda v, o=xh3[:, cc, 0:HALF], a=gsc[cc % 2], b=ctx.bcA[:, T:2 * T][:, hs]:
                   v.tensor_mul(o, a, b))
            xring.consumed(jr)
        xhat_done = sch.cur("ev")
        W(ctx, "tensor", "ev", xhat_done)
        for fc in range(NFC):
            slotg = ("qA", "qB", "qC")[fc % 3]
            slotu = ("qD", "qE", "qF")[fc % 3]
            bufg = ctx.pool[slotg][:, 0:NCC * 128]
            bufu = ctx.pool[slotu][:, 0:NCC * 128]
            qtg = slab_dma(ctx, slotg,
                           bufg.rearrange("p (kc m) -> p kc m", kc=NCC), dr["wgate"][fc])
            qtu = slab_dma(ctx, slotu,
                           bufu.rearrange("p (kc m) -> p kc m", kc=NCC), dr["wup"][fc])
            for wsel in ("g", "u"):
                W(ctx, "tensor", slotg if wsel == "g" else slotu,
                  qtg if wsel == "g" else qtu)
                b3 = (bufg if wsel == "g" else bufu).rearrange("p (kc m) -> p kc m", kc=NCC)
                b = ctx.banks.acquire()
                for kc in range(NCC):
                    last = (kc == NCC - 1)
                    if last:
                        sch.inc("mm")
                    sch.on("tensor", lambda t, o=ctx.PB[b][:, 0:512],
                           l=b3[:, kc, 0:128],
                           r=xh3[:, kc, 0:HALF], st=(kc == 0), sp=last, fin=last, s=sem["mm"]:
                           (t.matmul(o, l, r, start=st, stop=sp).then_inc(s, 1) if fin
                            else t.matmul(o, l, r, start=st, stop=sp)))
                last_mm = sch.cur("mm")
                if wsel == "g":
                    W(ctx, "scalar", "mm", last_mm)
                    W(ctx, "scalar", "ev", sch.cur("ev"))
                    sch.on("scalar", lambda s, o=gsc[fc % 2], i=ctx.PB[b][:, 0:512],
                           sc=ctx.fb[:, fc:fc + 1]:
                           s.activation(o, i, AF.Silu, scale=sc))
                    ctx.banks.free("scalar", b)
                else:
                    W(ctx, "vector", "mm", last_mm)
                    sch.on("vector", lambda v, o=upsc[fc % 2], i=ctx.PB[b][:, 0:512],
                           sc=ctx.fb[:, NFC + fc:NFC + fc + 1]:
                           v.tensor_scalar(o, i, sc, None, ALU.mult))
                    ctx.banks.free("vector", b)
                    W(ctx, "vector", "ac", sch.cur("ac"))
                    sch.on("vector", lambda v, o=f3[:, fc, 0:HALF],
                           a=gsc[fc % 2], b2=upsc[fc % 2]:
                           v.tensor_mul(o, a, b2))
            slab_mark(ctx, slotg)
            slab_mark(ctx, slotu)
        fuse_done = sch.cur("ev")

        for cg in range(2):
            ccs = range(cg * 8, (cg + 1) * 8)
            W(ctx, "tensor", "ev", fuse_done)
            for fp2 in range(NFC // 2):
                slot = ("qA", "qB", "qC")[fp2 % 3]
                buf = ctx.pool[slot]
                qt = slab_dma(ctx, slot,
                              buf.rearrange("p (f c) -> p f c", f=2),
                              fmv(dr["wdown"])[:, 2 * fp2:2 * fp2 + 2])
                W(ctx, "tensor", slot, qt)
                for fi in range(2):
                    fc = 2 * fp2 + fi
                    for cc in ccs:
                        pb = ctx.PB[cc % 8][:, 0:512]
                        last = (cc == ccs[-1])
                        if fc == 0:
                            bank_wait(ctx, cc % 8)
                        if last:
                            sch.inc("mm")
                        sch.on("tensor", lambda t, o=pb, l=buf[:, (fi * NCC + cc) * 128:(fi * NCC + cc + 1) * 128],
                               r=f3[:, fc, 0:HALF], st=(fc == 0), sp=(fc == NFC - 1), fin=last, s=sem["mm"]:
                               (t.matmul(o, l, r, start=st, stop=sp).then_inc(s, 1) if fin
                                else t.matmul(o, l, r, start=st, stop=sp)))
                slab_mark(ctx, slot)
            mdone = sch.cur("mm")
            for cc in ccs:
                xr, jr = xring.load(fmv(res, cc, hs))
                ot = xout[cc % 2]
                W(ctx, "vector", "mm", mdone)
                sslot = "sA" if cc % 2 == 0 else "sB"
                W(ctx, "vector", sslot, sch.cur(sslot))
                pb = ctx.PB[cc % 8][:, 0:512]
                sch.on("vector", lambda v, o=ot, i=pb, sc=ctx.DV["shalf"][:, cc:cc + 1], x=xr:
                       v.scalar_tensor_tensor(o, i, sc, x, ALU.mult, ALU.add))
                bank_free(ctx, "vector", cc % 8)
                xring.consumed(jr)
                W(ctx, "sync", "ev", sch.cur("ev"))
                dma(ctx, "sync", sslot, fmv(dst, cc, hs), ot)
                if debug_y:
                    dma(ctx, "sync", sslot, fmv(dr["y"], cc, hs), ot)
    f32a.pop(); bfa.pop()
    f32a.pop(); bfa.pop()

    if do_stats:
        f32a.push(); bfa.push()
        xin2 = f32_scratch(ctx, NCC * 512, "xin2")
        ctx.rows = f32a.alloc(4 * 1024, "rows")
        colb = f32a.alloc(16, "colb")
        store_barrier(ctx, "sync")
        for g in range(2):
            Wcur(ctx, "sync", ["ev", "ac"])
            for cc in range(NCC):
                dma(ctx, "sync", "qS", xin2[:, cc * 512:(cc + 1) * 512],
                    fmv(dst, cc, slice(g * 512, (g + 1) * 512)))
            q = sch.cur("qS")
            W(ctx, "scalar", "qS", q)
            W(ctx, "vector", "qS", q)
            if g == 1:
                for cc in range(NCC):
                    if cc == NCC - 1:
                        sch.on("vector", lambda v, o=colb[:, cc:cc + 1],
                               i=xin2[:, cc * 512 + 511:cc * 512 + 512]:
                               v.tensor_copy(o, i))
                    else:
                        sch.on("vector", lambda v, o=colb[:, cc:cc + 1],
                               i=xin2[:, cc * 512 + 511:cc * 512 + 512]: v.tensor_copy(o, i))
                send_col_round(ctx, colb, "pay0", "gat0", "r0")
            ln_stats(ctx, lambda cc, _g: xin2[:, cc * 512:(cc + 1) * 512], ctx.bcB,
                     double_ln=(g == 1), halves=(g,), bcast=(g == 1))
        f32a.pop(); bfa.pop()


def send_col_round(ctx, colb, pay, gat, rsem):
    sch, sem, dr = ctx.sch, ctx.sem, ctx.dr
    Wcur(ctx, "gpsimd", ["ev"])
    qt2 = dma(ctx, "gpsimd", "qG", dr[pay], colb)
    W(ctx, "gpsimd", "qG", qt2)
    sch.inc(rsem)
    sch.on("gpsimd", lambda g, i=dr[pay], o=dr[gat], s=sem[rsem]:
           g.collective_compute("AllGather", ALU.bypass,
                                replica_groups=[list(range(NCORES))],
                                ins=[i], outs=[o]).then_inc(s, 1))
    W(ctx, "vector", "qG", ctx.sch.cur("qG"))
    W(ctx, "scalar", "qG", ctx.sch.cur("qG"))


def recv_col(ctx, gat, rsem, dst_col, tmp16):
    sch, sem, dr = ctx.sch, ctx.sem, ctx.dr
    f32a = ctx.f32
    f32a.push()
    arena_fence(ctx)
    allb = f32a.alloc(8 * 16, "allb")
    a3 = allb.rearrange("p (j cc) -> p j cc", j=8)
    W(ctx, "sync", rsem, sch.cur(rsem))
    qg = dma(ctx, "sync", "qS", a3, dr[gat].rearrange("(j p) cc -> p j cc", j=8))
    W(ctx, "vector", "qS", qg)
    sch.on("vector", lambda v, o=dst_col, a=a3[:, 0], b=ctx.pmask[:, 0:1]:
           v.tensor_scalar(o, a, b, None, ALU.mult))
    for j in range(1, 8):
        sch.on("vector", lambda v, o=tmp16, a=a3[:, j], b=ctx.pmask[:, j:j + 1]:
               v.tensor_scalar(o, a, b, None, ALU.mult))
        sch.on("vector", lambda v, o=dst_col, a=dst_col, b=tmp16: v.tensor_add(o, a, b))
    f32a.pop()


def boundary_ln(ctx, xb, out_col, wname, bname, double_ln):
    sch, sem = ctx.sch, ctx.sem
    f32a, bfa = ctx.f32, ctx.bf
    f32a.push(); bfa.push()
    xbb = bfa.alloc(16, "xbb")
    xbs = bfa.alloc(16, "xbs")
    r0 = ctx.r8[0:1, 0:8]
    rb = f32a.alloc(4, "rbb")
    tmpc = f32a.alloc(16, "tmpc")
    Wcur(ctx, "scalar", ["ev"])
    a1 = sch.cur("ac") + 1
    sch.on("scalar", lambda s, o=xbb, i=xb: s.copy(o, i))
    e1 = sch.cur("ev") + 1
    sch.on("vector", lambda v, o=xbs, i=xb: v.tensor_mul(o, i, i))
    W(ctx, "tensor", "ac", a1)
    W(ctx, "tensor", "ev", e1)
    bank_wait(ctx, 6)
    sch.on("tensor", lambda t, o=ctx.PB[6][0:1, 0:16], l=ctx.ones_bf, r=xbb:
           t.matmul(o, l, r, start=True, stop=True))
    sch.inc("mm")
    m1 = sch.cur("mm")
    sch.on("tensor", lambda t, o=ctx.PB[6][0:1, 16:32], l=ctx.ones_bf, r=xbs, s=sem["mm"]:
           t.matmul(o, l, r, start=True, stop=True).then_inc(s, 1))
    W(ctx, "vector", "mm", m1)
    sch.on("vector", lambda v, o=r0[:, 0:1], i=ctx.PB[6][0:1, 0:16]: v.reduce_sum(o, i, axis=AX))
    sch.on("vector", lambda v, o=r0[:, 1:2], i=ctx.PB[6][0:1, 16:32]: v.reduce_sum(o, i, axis=AX))
    bank_free(ctx, "vector", 6)
    sch.on("vector", lambda v, o=r0[:, 0:1], i=r0[:, 0:1]: v.tensor_scalar(o, i, CINV, None, ALU.mult))
    sch.on("vector", lambda v, o=r0[:, 1:2], i=r0[:, 1:2]: v.tensor_scalar(o, i, CINV, None, ALU.mult))
    sch.on("vector", lambda v, o=r0[:, 2:3], a=r0[:, 0:1], b=r0[:, 0:1]: v.tensor_mul(o, a, b))
    e2 = sch.cur("ev") + 1
    sch.on("vector", lambda v, o=r0[:, 1:2], a=r0[:, 1:2], b=r0[:, 2:3]:
           v.tensor_sub(o, a, b))
    W(ctx, "scalar", "ev", e2)
    a2 = sch.cur("ac") + 1
    sch.on("scalar", lambda s, o=r0[:, 2:3], i=r0[:, 1:2]:
           s.activation(o, i, AF.Sqrt, bias=EPS))
    W(ctx, "vector", "ac", a2)
    sch.on("vector", lambda v, o=r0[:, 2:3], i=r0[:, 2:3]: v.reciprocal(o, i))
    if double_ln:
        sch.on("vector", lambda v, o=r0[:, 3:4], a=r0[:, 2:3], b=r0[:, 2:3]: v.tensor_mul(o, a, b))
        e3 = sch.cur("ev") + 1
        sch.on("vector", lambda v, o=r0[:, 3:4], a=r0[:, 1:2], b=r0[:, 3:4]:
               v.tensor_mul(o, a, b))
        W(ctx, "scalar", "ev", e3)
        a3 = sch.cur("ac") + 1
        sch.on("scalar", lambda s, o=r0[:, 3:4], i=r0[:, 3:4]:
               s.activation(o, i, AF.Sqrt, bias=EPS))
        W(ctx, "vector", "ac", a3)
        sch.on("vector", lambda v, o=r0[:, 3:4], i=r0[:, 3:4]: v.reciprocal(o, i))
        sch.on("vector", lambda v, o=r0[:, 2:3], a=r0[:, 2:3], b=r0[:, 3:4]: v.tensor_mul(o, a, b))
    e4 = sch.cur("ev")
    hi = ctx.hilo[0:1, 0:4]
    lo = ctx.hilo[0:1, 4:8]
    W(ctx, "scalar", "mm", sch.cur("mm"))
    W(ctx, "vector", "mm", sch.cur("mm"))
    W(ctx, "scalar", "ev", e4)
    a4 = sch.cur("ac") + 1
    sch.on("scalar", lambda s, o=hi, i=r0[:, 0:4]: s.copy(o, i))
    W(ctx, "vector", "ac", a4)
    e5 = sch.cur("ev") + 1
    sch.on("vector", lambda v, o=lo, a=r0[:, 0:4], b=hi: v.tensor_sub(o, a, b))
    W(ctx, "tensor", "ev", e5)
    bank_wait(ctx, 6)
    sch.on("tensor", lambda t, o=ctx.PB[6][:, 0:4], l=ctx.onesrow_bf, r=hi:
           t.matmul(o, l, r, start=True, stop=False))
    sch.inc("mm")
    m2 = sch.cur("mm")
    sch.on("tensor", lambda t, o=ctx.PB[6][:, 0:4], l=ctx.onesrow_bf, r=lo, s=sem["mm"]:
           t.matmul(o, l, r, start=False, stop=True).then_inc(s, 1))
    W(ctx, "vector", "mm", m2)
    sch.on("vector", lambda v, o=rb, i=ctx.PB[6][:, 0:4]: v.tensor_copy(o, i))
    bank_free(ctx, "vector", 6)
    sch.on("vector", lambda v, o=tmpc, i=xb, s1=rb[:, 0:1], s2=rb[:, 2:3]:
           v.tensor_scalar(o, i, s1, s2, ALU.subtract, ALU.mult))
    sch.on("vector", lambda v, o=tmpc, a=tmpc, b=ctx.VB(wname): v.tensor_mul(o, a, b))
    sch.on("vector", lambda v, o=tmpc, a=tmpc, b=ctx.VB(bname): v.tensor_add(o, a, b))
    sch.on("vector", lambda v, o=out_col, i=tmpc, s1=ctx.bmask[:, 0:1]:
           v.tensor_scalar(o, i, s1, None, ALU.mult))
    f32a.pop(); bfa.pop()



def attn_mix_stage(ctx, debug_y):
    sch, sem, dr = ctx.sch, ctx.sem, ctx.dr
    f32a, bfa = ctx.f32, ctx.bf
    f32a.push(); bfa.push()
    Hb = bfa.alloc(NCC * 1025, "H")
    h3 = Hb.rearrange("p (cc t) -> p cc t", cc=NCC)
    bfa.push()
    XK = bfa.alloc(NCC * T, "XK")
    XV = bfa.alloc(NCC * T, "XV")
    xk3 = XK.rearrange("p (cc t) -> p cc t", cc=NCC)
    xv3 = XV.rearrange("p (cc t) -> p cc t", cc=NCC)

    mu1 = ctx.bcB[:, 0:T]
    r1b = ctx.bcB[:, T:2 * T]
    rr1 = ctx.bcB[:, 2 * T:3 * T]

    f32a.push()
    x1h = f32a.alloc(NCC * 512, "x1h")
    tmp = f32a.alloc(1024, "tmpA")
    store_barrier(ctx, "sync")
    for g in range(2):
        cs = slice(g * 512, (g + 1) * 512)
        Wcur(ctx, "sync", ["ev"])
        for cc in range(NCC):
            dma(ctx, "sync", "qS", x1h[:, cc * 512:(cc + 1) * 512],
                fmv(dr["x1sp"], cc, cs))
        W(ctx, "vector", "qS", sch.cur("qS"))
        for cc in range(NCC):
            sch.on("vector", lambda v, o=tmp[:, 0:512], a=x1h[:, cc * 512:(cc + 1) * 512],
                   b=mu1[:, cs]: v.tensor_sub(o, a, b))
            sch.on("vector", lambda v, o=tmp[:, 0:512], a=tmp[:, 0:512], b=rr1[:, cs]:
                   v.tensor_mul(o, a, b))
            sch.on("vector", lambda v, o=h3[:, cc, 1 + g * 512:1 + (g + 1) * 512], i=tmp[:, 0:512],
                   s1=ctx.VB("ln1w")[:, cc:cc + 1], s2=ctx.VB("ln1b")[:, cc:cc + 1]:
                   v.tensor_scalar(o, i, s1, s2, ALU.mult, ALU.add))
    dlt = ctx.hilo[:, 0:1024]
    Wcur(ctx, "vector", ["ev"])
    for cc in range(NCC):
        for buf3, vec in ((xk3, "tmk"), (xv3, "tmv")):
            sch.on("vector", lambda v, o=dlt[:, 0:1023], a=h3[:, cc, 2:1025], b=h3[:, cc, 1:1024]:
                   v.tensor_sub(o, a, b))
            sch.on("vector", lambda v, o=dlt[:, 0:1023], i=dlt[:, 0:1023],
                   s1=ctx.VB(vec)[:, cc:cc + 1]: v.tensor_scalar(o, i, s1, None, ALU.mult))
            sch.on("vector", lambda v, o=buf3[:, cc, 1:1024], a=dlt[:, 0:1023],
                   b=h3[:, cc, 1:1024]: v.tensor_add(o, a, b))
    lerpA_done = sch.cur("ev")
    xb = f32a.alloc(16, "xb")
    hb = f32a.alloc(16, "hb")
    recv_col(ctx, "gat0", "r0", xb, tmp[:, 0:16])
    boundary_ln(ctx, xb, hb, "ln1w", "ln1b", double_ln=True)
    for cc in range(NCC):
        sch.on("vector", lambda v, o=h3[:, cc, 0:1], i=hb[:, cc:cc + 1]:
               v.tensor_copy(o, i))
    for cc in range(NCC):
        for buf3, vec in ((xk3, "tmk"), (xv3, "tmv")):
            sch.on("vector", lambda v, o=dlt[:, 0:1], a=h3[:, cc, 1:2], b=h3[:, cc, 0:1]:
                   v.tensor_sub(o, a, b))
            sch.on("vector", lambda v, o=dlt[:, 0:1], i=dlt[:, 0:1],
                   s1=ctx.VB(vec)[:, cc:cc + 1]: v.tensor_scalar(o, i, s1, None, ALU.mult))
            sch.on("vector", lambda v, o=buf3[:, cc, 0:1], a=dlt[:, 0:1], b=h3[:, cc, 0:1]:
                   v.tensor_add(o, a, b))
    lerps_done = sch.cur("ev")
    f32a.pop()

    bfa.push(); f32a.push()
    arena_fence(ctx)
    Kd2 = [f32a.alloc(1024, "Kd") for _ in range(2)]
    Vd2 = [f32a.alloc(1024, "Vd") for _ in range(2)]
    EB = f32a.alloc(1024, "EBs")
    EA = f32a.alloc(1024, "EAs")
    PAs = f32a.alloc(1025, "PAs")
    PBs = f32a.alloc(1025, "PBs")
    NUM = EA
    DEN = EB
    Dts = f32a.alloc(1024, "Dts")
    negm = f32a.alloc(1, "negm")
    scan_ev = {0: 0, 1: 0}
    for dw in range(8):
        slotk = ("qA", "qB")[dw % 2]
        slotv = ("qC", "qD")[dw % 2]
        bufk = ctx.pool[slotk]
        bufv = ctx.pool[slotv]
        qk = slab_dma(ctx, slotk,
                      bufk.rearrange("p (kc m) -> p kc m", kc=NCC), dr["wkT"][dw])
        qv = slab_dma(ctx, slotv,
                      bufv.rearrange("p (kc m) -> p kc m", kc=NCC), dr["wvT"][dw])
        W(ctx, "tensor", slotk, qk)
        W(ctx, "tensor", slotv, qv)
        b3k = bufk.rearrange("p (kc m) -> p kc m", kc=NCC)
        b3v = bufv.rearrange("p (kc m) -> p kc m", kc=NCC)
        for di in range(2):
            dc = dw * 2 + di
            par = dc % 2
            Kd = Kd2[par]
            Vd = Vd2[par]
            for b3, src3, dst in ((b3k, xk3, Kd), (b3v, xv3, Vd)):
                for g in (1, 0):
                    W(ctx, "tensor", "ev", lerpA_done if g == 1 else lerps_done)
                    b = ctx.banks.acquire()
                    for kc in range(NCC):
                        last = (kc == NCC - 1)
                        if last:
                            sch.inc("mm")
                        sch.on("tensor", lambda t, o=ctx.PB[b][:, 0:512],
                               l=b3[:, kc, di * 128:(di + 1) * 128],
                               r=src3[:, kc, g * 512:(g + 1) * 512],
                               st=(kc == 0), sp=last, fin=last, s=sem["mm"]:
                               (t.matmul(o, l, r, start=st, stop=sp).then_inc(s, 1) if fin
                                else t.matmul(o, l, r, start=st, stop=sp)))
                    mt = sch.cur("mm")
                    W(ctx, "scalar", "mm", mt)
                    W(ctx, "scalar", "ev", scan_ev[par])
                    sch.on("scalar", lambda s, o=dst[:, g * 512:(g + 1) * 512], i=ctx.PB[b][:, 0:512]:
                           s.copy(o, i))
                    ctx.banks.free("scalar", b)
            W(ctx, "vector", "ac", sch.cur("ac"))
            sch.on("vector", lambda v, o=ctx.mall[:, dc:dc + 1], i=Kd: v.reduce_max(o, i, axis=AX))
            e1 = sch.cur("ev") + 1
            sch.on("vector", lambda v, o=negm, i=ctx.mall[:, dc:dc + 1]:
                   v.tensor_scalar(o, i, -1.0, None, ALU.mult))
            W(ctx, "scalar", "ev", e1)
            a2 = sch.cur("ac") + 1
            sch.on("scalar", lambda s, o=EB, i=Kd, bi=negm:
                   s.activation(o, i, AF.Exp, bias=bi))
            W(ctx, "vector", "ac", a2)
            sch.on("vector", lambda v, o=Dts, i=EB, s1=ctx.DV["d"][:, dc:dc + 1]:
                   v.tensor_scalar(o, i, 0.0, s1, ALU.mult, ALU.add))
            sch.on("vector", lambda v, o=EA, a=EB, b=Vd: v.tensor_mul(o, a, b))
            scan_ev[par] = sch.cur("ev")
            sch.on("vector", lambda v, o=PAs[:, 0:1]: v.memset(o, 0.0))
            sch.on("vector", lambda v, o=PBs[:, 0:1]: v.memset(o, 0.0))
            sch.on("vector", lambda v, o=PAs[:, 1:1025], d0=Dts, d1=EA:
                   v.tensor_tensor_scan(o, d0, d1, 0.0, ALU.mult, ALU.add))
            sch.on("vector", lambda v, o=PBs[:, 1:1025], d0=Dts, d1=EB:
                   v.tensor_tensor_scan(o, d0, d1, 0.0, ALU.mult, ALU.add))
            sch.on("vector", lambda v, o=NUM, i=EA, s1=ctx.DV["eu"][:, dc:dc + 1], x=PAs[:, 0:1024]:
                   v.scalar_tensor_tensor(o, i, s1, x, ALU.mult, ALU.add))
            sch.on("vector", lambda v, o=DEN, i=EB, s1=ctx.DV["eu"][:, dc:dc + 1], x=PBs[:, 0:1024]:
                   v.scalar_tensor_tensor(o, i, s1, x, ALU.mult, ALU.add))
            sch.on("vector", lambda v, o=ctx.sumstg[:, dc:dc + 1], i=PAs[:, 1024:1025]:
                   v.tensor_copy(o, i))
            sch.on("vector", lambda v, o=ctx.sumstg[:, 16 + dc:17 + dc], i=PBs[:, 1024:1025]:
                   v.tensor_copy(o, i))
            evn = sch.cur("ev")
            W(ctx, "sync", "ev", evn)
            dma(ctx, "sync", "sC", dr["numsp"].rearrange("(dc p) t -> p dc t", p=P)[:, dc], NUM)
            dma(ctx, "sync", "sD", dr["densp"].rearrange("(dc p) t -> p dc t", p=P)[:, dc], DEN)
            W(ctx, "vector", "sC", sch.cur("sC"))
            W(ctx, "vector", "sD", sch.cur("sD"))
        slab_mark(ctx, "qA")
        slab_mark(ctx, "qB")

    sch.on("vector", lambda v, o=ctx.sumstg[:, 32:48], i=ctx.mall[:, 0:16]: v.tensor_copy(o, i))
    sch.on("vector", lambda v, o=ctx.sumstg[:, 48:64], i=ctx.DV["lw"]:
           v.tensor_copy(o, i))

    sr_qts = {}

    def sr_issue(wi):
        slot = ("qA", "qB", "qC")[wi % 3]
        buf = ctx.pool[slot]
        sr_qts[wi] = (slot, buf, slab_dma(ctx, slot,
                      buf.rearrange("p (kc m) -> p kc m", kc=NCC), dr["wrT"][wi]))

    for wi in range(3):
        sr_issue(wi)

    payt = f32a.alloc(128, "payt")
    hi = ctx.hilo[:, 0:64]
    lo = ctx.hilo[:, 64:128]
    Wcur(ctx, "scalar", ["ev", "mm"])
    Wcur(ctx, "vector", ["mm"])
    a3 = sch.cur("ac") + 1
    sch.on("scalar", lambda s, o=hi, i=ctx.sumstg: s.copy(o, i))
    W(ctx, "vector", "ac", a3)
    e5 = sch.cur("ev") + 1
    sch.on("vector", lambda v, o=lo, a=ctx.sumstg, b=hi: v.tensor_sub(o, a, b))
    W(ctx, "tensor", "ev", e5)
    bank_wait(ctx, 6)
    sch.on("tensor", lambda t, o=ctx.PB[6][0:64, 0:128], l=hi, r=ctx.identb:
           t.matmul(o, l, r, start=True, stop=False))
    sch.inc("mm")
    m3 = sch.cur("mm")
    sch.on("tensor", lambda t, o=ctx.PB[6][0:64, 0:128], l=lo, r=ctx.identb, s=sem["mm"]:
           t.matmul(o, l, r, start=False, stop=True).then_inc(s, 1))
    W(ctx, "vector", "mm", m3)
    e6 = sch.cur("ev") + 1
    sch.on("vector", lambda v, o=payt[0:64, :], i=ctx.PB[6][0:64, 0:128]:
           v.tensor_copy(o, i))
    bank_free(ctx, "vector", 6)
    W(ctx, "sync", "ev", e6)
    qp = dma(ctx, "sync", "qS", dr["pay1"], payt[0:64, :])
    W(ctx, "gpsimd", "qS", qp)
    sch.inc("r1")
    sch.on("gpsimd", lambda g, i=dr["pay1"], o=dr["gat1"], s=sem["r1"]:
           g.collective_compute("AllGather", ALU.bypass,
                                replica_groups=[list(range(NCORES))],
                                ins=[i], outs=[o]).then_inc(s, 1))
    f32a.pop()
    bfa.pop()
    bfa.pop()

    bfa.push()
    arena_fence(ctx)
    XR = bfa.alloc(NCC * T, "XR")
    xr3 = XR.rearrange("p (cc t) -> p cc t", cc=NCC)
    SR = bfa.alloc(NCC * T, "SR")
    dlt2 = ctx.hilo[:, 0:1024]
    Wcur(ctx, "vector", ["ev"])
    for cc in range(NCC):
        sch.on("vector", lambda v, o=dlt2, a=h3[:, cc, 1:1025], b=h3[:, cc, 0:1024]:
               v.tensor_sub(o, a, b))
        sch.on("vector", lambda v, o=dlt2, i=dlt2, s1=ctx.VB("tmr")[:, cc:cc + 1]:
               v.tensor_scalar(o, i, s1, None, ALU.mult))
        sch.on("vector", lambda v, o=xr3[:, cc], a=dlt2, b=h3[:, cc, 0:1024]:
               v.tensor_add(o, a, b))
    W(ctx, "tensor", "ev", sch.cur("ev"))

    f32a.push()
    arena_fence(ctx)
    SAt = f32a.alloc(8 * 128, "SAt")
    SBt = f32a.alloc(8 * 128, "SBt")
    Mt = f32a.alloc(8 * 128, "Mt")
    LWt = f32a.alloc(8 * 128, "LWt")
    CF = f32a.alloc(8 * 128, "CF")
    lagT = f32a.alloc(8 * 128, "lagT")
    W(ctx, "sync", "r1", sch.cur("r1"))
    dma(ctx, "sync", "qS", lagT[0:16, :], dr["lag"])
    g1 = dr["gat1"].rearrange("(j r dc) q -> r dc j q", j=8, r=4)
    for r, dstt in ((0, SAt), (1, SBt), (2, Mt), (3, LWt)):
        dma(ctx, "sync", "qS", dstt[0:16, :].rearrange("d (j q) -> d j q", j=8), g1[r])
    W(ctx, "vector", "qS", sch.cur("qS"))
    sch.on("vector", lambda v, o=CF[0:16, :], a=LWt[0:16, :], b=lagT[0:16, :]: v.tensor_mul(o, a, b))
    e7 = sch.cur("ev") + 1
    sch.on("vector", lambda v, o=CF[0:16, :], a=CF[0:16, :], b=Mt[0:16, :]:
           v.tensor_add(o, a, b))
    W(ctx, "scalar", "ev", e7)
    a5 = sch.cur("ac") + 1
    sch.on("scalar", lambda s, o=CF[0:16, :], i=CF[0:16, :]:
           s.activation(o, i, AF.Exp))
    W(ctx, "vector", "ac", a5)
    sch.on("vector", lambda v, o=SAt[0:16, :], a=CF[0:16, :], b=SAt[0:16, :]: v.tensor_mul(o, a, b))
    sch.on("vector", lambda v, o=SBt[0:16, :], a=CF[0:16, :], b=SBt[0:16, :]: v.tensor_mul(o, a, b))
    for dstt in (SAt, SBt):
        for j in range(1, 8):
            sch.on("vector", lambda v, o=dstt[0:16, 0:128], a=dstt[0:16, 0:128],
                   b=dstt[0:16, j * 128:(j + 1) * 128]: v.tensor_add(o, a, b))
    e8 = sch.cur("ev")
    for srcr, dstf in ((SAt, ctx.cafm), (SBt, ctx.cbfm)):
        hi2 = ctx.hilo[0:16, 128:256]
        lo2 = ctx.hilo[0:16, 256:384]
        W(ctx, "scalar", "mm", sch.cur("mm"))
        W(ctx, "vector", "mm", sch.cur("mm"))
        W(ctx, "scalar", "ev", e8)
        a6 = sch.cur("ac") + 1
        sch.on("scalar", lambda s, o=hi2, i=srcr[0:16, 0:128]: s.copy(o, i))
        W(ctx, "vector", "ac", a6)
        e9 = sch.cur("ev") + 1
        sch.on("vector", lambda v, o=lo2, a=srcr[0:16, 0:128], b=hi2:
               v.tensor_sub(o, a, b))
        W(ctx, "tensor", "ev", e9)
        bank_wait(ctx, 6)
        sch.on("tensor", lambda t, o=ctx.PB[6][:, 0:16], l=hi2, r=ctx.identb[0:16, 0:16]:
               t.matmul(o, l, r, start=True, stop=False))
        sch.inc("mm")
        m4 = sch.cur("mm")
        sch.on("tensor", lambda t, o=ctx.PB[6][:, 0:16], l=lo2, r=ctx.identb[0:16, 0:16], s=sem["mm"]:
               t.matmul(o, l, r, start=False, stop=True).then_inc(s, 1))
        W(ctx, "vector", "mm", m4)
        sch.on("vector", lambda v, o=dstf, i=ctx.PB[6][:, 0:16]:
               v.tensor_copy(o, i))
        bank_free(ctx, "vector", 6)
    f32a.pop()

    f32a.push()
    nrc = [f32a.alloc(1024, "nrc") for _ in range(2)]
    drc = [f32a.alloc(1024, "drc") for _ in range(2)]
    ALPH = f32a.alloc(1024, "ALPH")
    store_barrier(ctx, "sync")
    comb_ev = sch.cur("ev")
    comb_ac = sch.cur("ac")
    Wcur(ctx, "vector", ["pl"])
    for wi in range(8):
        slot, buf, qt = sr_qts[wi]
        W(ctx, "tensor", slot, qt)
        b3 = buf.rearrange("p (kc m) -> p kc m", kc=NCC)
        for mi in range(2):
            mc = wi * 2 + mi
            for g in range(2):
                b = ctx.banks.acquire()
                for kc in range(NCC):
                    last = (kc == NCC - 1)
                    if last:
                        sch.inc("mm")
                    sch.on("tensor", lambda t, o=ctx.PB[b][:, 0:512],
                           l=b3[:, kc, mi * 128:(mi + 1) * 128],
                           r=xr3[:, kc, g * 512:(g + 1) * 512],
                           st=(kc == 0), sp=last, fin=last, s=sem["mm"]:
                           (t.matmul(o, l, r, start=st, stop=sp).then_inc(s, 1) if fin
                            else t.matmul(o, l, r, start=st, stop=sp)))
                mt = sch.cur("mm")
                W(ctx, "scalar", "mm", mt)
                sch.on("scalar", lambda s, o=SR[:, mc * T + g * 512: mc * T + (g + 1) * 512],
                       i=ctx.PB[b][:, 0:512]:
                       s.activation(o, i, AF.Sigmoid))
                ctx.banks.free("scalar", b)
        slab_mark(ctx, slot)
        if wi + 3 < 8:
            sr_issue(wi + 3)
        for dc in (wi * 2, wi * 2 + 1):
            W(ctx, "sync", "ev", sch.cur("ev"))
            dma(ctx, "sync", "qS", nrc[dc % 2], dr["numsp"].rearrange("(dc p) t -> p dc t", p=P)[:, dc])
            dma(ctx, "sync", "qS", drc[dc % 2], dr["densp"].rearrange("(dc p) t -> p dc t", p=P)[:, dc])
            qn = sch.cur("qS")
            W(ctx, "vector", "qS", qn)
            if dc == 0:
                W(ctx, "vector", "ev", comb_ev)
                W(ctx, "vector", "ac", comb_ac)
            sch.on("vector", lambda v, o=ALPH, i=ctx.ramp, s1=ctx.DV["w"][:, dc:dc + 1],
                   s2=ctx.mall[:, dc:dc + 1]: v.tensor_scalar(o, i, s1, s2, ALU.mult, ALU.subtract))
            e10 = sch.cur("ev")
            W(ctx, "scalar", "ev", e10)
            a7 = sch.cur("ac") + 1
            sch.on("scalar", lambda s, o=ALPH, i=ALPH: s.activation(o, i, AF.Exp))
            W(ctx, "vector", "ac", a7)
            sch.on("vector", lambda v, o=nrc[dc % 2], i=ALPH, s1=ctx.cafm[:, dc:dc + 1], x=nrc[dc % 2]:
                   v.scalar_tensor_tensor(o, i, s1, x, ALU.mult, ALU.add))
            sch.on("vector", lambda v, o=drc[dc % 2], i=ALPH, s1=ctx.cbfm[:, dc:dc + 1], x=drc[dc % 2]:
                   v.scalar_tensor_tensor(o, i, s1, x, ALU.mult, ALU.add))
            sch.on("vector", lambda v, o=drc[dc % 2], i=drc[dc % 2]: v.reciprocal(o, i))
            sch.on("vector", lambda v, o=nrc[dc % 2], a=nrc[dc % 2], b=drc[dc % 2]: v.tensor_mul(o, a, b))
            sch.on("vector", lambda v, o=SR[:, dc * T:(dc + 1) * T], a=nrc[dc % 2],
                   b=SR[:, dc * T:(dc + 1) * T]: v.tensor_mul(o, a, b))
    rwkv_done = sch.cur("ev")
    f32a.pop()

    f32a.push()
    arena_fence(ctx)
    rw3 = SR.rearrange("p (dc t) -> p dc t", dc=NCC)
    z1t = f32a.alloc(512, "z1t")
    x1q = f32a.alloc(512, "x1q")
    W(ctx, "tensor", "ev", rwkv_done)

    def wo_emit(mc, g, b, mt):
        cs = slice(g * 512, (g + 1) * 512)
        Wcur(ctx, "sync", ["ev"])
        qx = dma(ctx, "sync", "qS", x1q, fmv(dr["x1sp"], mc, cs))
        W(ctx, "vector", "qS", qx)
        W(ctx, "vector", "mm", mt)
        W(ctx, "vector", "sC", sch.cur("sC"))
        sch.on("vector", lambda v, o=z1t, a=x1q, b=mu1[:, cs]: v.tensor_sub(o, a, b))
        sch.on("vector", lambda v, o=z1t, a=z1t, b=r1b[:, cs]: v.tensor_mul(o, a, b))
        sch.on("vector", lambda v, o=z1t, a=z1t, b=ctx.PB[b][:, 0:512]:
               v.tensor_add(o, a, b))
        ctx.banks.free("vector", b)
        W(ctx, "sync", "ev", sch.cur("ev"))
        dma(ctx, "sync", "sC", fmv(dr["z1sp"], mc, cs), z1t)
        if debug_y:
            dma(ctx, "sync", "sC", fmv(dr["y"], mc, cs), z1t)

    proj_stage(ctx, dr["woT"], rw3, wo_emit, windows=8, win=256, sfn="sf3")
    f32a.pop()
    bfa.pop()
    bfa.pop()

    f32a.push(); bfa.push()
    xin2 = f32_scratch(ctx, NCC * 512, "z1re")
    ctx.rows = f32a.alloc(4 * 1024, "rows")
    colb = f32a.alloc(16, "colb2")
    store_barrier(ctx, "sync")
    for g in range(2):
        Wcur(ctx, "sync", ["ev", "ac"])
        for cc in range(NCC):
            dma(ctx, "sync", "qS", xin2[:, cc * 512:(cc + 1) * 512],
                fmv(dr["z1sp"], cc, slice(g * 512, (g + 1) * 512)))
        q = sch.cur("qS")
        W(ctx, "scalar", "qS", q)
        W(ctx, "vector", "qS", q)
        if g == 1:
            for cc in range(NCC):
                if cc == NCC - 1:
                    sch.on("vector", lambda v, o=colb[:, cc:cc + 1],
                           i=xin2[:, cc * 512 + 511:cc * 512 + 512]:
                           v.tensor_copy(o, i))
                else:
                    sch.on("vector", lambda v, o=colb[:, cc:cc + 1],
                           i=xin2[:, cc * 512 + 511:cc * 512 + 512]: v.tensor_copy(o, i))
            send_col_round(ctx, colb, "pay2", "gat2", "r2")
        ln_stats(ctx, lambda cc, _g: xin2[:, cc * 512:(cc + 1) * 512], ctx.bcA, halves=(g,),
                 bcast=(g == 1))
    f32a.pop(); bfa.pop()
    f32a.pop()



def chmix_stage(ctx, debug_y):
    sch, sem, dr = ctx.sch, ctx.sem, ctx.dr
    f32a, bfa = ctx.f32, ctx.bf
    f32a.push(); bfa.push()
    XK2 = bfa.alloc(NCC * T, "XK2")
    xk23 = XK2.rearrange("p (cc t) -> p cc t", cc=NCC)
    bfa.push()
    XR2 = bfa.alloc(NCC * T, "XR2")
    xr23 = XR2.rearrange("p (cc t) -> p cc t", cc=NCC)

    mu2 = ctx.bcA[:, 0:T]
    r2b = ctx.bcA[:, T:2 * T]

    bfa.push()
    Gb = bfa.alloc(NCC * 1025, "G")
    g3 = Gb.rearrange("p (cc t) -> p cc t", cc=NCC)
    f32a.push()
    z1h = f32a.alloc(NCC * 512, "z1h")
    tmp = f32a.alloc(1024, "tmpC")
    store_barrier(ctx, "sync")
    for g in range(2):
        cs = slice(g * 512, (g + 1) * 512)
        Wcur(ctx, "sync", ["ev"])
        for cc in range(NCC):
            dma(ctx, "sync", "qS", z1h[:, cc * 512:(cc + 1) * 512], fmv(dr["z1sp"], cc, cs))
        W(ctx, "vector", "qS", sch.cur("qS"))
        for cc in range(NCC):
            sch.on("vector", lambda v, o=tmp[:, 0:512], a=z1h[:, cc * 512:(cc + 1) * 512],
                   b=mu2[:, cs]: v.tensor_sub(o, a, b))
            sch.on("vector", lambda v, o=tmp[:, 0:512], a=tmp[:, 0:512], b=r2b[:, cs]:
                   v.tensor_mul(o, a, b))
            sch.on("vector", lambda v, o=g3[:, cc, 1 + g * 512:1 + (g + 1) * 512], i=tmp[:, 0:512],
                   s1=ctx.VB("ln2w")[:, cc:cc + 1], s2=ctx.VB("ln2b")[:, cc:cc + 1]:
                   v.tensor_scalar(o, i, s1, s2, ALU.mult, ALU.add))
    zb = f32a.alloc(16, "zb")
    gb = f32a.alloc(16, "gb")
    recv_col(ctx, "gat2", "r2", zb, tmp[:, 0:16])
    boundary_ln(ctx, zb, gb, "ln2w", "ln2b", double_ln=False)
    W(ctx, "vector", "ev", sch.cur("ev"))
    for cc in range(NCC):
        sch.on("vector", lambda v, o=g3[:, cc, 0:1], i=gb[:, cc:cc + 1]:
               v.tensor_copy(o, i))
    dlt = ctx.hilo[:, 0:1024]
    Wcur(ctx, "vector", ["ev"])
    for cc in range(NCC):
        for buf3, vec in ((xk23, "cmk"), (xr23, "cmr")):
            sch.on("vector", lambda v, o=dlt, a=g3[:, cc, 1:1025], b=g3[:, cc, 0:1024]:
                   v.tensor_sub(o, a, b))
            sch.on("vector", lambda v, o=dlt, i=dlt, s1=ctx.VB(vec)[:, cc:cc + 1]:
                   v.tensor_scalar(o, i, s1, None, ALU.mult))
            sch.on("vector", lambda v, o=buf3[:, cc], a=dlt, b=g3[:, cc, 0:1024]:
                   v.tensor_add(o, a, b))
    lerp2 = sch.cur("ev")
    f32a.pop()
    bfa.pop()

    bfa.push()
    arena_fence(ctx)
    sgt = bfa.alloc(512, "sigt")
    W(ctx, "tensor", "ev", lerp2)

    def sig_emit(mc, g, b, mt):
        cs = slice(g * 512, (g + 1) * 512)
        W(ctx, "scalar", "mm", mt)
        W(ctx, "scalar", "sC", sch.cur("sC"))
        sch.on("scalar", lambda s, o=sgt, i=ctx.PB[b][:, 0:512]:
               s.activation(o, i, AF.Sigmoid))
        ctx.banks.free("scalar", b)
        W(ctx, "sync", "ac", sch.cur("ac"))
        dma(ctx, "sync", "sC", dr["sigsp"].rearrange("(cc p) t -> p cc t", p=P)[:, mc, cs], sgt)

    proj_stage(ctx, dr["wcrT"], xr23, sig_emit, windows=8, win=256, sfn="sf3")
    bfa.pop()
    bfa.pop()

    bfa.push()
    arena_fence(ctx)
    KF = bfa.alloc(NFC * HALF, "KF")
    kf3 = KF.rearrange("p (fc t) -> p fc t", fc=NFC)
    rel = ctx.f32.alloc(512, "relscr")
    sgr = Ring(ctx, [bfa.alloc(HALF, "sgq") for _ in range(2)])
    f32a.push()
    rx1 = Ring(ctx, [f32a.alloc(HALF, "x1q") for _ in range(3)])
    rz1 = Ring(ctx, [f32a.alloc(HALF, "z1q") for _ in range(3)])
    x2t = [f32a.alloc(HALF, "x2t") for _ in range(2)]
    W(ctx, "sync", "sC", sch.cur("sC"))

    for h in range(2):
        hs = slice(h * HALF, (h + 1) * HALF)

        def kf_emit(mc, g_unused, b, mt, h=h):
            W(ctx, "scalar", "mm", mt)
            W(ctx, "scalar", "ev", sch.cur("ev"))
            a1 = sch.cur("ac") + 1
            sch.on("scalar", lambda s, o=rel, i=ctx.PB[b][:, 0:512]:
                   s.activation(o, i, AF.Relu))
            ctx.banks.free("scalar", b)
            W(ctx, "vector", "ac", a1)
            sch.on("vector", lambda v, o=kf3[:, mc, 0:HALF], a=rel, b=rel:
                   v.tensor_mul(o, a, b))

        xk_h = XK2.rearrange("p (cc t) -> p cc t", cc=NCC)

        def kf_rhs3(_):
            return None
        for fw in range(NFC):
            slot = ("qA", "qB", "qC")[fw % 3]
            buf = ctx.pool[slot][:, 0:NCC * 128]
            qt = slab_dma(ctx, slot,
                          buf.rearrange("p (kc m) -> p kc m", kc=NCC), dr["wckT"][fw])
            W(ctx, "tensor", slot, qt)
            b3 = buf.rearrange("p (kc m) -> p kc m", kc=NCC)
            for fi in range(1):
                fc = fw
                b = ctx.banks.acquire()
                for kc in range(NCC):
                    last = (kc == NCC - 1)
                    if last:
                        sch.inc("mm")
                    sch.on("tensor", lambda t, o=ctx.PB[b][:, 0:512],
                           l=b3[:, kc, 0:128],
                           r=xk_h[:, kc, hs], st=(kc == 0), sp=last, fin=last, s=sem["mm"]:
                           (t.matmul(o, l, r, start=st, stop=sp).then_inc(s, 1) if fin
                            else t.matmul(o, l, r, start=st, stop=sp)))
                kf_emit(fc, 0, b, sch.cur("mm"))
            slab_mark(ctx, slot)
        kf_done = sch.cur("ev")

        for cg in range(2):
            ccs = range(cg * 8, (cg + 1) * 8)
            W(ctx, "tensor", "ev", kf_done)
            for fc in range(NFC):
                slot = ("qD", "qE", "qF")[fc % 3]
                buf = ctx.pool[slot][:, 0:1024]
                qt = slab_dma(ctx, slot, buf,
                              fmv(dr["wcvT"], fc)[:, cg * 1024:(cg + 1) * 1024])
                W(ctx, "tensor", slot, qt)
                for cc in ccs:
                    pb = ctx.PB[cc % 8][:, 0:512]
                    last = (cc == ccs[-1])
                    if fc == 0:
                        bank_wait(ctx, cc % 8)
                    if last:
                        sch.inc("mm")
                    sch.on("tensor", lambda t, o=pb,
                           l=buf[:, (cc - cg * 8) * 128:(cc - cg * 8 + 1) * 128],
                           r=kf3[:, fc, 0:HALF], st=(fc == 0), sp=(fc == NFC - 1), fin=last, s=sem["mm"]:
                           (t.matmul(o, l, r, start=st, stop=sp).then_inc(s, 1) if fin
                            else t.matmul(o, l, r, start=st, stop=sp)))
                slab_mark(ctx, slot)
            mdone = sch.cur("mm")
            for cc in ccs:
                x1b, j1 = rx1.load(fmv(dr["x1sp"], cc, hs))
                z1b, j2 = rz1.load(fmv(dr["z1sp"], cc, hs))
                sgb, j3 = sgr.load(dr["sigsp"].rearrange("(cc p) t -> p cc t", p=P)[:, cc, hs])
                W(ctx, "vector", "mm", mdone)
                sslot = "sA" if cc % 2 == 0 else "sB"
                W(ctx, "vector", sslot, sch.cur(sslot))
                pb = ctx.PB[cc % 8][:, 0:512]
                ot = x2t[cc % 2]
                sch.on("vector", lambda v, o=ot, a=sgb, b=pb: v.tensor_mul(o, a, b))
                bank_free(ctx, "vector", cc % 8)
                sch.on("vector", lambda v, o=ot, a=ot, b=x1b: v.tensor_add(o, a, b))
                sch.on("vector", lambda v, o=ot, a=ot, b=z1b: v.tensor_add(o, a, b))
                for j, rr in ((j1, rx1), (j2, rz1), (j3, sgr)):
                    rr.consumed(j)
                W(ctx, "sync", "ev", sch.cur("ev"))
                dma(ctx, "sync", sslot, fmv(dr["x2sp"], cc, hs), ot)
                if debug_y:
                    dma(ctx, "sync", sslot, fmv(dr["y"], cc, hs), ot)
    f32a.pop()
    bfa.pop()
    bfa.pop()
    f32a.pop()



def kernel(**inputs):
    upto = inputs.pop("_upto", UPTO)
    d = inputs
    xT = np.ascontiguousarray(np.asarray(d["L0_attn_x"])[0].T.astype(np.float32))

    def tr(a):
        return np.ascontiguousarray(np.asarray(a, dtype=np.float32).T)

    def tile4(w2d, win):
        m = w2d.shape[1]
        return np.ascontiguousarray(
            w2d.reshape(NCC, P, m // win, win).transpose(2, 1, 0, 3))

    wkT = tile4(tr(d["L0_attn_Wk"]), 256)
    wvT = tile4(tr(d["L0_attn_Wv"]), 256)
    wrT = tile4(tr(d["L0_attn_Wr"]), 256)
    woT = tile4(tr(d["L0_attn_Wo"]), 256)
    wcrT = tile4(tr(d["L0_attn_Wcr"]), 256)
    wckT = tile4(tr(d["L0_attn_Wck"]), 128)
    wcvT = tr(d["L0_attn_Wcv"])
    wgate = tile4(np.ascontiguousarray(np.asarray(d["L0_ffn_w_gate"], np.int8)), 128)
    wup = tile4(np.ascontiguousarray(np.asarray(d["L0_ffn_w_up"], np.int8)), 128)
    wdown = np.ascontiguousarray(np.asarray(d["L0_ffn_w_down"], np.int8))

    def fm(v):
        return np.ascontiguousarray(np.asarray(v, np.float32).reshape(16, 128).T)

    vecs = [d["L0_attn_ln1_w"], d["L0_attn_ln1_b"], d["L0_attn_ln2_w"], d["L0_attn_ln2_b"],
            np.asarray(d["L0_attn_tm_k"]).reshape(-1), np.asarray(d["L0_attn_tm_v"]).reshape(-1),
            np.asarray(d["L0_attn_tm_r"]).reshape(-1),
            np.asarray(d["L0_attn_cm_k"]).reshape(-1), np.asarray(d["L0_attn_cm_r"]).reshape(-1),
            d["L0_attn_time_decay"], d["L0_attn_time_first"], d["L0_ffn_s_down"]]
    vecbank = np.hstack([fm(v) for v in vecs]).astype(np.float32)
    fbank = np.hstack([np.ascontiguousarray(np.asarray(v, np.float32).reshape(64, 128).T)
                       for v in (d["L0_ffn_s_gate"], d["L0_ffn_s_up"])]).astype(np.float32)
    ident = np.eye(P, dtype=np.float32)

    in_maps = []
    for c in range(NCORES):
        lag = np.full((16, NCORES * P), 1e6, dtype=np.float32)
        for j in range(c):
            lag[:, j * P:(j + 1) * P] = float(c - 1 - j)
        bm = np.full((P, 1), 0.0 if c == 0 else 1.0, dtype=np.float32)
        pm = np.zeros((P, 8), dtype=np.float32)
        if c > 0:
            pm[:, c - 1] = 1.0
        m = {"x": np.ascontiguousarray(xT[:, c * T:(c + 1) * T]),
             "vecbank": vecbank, "fbank": fbank, "ident": ident,
             "lag": lag, "bmask": bm, "pmask": pm,
             "wgate": wgate, "wup": wup, "wdown": wdown,
             "wkT": wkT, "wvT": wvT, "wrT": wrT, "woT": woT, "wcrT": wcrT,
             "wckT": wckT, "wcvT": wcvT}
        in_maps.append(m)

    nc = build_nc(upto=upto)
    res = run_bass_kernel_spmd(nc, in_maps, core_ids=list(range(NCORES)))
    out = np.empty((8192, 2048), dtype=np.float32)
    for c in range(NCORES):
        out[c * T:(c + 1) * T, :] = res.results[c]["y"].T
    return out.reshape(1, 8192, 2048)


# version 17 (final)
# speedup vs baseline: 1.2375x; 1.0344x over previous
import numpy as np
import ml_dtypes
import concourse.bass as bass
import concourse.mybir as mybir
from concourse.bass_utils import run_bass_kernel_spmd

F32 = mybir.dt.float32
BF16 = mybir.dt.bfloat16
I8 = mybir.dt.int8
AF = mybir.ActivationFunctionType
ALU = mybir.AluOpType
AX = mybir.AxisListType.X

P = 128
C = 2048
NCC = 16
FF = 8192
NFC = 64
T = 1024
HALF = 512
QUAR = 256
NCORES = 8
EPS = 1e-5
CINV = 1.0 / C
UPTO = 4

ENGS = ("sync", "tensor", "vector", "scalar", "gpsimd")

VECS = ["ln1w", "ln1b", "ln2w", "ln2b", "tmk", "tmv", "tmr", "cmk", "cmr",
        "td", "tf", "sdown"]
VID = {v: i for i, v in enumerate(VECS)}

SEM_NAMES = ["qS", "qG", "qA", "qB", "qC", "qD", "qE", "qF", "mm", "ev", "ac", "gc", "pl",
             "r0", "r1", "r2", "sA", "sB", "sC", "sD", "sE", "sF",
             "qR0", "qR1", "qR2", "qR3", "qR4", "qR5", "qR6", "qR7"]

NF32 = 16100
NBF = 74200


class Sch:

    def __init__(self):
        self.ops = {e: [] for e in ENGS}
        self.cnt = {}
        self.ctx = None

    def inc(self, n, by=1):
        self.cnt[n] = self.cnt.get(n, 0) + by
        return self.cnt[n]

    def cur(self, n):
        return self.cnt.get(n, 0)

    def on(self, eng, fn, chain=True):
        if chain and eng in ("vector", "scalar", "gpsimd"):
            sname = {"vector": "ev", "scalar": "ac", "gpsimd": "gc"}[eng]
            idx = self.inc(sname)
            sem = self.ctx.sem[sname]

            def wrapped(e, fn=fn, idx=idx, sem=sem):
                if idx > 1:
                    e.wait_ge(sem, idx - 1)
                inst = fn(e)
                inst.then_inc(sem, 1)
                return inst
            self.ops[eng].append(wrapped)
        else:
            self.ops[eng].append(fn)


class Arena:
    def __init__(self, tensor, nelem, name):
        self.t = tensor
        self.n = nelem
        self.off = 0
        self.marks = []
        self.hi = 0
        self.name = name

    def alloc(self, nelem, name=""):
        assert self.off + nelem <= self.n, \
            f"{self.name} arena overflow: {name} needs {nelem} at {self.off}/{self.n}"
        sl = self.t[:, self.off:self.off + nelem]
        self.off += nelem
        self.hi = max(self.hi, self.off)
        return sl

    def push(self):
        self.marks.append(self.off)

    def pop(self):
        self.off = self.marks.pop()


class Ctx:
    pass


def W(ctx, eng, sem_name, val):
    if val > 0:
        ctx.sch.on(eng, lambda e, s=ctx.sem[sem_name], v=val: e.wait_ge(s, v), chain=False)


def Wcur(ctx, eng, names):
    for n in names:
        W(ctx, eng, n, ctx.sch.cur(n))


def milestone(ctx, eng, sem_name):
    t = ctx.sch.inc(sem_name)
    ctx.sch.on(eng, lambda e, s=ctx.sem[sem_name]: e.nop().then_inc(s, 1), chain=False)
    return t


def dma(ctx, eng, qsem, out_ap, in_ap, **kw):
    tgt = ctx.sch.inc(qsem, 16)
    ctx.sch.on(eng, lambda e, o=out_ap, i=in_ap, s=ctx.sem[qsem], kk=kw:
               e.dma_start(out=o, in_=i, **kk).then_inc(s, 16), chain=False)
    return tgt


class Banks:

    def __init__(self, ctx):
        self.ctx = ctx
        self.idxs = [0, 1, 2, 3, 4, 5]
        self.rr = 0

    def acquire(self):
        b = self.idxs[self.rr]
        self.rr = (self.rr + 1) % len(self.idxs)
        bank_wait(self.ctx, b)
        return b

    def free(self, eng, b):
        bank_rec(self.ctx, eng, b)


def bank_wait(ctx, b, eng="tensor"):
    rec = ctx.bankrec.get(b)
    if rec is not None:
        W(ctx, eng, rec[0], rec[1])


def bank_rec(ctx, eng, b):
    s = {"vector": "ev", "scalar": "ac", "tensor": "mm"}[eng]
    ctx.bankrec[b] = (s, ctx.sch.cur(s))


def bank_wait_all(ctx):
    for b in range(8):
        bank_wait(ctx, b)


def bank_free(ctx, eng, b):
    bank_rec(ctx, eng, b)


STORE_SLOTS = ("sA", "sB", "sC", "sD", "sE", "sF")


def arena_fence(ctx):
    for s in ("ev", "ac", "mm"):
        W(ctx, "sync", s, ctx.sch.cur(s))
    for e in ("vector", "scalar"):
        for s in STORE_SLOTS:
            W(ctx, e, s, ctx.sch.cur(s))


def store_barrier(ctx, eng):
    for s in STORE_SLOTS:
        W(ctx, eng, s, ctx.sch.cur(s))


def slab_dma(ctx, slot, out_ap, in_ap):
    marks = ctx.streams.setdefault(slot, [])
    if marks:
        for s, v in marks[-1]:
            W(ctx, "gpsimd", s, v)
    return dma(ctx, "gpsimd", slot, out_ap, in_ap)


def slab_mark(ctx, slot, extra=()):
    ctx.streams.setdefault(slot, []).append((("mm", ctx.sch.cur("mm")),) + tuple(extra))



def build_nc(upto=UPTO):
    nc = bass.Bass("TRN2", target_bir_lowering=False, debug=False)
    ctx = Ctx()
    ctx.nc = nc
    ctx.sch = Sch()
    ctx.sch.ctx = ctx

    dr = {}
    dr["x"] = nc.dram_tensor("x", [C, T], F32, kind="ExternalInput").ap()
    dr["vecbank"] = nc.dram_tensor("vecbank", [P, 16 * len(VECS)], F32, kind="ExternalInput").ap()
    dr["fbank"] = nc.dram_tensor("fbank", [P, 2 * NFC], F32, kind="ExternalInput").ap()
    dr["ident"] = nc.dram_tensor("ident", [P, P], F32, kind="ExternalInput").ap()
    dr["lag"] = nc.dram_tensor("lag", [16, NCORES * P], F32, kind="ExternalInput").ap()
    dr["bmask"] = nc.dram_tensor("bmask", [P, 1], F32, kind="ExternalInput").ap()
    dr["pmask"] = nc.dram_tensor("pmask", [P, 8], F32, kind="ExternalInput").ap()
    dr["wgate"] = nc.dram_tensor("wgate", [NFC, P, NCC, 128], I8, kind="ExternalInput").ap()
    dr["wup"] = nc.dram_tensor("wup", [NFC, P, NCC, 128], I8, kind="ExternalInput").ap()
    dr["wdown"] = nc.dram_tensor("wdown", [FF, C], I8, kind="ExternalInput").ap()
    for nm in ("wkT", "wvT", "wrT", "woT", "wcrT"):
        dr[nm] = nc.dram_tensor(nm, [8, P, NCC, 256], F32, kind="ExternalInput").ap()
    dr["wckT"] = nc.dram_tensor("wckT", [NFC, P, NCC, 128], F32, kind="ExternalInput").ap()
    dr["wcvT"] = nc.dram_tensor("wcvT", [FF, C], F32, kind="ExternalInput").ap()
    dr["y"] = nc.dram_tensor("y", [C, T], F32, kind="ExternalOutput").ap()

    dr["x1sp"] = nc.dram_tensor("x1sp", [C, T], F32).ap()
    dr["z1sp"] = nc.dram_tensor("z1sp", [C, T], F32).ap()
    dr["x2sp"] = nc.dram_tensor("x2sp", [C, T], F32).ap()
    dr["numsp"] = nc.dram_tensor("numsp", [C, T], F32).ap()
    dr["densp"] = nc.dram_tensor("densp", [C, T], F32).ap()
    dr["sigsp"] = nc.dram_tensor("sigsp", [C, T], BF16).ap()
    dr["pay0"] = nc.dram_tensor("pay0", [P, 16], F32).ap()
    dr["gat0"] = nc.dram_tensor("gat0", [NCORES * P, 16], F32).ap()
    dr["pay1"] = nc.dram_tensor("pay1", [64, P], F32).ap()
    dr["gat1"] = nc.dram_tensor("gat1", [NCORES * 64, P], F32).ap()
    dr["pay2"] = nc.dram_tensor("pay2", [P, 16], F32).ap()
    dr["gat2"] = nc.dram_tensor("gat2", [NCORES * P, 16], F32).ap()
    ctx.dr = dr

    import contextlib
    with contextlib.ExitStack() as es:
        AF32 = es.enter_context(nc.sbuf_tensor("af32", [P, NF32], F32))
        ABF = es.enter_context(nc.sbuf_tensor("abf", [P, NBF], BF16))
        ctx.PB = [es.enter_context(nc.psum_tensor(f"pb{i}", [P, 512], F32)) for i in range(8)]
        ctx.sem = {n: es.enter_context(nc.semaphore(n)) for n in SEM_NAMES}
        block = es.enter_context(nc.Block())

        ctx.f32 = Arena(AF32, NF32, "f32")
        ctx.bf = Arena(ABF, NBF, "bf16")
        ctx.bankrec = {}
        ctx.streams = {}
        ctx.banks = Banks(ctx)

        build_program(ctx, upto)

        sch = ctx.sch
        for eng in ENGS:
            getattr(block, eng)(lambda e, ops=tuple(sch.ops[eng]): [f(e) for f in ops])

    import os
    if os.environ.get("KARENA"):
        print(f"arena peaks: f32={ctx.f32.hi} bf={ctx.bf.hi} "
              f"bytes={ctx.f32.hi*4 + ctx.bf.hi*2}")
    return nc



def build_program(ctx, upto):
    sch, sem, dr = ctx.sch, ctx.sem, ctx.dr
    f32a, bfa = ctx.f32, ctx.bf

    ctx.vb = f32a.alloc(16 * len(VECS), "vecbank")
    ctx.fb = f32a.alloc(2 * NFC, "fbank")
    ctx.dv = f32a.alloc(16 * 6, "derived")
    ctx.ramp = f32a.alloc(T, "ramp")
    ctx.r8 = f32a.alloc(8, "r8")
    ctx.bcA = f32a.alloc(2 * T, "bcastA")
    ctx.bcB = f32a.alloc(3 * T, "bcastB")
    ctx.bmask = f32a.alloc(1, "bmask")
    ctx.pmask = f32a.alloc(8, "pmask")
    ctx.mall = f32a.alloc(16, "m_all")
    ctx.cafm = f32a.alloc(16, "ca_fm")
    ctx.cbfm = f32a.alloc(16, "cb_fm")
    ctx.sumstg = f32a.alloc(64, "sumstg")
    ctx.identb = bfa.alloc(P, "identb")
    onescol = bfa.alloc(1, "ones_col")
    onesrow_r = bfa.alloc(P, "ones_row")
    ctx.ones_bf = onescol
    ctx.onesrow_bf = onesrow_r[0:1, :]
    ctx.hilo = bfa.alloc(2 * 1024, "hilo")
    ctx.pool = {"qA": bfa.alloc(4096, "poolA"), "qB": bfa.alloc(4096, "poolB"),
                "qC": bfa.alloc(4096, "poolC"), "qD": bfa.alloc(4096, "poolD"),
                "qE": bfa.alloc(2048, "poolE"), "qF": bfa.alloc(2048, "poolF")}

    def VB(name):
        return ctx.vb[:, VID[name] * 16:(VID[name] + 1) * 16]
    ctx.VB = VB
    DV = {n: ctx.dv[:, i * 16:(i + 1) * 16] for i, n in
          enumerate(["w", "d", "eu", "lw", "shalf", "scr"])}
    ctx.DV = DV

    creg = f32a.alloc(2, "consts")
    ctx.nc.const_aps.register(F32, EPS, creg[:, 0:1])
    ctx.nc.const_aps.register(F32, 0.0, creg[:, 1:2])
    sch.on("vector", lambda v, o=creg[:, 0:1]: v.memset(o, EPS))
    sch.on("vector", lambda v, o=creg[:, 1:2]: v.memset(o, 0.0))
    dma(ctx, "sync", "qS", ctx.vb, dr["vecbank"])
    dma(ctx, "sync", "qS", ctx.fb, dr["fbank"])
    dma(ctx, "sync", "qS", ctx.bmask, dr["bmask"])
    dma(ctx, "sync", "qS", ctx.pmask, dr["pmask"])
    dma(ctx, "gpsimd", "qG", ctx.identb, dr["ident"])
    Wcur(ctx, "scalar", ["qS"])
    Wcur(ctx, "scalar", ["ev"])
    sch.on("scalar", lambda s: s.activation(DV["scr"], VB("td"), AF.Exp))
    sch.on("scalar", lambda s: s.mul(DV["w"], DV["scr"], -1.0))
    sch.on("scalar", lambda s: s.activation(DV["d"], DV["w"], AF.Exp))
    sch.on("scalar", lambda s: s.activation(DV["eu"], VB("tf"), AF.Exp))
    sch.on("scalar", lambda s: s.mul(DV["lw"], DV["w"], float(T)))
    sch.on("scalar", lambda s: s.mul(DV["shalf"], VB("sdown"), 0.5))
    sch.on("vector", lambda v: v.memset(ctx.ones_bf, 1.0))
    sch.on("vector", lambda v: v.memset(ctx.onesrow_bf, 1.0))
    sch.inc("pl")
    sch.on("gpsimd", lambda g: g.iota(ctx.ramp, pattern=[[1, T]], base=0, channel_multiplier=0,
                                      allow_small_or_imprecise_dtypes=True).then_inc(sem["pl"], 1),
           chain=False)

    ffn_stage(ctx, src=dr["x"], dst=dr["x1sp"], res=dr["x"], do_stats=True,
              debug_y=(upto == 1))
    if upto == 1:
        return finish(ctx)
    attn_mix_stage(ctx, debug_y=(upto == 2))
    if upto == 2:
        return finish(ctx)
    chmix_stage(ctx, debug_y=(upto == 3))
    if upto == 3:
        return finish(ctx)
    ffn_stage(ctx, src=dr["x2sp"], dst=dr["y"], res=dr["x2sp"], do_stats=False,
              debug_y=False)
    return finish(ctx)


def finish(ctx):
    for s in STORE_SLOTS + ("qS", "qG", "qA", "qB", "qC", "qD", "qE", "qF"):
        W(ctx, "gpsimd", s, ctx.sch.cur(s))



class Ring:

    def __init__(self, ctx, bufs, sems):
        self.ctx = ctx
        self.bufs = bufs
        self.sems = sems
        assert len(sems) == len(bufs)
        self.cons = [(0, 0)] * len(bufs)
        self.i = 0

    def load(self, src, engs=("vector",)):
        ctx = self.ctx
        j = self.i % len(self.bufs)
        self.i += 1
        ev, ac = self.cons[j]
        W(ctx, "sync", "ev", ev)
        W(ctx, "sync", "ac", ac)
        sname = self.sems[j]
        W(ctx, "sync", sname, ctx.sch.cur(sname))
        qt = dma(ctx, "sync", sname, self.bufs[j], src)
        for e in engs:
            W(ctx, e, sname, qt)
        return self.bufs[j], j

    def consumed(self, j):
        self.cons[j] = (self.ctx.sch.cur("ev"), self.ctx.sch.cur("ac"))


def f32_scratch(ctx, n, name="f32scr"):
    if ctx.bf.off % 2:
        ctx.bf.alloc(1, "align")
    return ctx.bf.alloc(2 * n, name).bitcast(F32)


def fmv(dram, cc=None, cols=None):
    v = dram.rearrange("(cc p) t -> p cc t", p=P)
    if cc is None:
        return v
    if cols is None:
        return v[:, cc]
    return v[:, cc, cols]


def ln_stats(ctx, xin_fn, bc, double_ln=False, halves=(0, 1), bcast=True):
    sch, sem = ctx.sch, ctx.sem
    bfa = ctx.bf
    bfa.push()
    xbf = [bfa.alloc(512, "xbf") for _ in range(2)]
    xsq = [bfa.alloc(512, "xsq") for _ in range(2)]
    rows = ctx.rows
    mu_row = rows[0:1, 0:1024]
    msq_row = rows[0:1, 1024:2048]
    var_row = rows[0:1, 2048:3072]
    r_row = rows[0:1, 3072:4096]

    for g in halves:
        cs = slice(g * 512, (g + 1) * 512)
        bank_wait(ctx, 6)
        bank_wait(ctx, 7)
        mmcc = {}
        for cc in range(NCC):
            xa = xin_fn(cc, g)
            sl = cc % 2
            if cc >= 2:
                W(ctx, "scalar", "mm", mmcc[cc - 2])
                W(ctx, "vector", "mm", mmcc[cc - 2])
            a1 = sch.cur("ac") + 1
            sch.on("scalar", lambda s, o=xbf[sl], i=xa: s.copy(o, i))
            e1 = sch.cur("ev") + 1
            sch.on("vector", lambda v, o=xsq[sl], i=xa: v.tensor_mul(o, i, i))
            W(ctx, "tensor", "ac", a1)
            W(ctx, "tensor", "ev", e1)
            sch.on("tensor", lambda t, o=ctx.PB[6][0:1, 0:512], l=ctx.ones_bf, r=xbf[sl],
                   st=(cc == 0), sp=(cc == NCC - 1): t.matmul(o, l, r, start=st, stop=sp))
            mmcc[cc] = sch.inc("mm")
            sch.on("tensor", lambda t, o=ctx.PB[7][0:1, 0:512], l=ctx.ones_bf, r=xsq[sl],
                   st=(cc == 0), sp=(cc == NCC - 1), s=sem["mm"]:
                   t.matmul(o, l, r, start=st, stop=sp).then_inc(s, 1))
        mdone = sch.cur("mm")
        W(ctx, "vector", "mm", mdone)
        sch.on("vector", lambda v, o=mu_row[:, cs], i=ctx.PB[6][0:1, 0:512]:
               v.tensor_scalar(o, i, CINV, None, ALU.mult))
        sch.on("vector", lambda v, o=msq_row[:, cs], i=ctx.PB[7][0:1, 0:512]:
               v.tensor_scalar(o, i, CINV, None, ALU.mult))
        bank_free(ctx, "vector", 6)
        bank_free(ctx, "vector", 7)
        sch.on("vector", lambda v, o=var_row[:, cs], a=mu_row[:, cs]: v.tensor_mul(o, a, a))
        e2 = sch.cur("ev") + 1
        sch.on("vector", lambda v, o=var_row[:, cs], a=msq_row[:, cs], b=var_row[:, cs]:
               v.tensor_sub(o, a, b))
        W(ctx, "scalar", "ev", e2)
        a2 = sch.cur("ac") + 1
        sch.on("scalar", lambda s, o=r_row[:, cs], i=var_row[:, cs]:
               s.activation(o, i, AF.Sqrt, bias=EPS))
        W(ctx, "vector", "ac", a2)
        sch.on("vector", lambda v, o=r_row[:, cs], i=r_row[:, cs]:
               v.reciprocal(o, i))
    bfa.pop()
    if not bcast:
        return
    bcast_row(ctx, mu_row, bc[:, 0:T])
    bcast_row(ctx, r_row, bc[:, T:2 * T])
    if double_ln:
        t1 = msq_row
        sch.on("vector", lambda v, o=t1, a=r_row, b=r_row: v.tensor_mul(o, a, b))
        e3 = sch.cur("ev") + 1
        sch.on("vector", lambda v, o=t1, a=var_row, b=t1: v.tensor_mul(o, a, b))
        W(ctx, "scalar", "ev", e3)
        a3 = sch.cur("ac") + 1
        sch.on("scalar", lambda s, o=t1, i=t1: s.activation(o, i, AF.Sqrt, bias=EPS))
        W(ctx, "vector", "ac", a3)
        sch.on("vector", lambda v, o=t1, i=t1: v.reciprocal(o, i))
        sch.on("vector", lambda v, o=t1, a=r_row, b=t1: v.tensor_mul(o, a, b))
        bcast_row(ctx, t1, bc[:, 2 * T:3 * T])


def bcast_row(ctx, row_f32, dst_f32, n=T):
    sch, sem = ctx.sch, ctx.sem
    hi = ctx.hilo[0:1, 0:1024]
    lo = ctx.hilo[0:1, 1024:2048]
    Wcur(ctx, "scalar", ["ev", "mm"])
    Wcur(ctx, "vector", ["mm"])
    a1 = sch.cur("ac") + 1
    sch.on("scalar", lambda s, o=hi[:, 0:n], i=row_f32[:, 0:n]: s.copy(o, i))
    W(ctx, "vector", "ac", a1)
    e1 = sch.cur("ev") + 1
    sch.on("vector", lambda v, o=lo[:, 0:n], a=row_f32[:, 0:n], b=hi[:, 0:n]:
           v.tensor_sub(o, a, b))
    W(ctx, "tensor", "ev", e1)
    for g in range((n + 511) // 512):
        lo_c = g * 512
        hi_c = min(n, (g + 1) * 512)
        wdt = hi_c - lo_c
        bank_wait(ctx, 7)
        sch.on("tensor", lambda t, o=ctx.PB[7][:, 0:wdt], l=ctx.onesrow_bf, r=hi[:, lo_c:hi_c]:
               t.matmul(o, l, r, start=True, stop=False))
        sch.inc("mm")
        m1 = sch.cur("mm")
        sch.on("tensor", lambda t, o=ctx.PB[7][:, 0:wdt], l=ctx.onesrow_bf, r=lo[:, lo_c:hi_c],
               s=sem["mm"]: t.matmul(o, l, r, start=False, stop=True).then_inc(s, 1))
        W(ctx, "vector", "mm", m1)
        sch.on("vector", lambda v, o=dst_f32[:, lo_c:hi_c], i=ctx.PB[7][:, 0:wdt]:
               v.tensor_copy(o, i))
        bank_free(ctx, "vector", 7)


def proj_stage(ctx, wdram, rhs3, out_emit, windows=4, win=512, sfn="sf3",
               slots=("qA", "qB", "qC"), rhs_cols=(0, 1)):
    sch, sem = ctx.sch, ctx.sem
    per = win // 128
    qts = {}

    def issue(wi):
        slot = slots[wi % len(slots)]
        buf = ctx.pool[slot][:, 0:NCC * win]
        qts[wi] = (slot, buf, slab_dma(ctx, slot,
                   buf.rearrange("p (kc m) -> p kc m", kc=NCC), wdram[wi]))

    for wi in range(min(len(slots), windows)):
        issue(wi)
    for wi in range(windows):
        slot, buf, qt = qts[wi]
        W(ctx, "tensor", slot, qt)
        b3 = buf.rearrange("p (kc m) -> p kc m", kc=NCC)
        for mi in range(per):
            mc = wi * per + mi
            for g in rhs_cols:
                b = ctx.banks.acquire()
                for kc in range(NCC):
                    last = (kc == NCC - 1)
                    if last:
                        sch.inc("mm")
                    sch.on("tensor", lambda t, o=ctx.PB[b][:, 0:512],
                           l=b3[:, kc, mi * 128:(mi + 1) * 128],
                           r=rhs3[:, kc, g * 512:(g + 1) * 512],
                           st=(kc == 0), sp=last, fin=last, s=sem["mm"]:
                           (t.matmul(o, l, r, start=st, stop=sp).then_inc(s, 1) if fin
                            else t.matmul(o, l, r, start=st, stop=sp)))
                out_emit(mc, g, b, sch.cur("mm"))
        slab_mark(ctx, slot)
        if wi + len(slots) < windows:
            issue(wi + len(slots))



def ffn_stage(ctx, src, dst, res, do_stats, debug_y):
    sch, sem, dr = ctx.sch, ctx.sem, ctx.dr
    f32a, bfa = ctx.f32, ctx.bf
    f32a.push(); bfa.push()
    xhat = bfa.alloc(NCC * HALF, "xhat")
    xh3 = xhat.rearrange("p (cc t) -> p cc t", cc=NCC)

    f32a.push(); bfa.push()
    arena_fence(ctx)
    ctx.rows = f32a.alloc(4 * 1024, "rows")
    iring = Ring(ctx, [f32a.alloc(512, "xin") for _ in range(4)], ["qR0", "qR1", "qR2", "qR3"])
    store_barrier(ctx, "sync")
    ipend = [None]

    def in_xin(cc, g):
        if ipend[0] is not None:
            iring.consumed(ipend[0])
        buf, j = iring.load(fmv(src, cc, slice(g * 512, (g + 1) * 512)),
                            engs=("vector", "scalar"))
        ipend[0] = j
        return buf

    for g in range(2):
        ln_stats(ctx, in_xin, ctx.bcA, halves=(g,), bcast=(g == 1))
        iring.consumed(ipend[0])
        ipend[0] = None

    f32a.pop(); bfa.pop()

    f32a.push(); bfa.push()
    arena_fence(ctx)
    fuse = bfa.alloc(NFC * HALF, "fuse")
    f3 = fuse.rearrange("p (fc t) -> p fc t", fc=NFC)
    gsc = [f32a.alloc(HALF, "gsc") for _ in range(2)]
    upsc = [f32a.alloc(HALF, "upsc") for _ in range(2)]
    xring = Ring(ctx, [f32a.alloc(HALF, "xres") for _ in range(4)], ["qR0", "qR1", "qR2", "qR3"])
    xout = [f32a.alloc(HALF, "xout") for _ in range(2)]

    for h in range(2):
        hs = slice(h * HALF, (h + 1) * HALF)
        W(ctx, "tensor", "mm", sch.cur("mm"))
        for cc in range(NCC):
            xr, jr = xring.load(fmv(src, cc, hs))
            if cc < 2:
                W(ctx, "vector", "mm", sch.cur("mm"))
            sch.on("vector", lambda v, o=gsc[cc % 2], a=xr, b=ctx.bcA[:, 0:T][:, hs]:
                   v.tensor_sub(o, a, b))
            sch.on("vector", lambda v, o=xh3[:, cc, 0:HALF], a=gsc[cc % 2], b=ctx.bcA[:, T:2 * T][:, hs]:
                   v.tensor_mul(o, a, b))
            xring.consumed(jr)
        xhat_done = sch.cur("ev")
        W(ctx, "tensor", "ev", xhat_done)
        for fc in range(NFC):
            slotg = ("qA", "qB", "qC")[fc % 3]
            slotu = ("qD", "qE", "qF")[fc % 3]
            bufg = ctx.pool[slotg][:, 0:NCC * 128]
            bufu = ctx.pool[slotu][:, 0:NCC * 128]
            qtg = slab_dma(ctx, slotg,
                           bufg.rearrange("p (kc m) -> p kc m", kc=NCC), dr["wgate"][fc])
            qtu = slab_dma(ctx, slotu,
                           bufu.rearrange("p (kc m) -> p kc m", kc=NCC), dr["wup"][fc])
            for wsel in ("g", "u"):
                W(ctx, "tensor", slotg if wsel == "g" else slotu,
                  qtg if wsel == "g" else qtu)
                b3 = (bufg if wsel == "g" else bufu).rearrange("p (kc m) -> p kc m", kc=NCC)
                b = ctx.banks.acquire()
                for kc in range(NCC):
                    last = (kc == NCC - 1)
                    if last:
                        sch.inc("mm")
                    sch.on("tensor", lambda t, o=ctx.PB[b][:, 0:512],
                           l=b3[:, kc, 0:128],
                           r=xh3[:, kc, 0:HALF], st=(kc == 0), sp=last, fin=last, s=sem["mm"]:
                           (t.matmul(o, l, r, start=st, stop=sp).then_inc(s, 1) if fin
                            else t.matmul(o, l, r, start=st, stop=sp)))
                last_mm = sch.cur("mm")
                if wsel == "g":
                    W(ctx, "scalar", "mm", last_mm)
                    W(ctx, "scalar", "ev", sch.cur("ev"))
                    sch.on("scalar", lambda s, o=gsc[fc % 2], i=ctx.PB[b][:, 0:512],
                           sc=ctx.fb[:, fc:fc + 1]:
                           s.activation(o, i, AF.Silu, scale=sc))
                    ctx.banks.free("scalar", b)
                else:
                    W(ctx, "vector", "mm", last_mm)
                    sch.on("vector", lambda v, o=upsc[fc % 2], i=ctx.PB[b][:, 0:512],
                           sc=ctx.fb[:, NFC + fc:NFC + fc + 1]:
                           v.tensor_scalar(o, i, sc, None, ALU.mult))
                    ctx.banks.free("vector", b)
                    W(ctx, "vector", "ac", sch.cur("ac"))
                    sch.on("vector", lambda v, o=f3[:, fc, 0:HALF],
                           a=gsc[fc % 2], b2=upsc[fc % 2]:
                           v.tensor_mul(o, a, b2))
            slab_mark(ctx, slotg)
            slab_mark(ctx, slotu)
        fuse_done = sch.cur("ev")

        for cg in range(2):
            ccs = range(cg * 8, (cg + 1) * 8)
            W(ctx, "tensor", "ev", fuse_done)
            for fp2 in range(NFC // 2):
                slot = ("qA", "qB", "qC")[fp2 % 3]
                buf = ctx.pool[slot]
                qt = slab_dma(ctx, slot,
                              buf.rearrange("p (f c) -> p f c", f=2),
                              fmv(dr["wdown"])[:, 2 * fp2:2 * fp2 + 2])
                W(ctx, "tensor", slot, qt)
                for fi in range(2):
                    fc = 2 * fp2 + fi
                    for cc in ccs:
                        pb = ctx.PB[cc % 8][:, 0:512]
                        last = (cc == ccs[-1])
                        if fc == 0:
                            bank_wait(ctx, cc % 8)
                        if last:
                            sch.inc("mm")
                        sch.on("tensor", lambda t, o=pb, l=buf[:, (fi * NCC + cc) * 128:(fi * NCC + cc + 1) * 128],
                               r=f3[:, fc, 0:HALF], st=(fc == 0), sp=(fc == NFC - 1), fin=last, s=sem["mm"]:
                               (t.matmul(o, l, r, start=st, stop=sp).then_inc(s, 1) if fin
                                else t.matmul(o, l, r, start=st, stop=sp)))
                slab_mark(ctx, slot)
            mdone = sch.cur("mm")
            for cc in ccs:
                xr, jr = xring.load(fmv(res, cc, hs))
                ot = xout[cc % 2]
                W(ctx, "vector", "mm", mdone)
                sslot = "sA" if cc % 2 == 0 else "sB"
                W(ctx, "vector", sslot, sch.cur(sslot))
                pb = ctx.PB[cc % 8][:, 0:512]
                sch.on("vector", lambda v, o=ot, i=pb, sc=ctx.DV["shalf"][:, cc:cc + 1], x=xr:
                       v.scalar_tensor_tensor(o, i, sc, x, ALU.mult, ALU.add))
                bank_free(ctx, "vector", cc % 8)
                xring.consumed(jr)
                W(ctx, "sync", "ev", sch.cur("ev"))
                dma(ctx, "sync", sslot, fmv(dst, cc, hs), ot)
                if debug_y:
                    dma(ctx, "sync", sslot, fmv(dr["y"], cc, hs), ot)
    f32a.pop(); bfa.pop()
    f32a.pop(); bfa.pop()

    if do_stats:
        f32a.push(); bfa.push()
        xin2 = f32_scratch(ctx, NCC * 512, "xin2")
        ctx.rows = f32a.alloc(4 * 1024, "rows")
        colb = f32a.alloc(16, "colb")
        store_barrier(ctx, "sync")
        for g in range(2):
            Wcur(ctx, "sync", ["ev", "ac"])
            for cc in range(NCC):
                dma(ctx, "sync", "qS", xin2[:, cc * 512:(cc + 1) * 512],
                    fmv(dst, cc, slice(g * 512, (g + 1) * 512)))
            q = sch.cur("qS")
            W(ctx, "scalar", "qS", q)
            W(ctx, "vector", "qS", q)
            if g == 1:
                for cc in range(NCC):
                    if cc == NCC - 1:
                        sch.on("vector", lambda v, o=colb[:, cc:cc + 1],
                               i=xin2[:, cc * 512 + 511:cc * 512 + 512]:
                               v.tensor_copy(o, i))
                    else:
                        sch.on("vector", lambda v, o=colb[:, cc:cc + 1],
                               i=xin2[:, cc * 512 + 511:cc * 512 + 512]: v.tensor_copy(o, i))
                send_col_round(ctx, colb, "pay0", "gat0", "r0")
            ln_stats(ctx, lambda cc, _g: xin2[:, cc * 512:(cc + 1) * 512], ctx.bcB,
                     double_ln=(g == 1), halves=(g,), bcast=(g == 1))
        f32a.pop(); bfa.pop()


def send_col_round(ctx, colb, pay, gat, rsem):
    sch, sem, dr = ctx.sch, ctx.sem, ctx.dr
    Wcur(ctx, "gpsimd", ["ev"])
    qt2 = dma(ctx, "gpsimd", "qG", dr[pay], colb)
    W(ctx, "gpsimd", "qG", qt2)
    sch.inc(rsem)
    sch.on("gpsimd", lambda g, i=dr[pay], o=dr[gat], s=sem[rsem]:
           g.collective_compute("AllGather", ALU.bypass,
                                replica_groups=[list(range(NCORES))],
                                ins=[i], outs=[o]).then_inc(s, 1), chain=False)
    W(ctx, "vector", "qG", ctx.sch.cur("qG"))
    W(ctx, "scalar", "qG", ctx.sch.cur("qG"))


def recv_col(ctx, gat, rsem, dst_col, tmp16):
    sch, sem, dr = ctx.sch, ctx.sem, ctx.dr
    f32a = ctx.f32
    f32a.push()
    arena_fence(ctx)
    allb = f32a.alloc(8 * 16, "allb")
    a3 = allb.rearrange("p (j cc) -> p j cc", j=8)
    W(ctx, "sync", rsem, sch.cur(rsem))
    W(ctx, "sync", "qS", sch.cur("qS"))
    qg = dma(ctx, "sync", "qS", a3, dr[gat].rearrange("(j p) cc -> p j cc", j=8))
    W(ctx, "vector", "qS", qg)
    sch.on("vector", lambda v, o=dst_col, a=a3[:, 0], b=ctx.pmask[:, 0:1]:
           v.tensor_scalar(o, a, b, None, ALU.mult))
    for j in range(1, 8):
        sch.on("vector", lambda v, o=tmp16, a=a3[:, j], b=ctx.pmask[:, j:j + 1]:
               v.tensor_scalar(o, a, b, None, ALU.mult))
        sch.on("vector", lambda v, o=dst_col, a=dst_col, b=tmp16: v.tensor_add(o, a, b))
    f32a.pop()


def boundary_ln(ctx, xb, out_col, wname, bname, double_ln):
    sch, sem = ctx.sch, ctx.sem
    f32a, bfa = ctx.f32, ctx.bf
    f32a.push(); bfa.push()
    xbb = bfa.alloc(16, "xbb")
    xbs = bfa.alloc(16, "xbs")
    r0 = ctx.r8[0:1, 0:8]
    rb = f32a.alloc(4, "rbb")
    tmpc = f32a.alloc(16, "tmpc")
    Wcur(ctx, "scalar", ["ev"])
    a1 = sch.cur("ac") + 1
    sch.on("scalar", lambda s, o=xbb, i=xb: s.copy(o, i))
    e1 = sch.cur("ev") + 1
    sch.on("vector", lambda v, o=xbs, i=xb: v.tensor_mul(o, i, i))
    W(ctx, "tensor", "ac", a1)
    W(ctx, "tensor", "ev", e1)
    bank_wait(ctx, 6)
    sch.on("tensor", lambda t, o=ctx.PB[6][0:1, 0:16], l=ctx.ones_bf, r=xbb:
           t.matmul(o, l, r, start=True, stop=True))
    sch.inc("mm")
    m1 = sch.cur("mm")
    sch.on("tensor", lambda t, o=ctx.PB[6][0:1, 16:32], l=ctx.ones_bf, r=xbs, s=sem["mm"]:
           t.matmul(o, l, r, start=True, stop=True).then_inc(s, 1))
    W(ctx, "vector", "mm", m1)
    sch.on("vector", lambda v, o=r0[:, 0:1], i=ctx.PB[6][0:1, 0:16]: v.reduce_sum(o, i, axis=AX))
    sch.on("vector", lambda v, o=r0[:, 1:2], i=ctx.PB[6][0:1, 16:32]: v.reduce_sum(o, i, axis=AX))
    bank_free(ctx, "vector", 6)
    sch.on("vector", lambda v, o=r0[:, 0:1], i=r0[:, 0:1]: v.tensor_scalar(o, i, CINV, None, ALU.mult))
    sch.on("vector", lambda v, o=r0[:, 1:2], i=r0[:, 1:2]: v.tensor_scalar(o, i, CINV, None, ALU.mult))
    sch.on("vector", lambda v, o=r0[:, 2:3], a=r0[:, 0:1], b=r0[:, 0:1]: v.tensor_mul(o, a, b))
    e2 = sch.cur("ev") + 1
    sch.on("vector", lambda v, o=r0[:, 1:2], a=r0[:, 1:2], b=r0[:, 2:3]:
           v.tensor_sub(o, a, b))
    W(ctx, "scalar", "ev", e2)
    a2 = sch.cur("ac") + 1
    sch.on("scalar", lambda s, o=r0[:, 2:3], i=r0[:, 1:2]:
           s.activation(o, i, AF.Sqrt, bias=EPS))
    W(ctx, "vector", "ac", a2)
    sch.on("vector", lambda v, o=r0[:, 2:3], i=r0[:, 2:3]: v.reciprocal(o, i))
    if double_ln:
        sch.on("vector", lambda v, o=r0[:, 3:4], a=r0[:, 2:3], b=r0[:, 2:3]: v.tensor_mul(o, a, b))
        e3 = sch.cur("ev") + 1
        sch.on("vector", lambda v, o=r0[:, 3:4], a=r0[:, 1:2], b=r0[:, 3:4]:
               v.tensor_mul(o, a, b))
        W(ctx, "scalar", "ev", e3)
        a3 = sch.cur("ac") + 1
        sch.on("scalar", lambda s, o=r0[:, 3:4], i=r0[:, 3:4]:
               s.activation(o, i, AF.Sqrt, bias=EPS))
        W(ctx, "vector", "ac", a3)
        sch.on("vector", lambda v, o=r0[:, 3:4], i=r0[:, 3:4]: v.reciprocal(o, i))
        sch.on("vector", lambda v, o=r0[:, 2:3], a=r0[:, 2:3], b=r0[:, 3:4]: v.tensor_mul(o, a, b))
    e4 = sch.cur("ev")
    hi = ctx.hilo[0:1, 0:4]
    lo = ctx.hilo[0:1, 4:8]
    W(ctx, "scalar", "mm", sch.cur("mm"))
    W(ctx, "vector", "mm", sch.cur("mm"))
    W(ctx, "scalar", "ev", e4)
    a4 = sch.cur("ac") + 1
    sch.on("scalar", lambda s, o=hi, i=r0[:, 0:4]: s.copy(o, i))
    W(ctx, "vector", "ac", a4)
    e5 = sch.cur("ev") + 1
    sch.on("vector", lambda v, o=lo, a=r0[:, 0:4], b=hi: v.tensor_sub(o, a, b))
    W(ctx, "tensor", "ev", e5)
    bank_wait(ctx, 6)
    sch.on("tensor", lambda t, o=ctx.PB[6][:, 0:4], l=ctx.onesrow_bf, r=hi:
           t.matmul(o, l, r, start=True, stop=False))
    sch.inc("mm")
    m2 = sch.cur("mm")
    sch.on("tensor", lambda t, o=ctx.PB[6][:, 0:4], l=ctx.onesrow_bf, r=lo, s=sem["mm"]:
           t.matmul(o, l, r, start=False, stop=True).then_inc(s, 1))
    W(ctx, "vector", "mm", m2)
    sch.on("vector", lambda v, o=rb, i=ctx.PB[6][:, 0:4]: v.tensor_copy(o, i))
    bank_free(ctx, "vector", 6)
    sch.on("vector", lambda v, o=tmpc, i=xb, s1=rb[:, 0:1], s2=rb[:, 2:3]:
           v.tensor_scalar(o, i, s1, s2, ALU.subtract, ALU.mult))
    sch.on("vector", lambda v, o=tmpc, a=tmpc, b=ctx.VB(wname): v.tensor_mul(o, a, b))
    sch.on("vector", lambda v, o=tmpc, a=tmpc, b=ctx.VB(bname): v.tensor_add(o, a, b))
    sch.on("vector", lambda v, o=out_col, i=tmpc, s1=ctx.bmask[:, 0:1]:
           v.tensor_scalar(o, i, s1, None, ALU.mult))
    f32a.pop(); bfa.pop()



def attn_mix_stage(ctx, debug_y):
    sch, sem, dr = ctx.sch, ctx.sem, ctx.dr
    f32a, bfa = ctx.f32, ctx.bf
    f32a.push(); bfa.push()
    Hb = bfa.alloc(NCC * 1025, "H")
    h3 = Hb.rearrange("p (cc t) -> p cc t", cc=NCC)
    bfa.push()
    XK = bfa.alloc(NCC * T, "XK")
    XV = bfa.alloc(NCC * T, "XV")
    xk3 = XK.rearrange("p (cc t) -> p cc t", cc=NCC)
    xv3 = XV.rearrange("p (cc t) -> p cc t", cc=NCC)

    mu1 = ctx.bcB[:, 0:T]
    r1b = ctx.bcB[:, T:2 * T]
    rr1 = ctx.bcB[:, 2 * T:3 * T]

    f32a.push()
    arena_fence(ctx)
    hring = Ring(ctx, [f32a.alloc(512, "x1h") for _ in range(4)], ["qR0", "qR1", "qR2", "qR3"])
    tmp = f32a.alloc(1024, "tmpA")
    store_barrier(ctx, "sync")
    for g in range(2):
        cs = slice(g * 512, (g + 1) * 512)
        for cc in range(NCC):
            xb_, jh = hring.load(fmv(dr["x1sp"], cc, cs))
            sch.on("vector", lambda v, o=tmp[:, 0:512], a=xb_,
                   b=mu1[:, cs]: v.tensor_sub(o, a, b))
            sch.on("vector", lambda v, o=tmp[:, 0:512], a=tmp[:, 0:512], b=rr1[:, cs]:
                   v.tensor_mul(o, a, b))
            sch.on("vector", lambda v, o=h3[:, cc, 1 + g * 512:1 + (g + 1) * 512], i=tmp[:, 0:512],
                   s1=ctx.VB("ln1w")[:, cc:cc + 1], s2=ctx.VB("ln1b")[:, cc:cc + 1]:
                   v.tensor_scalar(o, i, s1, s2, ALU.mult, ALU.add))
            hring.consumed(jh)
    dlt = ctx.hilo[:, 0:1024]
    Wcur(ctx, "vector", ["ev"])
    for cc in range(NCC):
        for buf3, vec in ((xk3, "tmk"), (xv3, "tmv")):
            sch.on("vector", lambda v, o=dlt[:, 0:1023], a=h3[:, cc, 2:1025], b=h3[:, cc, 1:1024]:
                   v.tensor_sub(o, a, b))
            sch.on("vector", lambda v, o=dlt[:, 0:1023], i=dlt[:, 0:1023],
                   s1=ctx.VB(vec)[:, cc:cc + 1]: v.tensor_scalar(o, i, s1, None, ALU.mult))
            sch.on("vector", lambda v, o=buf3[:, cc, 1:1024], a=dlt[:, 0:1023],
                   b=h3[:, cc, 1:1024]: v.tensor_add(o, a, b))
    lerpA_done = sch.cur("ev")
    xb = f32a.alloc(16, "xb")
    hb = f32a.alloc(16, "hb")
    recv_col(ctx, "gat0", "r0", xb, tmp[:, 0:16])
    boundary_ln(ctx, xb, hb, "ln1w", "ln1b", double_ln=True)
    for cc in range(NCC):
        sch.on("vector", lambda v, o=h3[:, cc, 0:1], i=hb[:, cc:cc + 1]:
               v.tensor_copy(o, i))
    for cc in range(NCC):
        for buf3, vec in ((xk3, "tmk"), (xv3, "tmv")):
            sch.on("vector", lambda v, o=dlt[:, 0:1], a=h3[:, cc, 1:2], b=h3[:, cc, 0:1]:
                   v.tensor_sub(o, a, b))
            sch.on("vector", lambda v, o=dlt[:, 0:1], i=dlt[:, 0:1],
                   s1=ctx.VB(vec)[:, cc:cc + 1]: v.tensor_scalar(o, i, s1, None, ALU.mult))
            sch.on("vector", lambda v, o=buf3[:, cc, 0:1], a=dlt[:, 0:1], b=h3[:, cc, 0:1]:
                   v.tensor_add(o, a, b))
    lerps_done = sch.cur("ev")
    f32a.pop()

    bfa.push(); f32a.push()
    arena_fence(ctx)
    Kd1 = f32a.alloc(1024, "Kd")
    Vd2 = [f32a.alloc(1024, "Vd") for _ in range(2)]
    EB2 = [f32a.alloc(1024, "EBs") for _ in range(2)]
    EA = f32a.alloc(1024, "EAs")
    PAs = f32a.alloc(1025, "PAs")
    PBs = f32a.alloc(1025, "PBs")
    NUM = EA
    Dts = f32a.alloc(1024, "Dts")
    negm2 = f32a.alloc(2, "negm")
    scan_ev = {0: 0, 1: 0}
    gmax = {"last": 0}
    den_ev = {0: 0, 1: 0}
    sd_done = {0: 0, 1: 0}
    pending = None

    def emit_scan_rest(dc, par, EB, Vd, exp_ac):
        W(ctx, "vector", "ac", exp_ac)
        sch.on("vector", lambda v, o=Dts, i=EB, s1=ctx.DV["d"][:, dc:dc + 1]:
               v.tensor_scalar(o, i, 0.0, s1, ALU.mult, ALU.add))
        sch.on("vector", lambda v, o=EA, a=EB, b=Vd: v.tensor_mul(o, a, b))
        scan_ev[par] = sch.cur("ev")
        sch.on("vector", lambda v, o=PAs[:, 0:1]: v.memset(o, 0.0))
        sch.on("vector", lambda v, o=PBs[:, 0:1]: v.memset(o, 0.0))
        sch.on("vector", lambda v, o=PAs[:, 1:1025], d0=Dts, d1=EA:
               v.tensor_tensor_scan(o, d0, d1, 0.0, ALU.mult, ALU.add))
        sch.on("vector", lambda v, o=PBs[:, 1:1025], d0=Dts, d1=EB:
               v.tensor_tensor_scan(o, d0, d1, 0.0, ALU.mult, ALU.add))
        sch.on("vector", lambda v, o=NUM, i=EA, s1=ctx.DV["eu"][:, dc:dc + 1], x=PAs[:, 0:1024]:
               v.scalar_tensor_tensor(o, i, s1, x, ALU.mult, ALU.add))
        sch.on("vector", lambda v, o=EB, i=EB, s1=ctx.DV["eu"][:, dc:dc + 1], x=PBs[:, 0:1024]:
               v.scalar_tensor_tensor(o, i, s1, x, ALU.mult, ALU.add))
        den_ev[par] = sch.cur("ev")
        sch.on("vector", lambda v, o=ctx.sumstg[:, dc:dc + 1], i=PAs[:, 1024:1025]:
               v.tensor_copy(o, i))
        sch.on("vector", lambda v, o=ctx.sumstg[:, 16 + dc:17 + dc], i=PBs[:, 1024:1025]:
               v.tensor_copy(o, i))
        W(ctx, "sync", "ev", sch.cur("ev"))
        dma(ctx, "sync", "sC", dr["numsp"].rearrange("(dc p) t -> p dc t", p=P)[:, dc], NUM)
        dma(ctx, "sync", "sD", dr["densp"].rearrange("(dc p) t -> p dc t", p=P)[:, dc], EB)
        W(ctx, "vector", "sC", sch.cur("sC"))
        W(ctx, "vector", "sD", sch.cur("sD"))
        sd_done[par] = sch.cur("sD")

    for dw in range(8):
        slotk = ("qA", "qB")[dw % 2]
        slotv = ("qC", "qD")[dw % 2]
        bufk = ctx.pool[slotk]
        bufv = ctx.pool[slotv]
        qk = slab_dma(ctx, slotk,
                      bufk.rearrange("p (kc m) -> p kc m", kc=NCC), dr["wkT"][dw])
        qv = slab_dma(ctx, slotv,
                      bufv.rearrange("p (kc m) -> p kc m", kc=NCC), dr["wvT"][dw])
        W(ctx, "tensor", slotk, qk)
        W(ctx, "tensor", slotv, qv)
        b3k = bufk.rearrange("p (kc m) -> p kc m", kc=NCC)
        b3v = bufv.rearrange("p (kc m) -> p kc m", kc=NCC)
        for di in range(2):
            dc = dw * 2 + di
            par = dc % 2
            Kd = Kd1
            Vd = Vd2[par]
            for b3, src3, dst in ((b3k, xk3, Kd), (b3v, xv3, Vd)):
                for g in (1, 0):
                    W(ctx, "tensor", "ev", lerpA_done if g == 1 else lerps_done)
                    b = ctx.banks.acquire()
                    for kc in range(NCC):
                        last = (kc == NCC - 1)
                        if last:
                            sch.inc("mm")
                        sch.on("tensor", lambda t, o=ctx.PB[b][:, 0:512],
                               l=b3[:, kc, di * 128:(di + 1) * 128],
                               r=src3[:, kc, g * 512:(g + 1) * 512],
                               st=(kc == 0), sp=last, fin=last, s=sem["mm"]:
                               (t.matmul(o, l, r, start=st, stop=sp).then_inc(s, 1) if fin
                                else t.matmul(o, l, r, start=st, stop=sp)))
                    mt = sch.cur("mm")
                    W(ctx, "scalar", "mm", mt)
                    if dst is Kd:
                        W(ctx, "scalar", "ev", gmax["last"])
                    else:
                        W(ctx, "scalar", "ev", scan_ev[par])
                    sch.on("scalar", lambda s, o=dst[:, g * 512:(g + 1) * 512], i=ctx.PB[b][:, 0:512]:
                           s.copy(o, i))
                    ctx.banks.free("scalar", b)
            EB = EB2[par]
            W(ctx, "vector", "ac", sch.cur("ac"))
            sch.on("vector", lambda v, o=ctx.mall[:, dc:dc + 1], i=Kd: v.reduce_max(o, i, axis=AX))
            sch.on("vector", lambda v, o=negm2[:, par:par + 1], i=ctx.mall[:, dc:dc + 1]:
                   v.tensor_scalar(o, i, -1.0, None, ALU.mult))
            gmax["last"] = sch.cur("ev")
            W(ctx, "scalar", "ev", max(gmax["last"], den_ev[par]))
            W(ctx, "scalar", "sD", sch.cur("sD"))
            a2 = sch.cur("ac") + 1
            sch.on("scalar", lambda s, o=EB, i=Kd, bi=negm2[:, par:par + 1]:
                   s.activation(o, i, AF.Exp, bias=bi))
            if pending is not None:
                emit_scan_rest(*pending)
            pending = (dc, par, EB, Vd, a2)
        slab_mark(ctx, slotk)
        slab_mark(ctx, slotv)
    emit_scan_rest(*pending)

    sch.on("vector", lambda v, o=ctx.sumstg[:, 32:48], i=ctx.mall[:, 0:16]: v.tensor_copy(o, i))
    sch.on("vector", lambda v, o=ctx.sumstg[:, 48:64], i=ctx.DV["lw"]:
           v.tensor_copy(o, i))

    sr_qts = {}

    def sr_issue(wi):
        slot = ("qA", "qB", "qC")[wi % 3]
        buf = ctx.pool[slot]
        sr_qts[wi] = (slot, buf, slab_dma(ctx, slot,
                      buf.rearrange("p (kc m) -> p kc m", kc=NCC), dr["wrT"][wi]))

    for wi in range(3):
        sr_issue(wi)

    payt = f32a.alloc(128, "payt")
    hi = ctx.hilo[:, 0:64]
    lo = ctx.hilo[:, 64:128]
    Wcur(ctx, "scalar", ["ev", "mm"])
    Wcur(ctx, "vector", ["mm"])
    a3 = sch.cur("ac") + 1
    sch.on("scalar", lambda s, o=hi, i=ctx.sumstg: s.copy(o, i))
    W(ctx, "vector", "ac", a3)
    e5 = sch.cur("ev") + 1
    sch.on("vector", lambda v, o=lo, a=ctx.sumstg, b=hi: v.tensor_sub(o, a, b))
    W(ctx, "tensor", "ev", e5)
    bank_wait(ctx, 6)
    sch.on("tensor", lambda t, o=ctx.PB[6][0:64, 0:128], l=hi, r=ctx.identb:
           t.matmul(o, l, r, start=True, stop=False))
    sch.inc("mm")
    m3 = sch.cur("mm")
    sch.on("tensor", lambda t, o=ctx.PB[6][0:64, 0:128], l=lo, r=ctx.identb, s=sem["mm"]:
           t.matmul(o, l, r, start=False, stop=True).then_inc(s, 1))
    W(ctx, "vector", "mm", m3)
    e6 = sch.cur("ev") + 1
    sch.on("vector", lambda v, o=payt[0:64, :], i=ctx.PB[6][0:64, 0:128]:
           v.tensor_copy(o, i))
    bank_free(ctx, "vector", 6)
    W(ctx, "sync", "ev", e6)
    W(ctx, "sync", "qS", sch.cur("qS"))
    qp = dma(ctx, "sync", "qS", dr["pay1"], payt[0:64, :])
    W(ctx, "gpsimd", "qS", qp)
    sch.inc("r1")
    sch.on("gpsimd", lambda g, i=dr["pay1"], o=dr["gat1"], s=sem["r1"]:
           g.collective_compute("AllGather", ALU.bypass,
                                replica_groups=[list(range(NCORES))],
                                ins=[i], outs=[o]).then_inc(s, 1), chain=False)
    f32a.pop()
    bfa.pop()
    bfa.pop()

    bfa.push()
    arena_fence(ctx)
    XR = bfa.alloc(NCC * T, "XR")
    xr3 = XR.rearrange("p (cc t) -> p cc t", cc=NCC)
    SR = bfa.alloc(NCC * T, "SR")
    dlt2 = ctx.hilo[:, 0:1024]
    Wcur(ctx, "vector", ["ev"])
    for cc in range(NCC):
        sch.on("vector", lambda v, o=dlt2, a=h3[:, cc, 1:1025], b=h3[:, cc, 0:1024]:
               v.tensor_sub(o, a, b))
        sch.on("vector", lambda v, o=dlt2, i=dlt2, s1=ctx.VB("tmr")[:, cc:cc + 1]:
               v.tensor_scalar(o, i, s1, None, ALU.mult))
        sch.on("vector", lambda v, o=xr3[:, cc], a=dlt2, b=h3[:, cc, 0:1024]:
               v.tensor_add(o, a, b))
    W(ctx, "tensor", "ev", sch.cur("ev"))

    f32a.push()
    arena_fence(ctx)
    SAt = f32a.alloc(8 * 128, "SAt")
    SBt = f32a.alloc(8 * 128, "SBt")
    Mt = f32a.alloc(8 * 128, "Mt")
    LWt = f32a.alloc(8 * 128, "LWt")
    CF = f32a.alloc(8 * 128, "CF")
    lagT = f32a.alloc(8 * 128, "lagT")
    W(ctx, "sync", "r1", sch.cur("r1"))
    W(ctx, "sync", "qS", sch.cur("qS"))
    dma(ctx, "sync", "qS", lagT[0:16, :], dr["lag"])
    g1 = dr["gat1"].rearrange("(j r dc) q -> r dc j q", j=8, r=4)
    for r, dstt in ((0, SAt), (1, SBt), (2, Mt), (3, LWt)):
        dma(ctx, "sync", "qS", dstt[0:16, :].rearrange("d (j q) -> d j q", j=8), g1[r])
    W(ctx, "vector", "qS", sch.cur("qS"))
    sch.on("vector", lambda v, o=CF[0:16, :], a=LWt[0:16, :], b=lagT[0:16, :]: v.tensor_mul(o, a, b))
    e7 = sch.cur("ev") + 1
    sch.on("vector", lambda v, o=CF[0:16, :], a=CF[0:16, :], b=Mt[0:16, :]:
           v.tensor_add(o, a, b))
    W(ctx, "scalar", "ev", e7)
    a5 = sch.cur("ac") + 1
    sch.on("scalar", lambda s, o=CF[0:16, :], i=CF[0:16, :]:
           s.activation(o, i, AF.Exp))
    W(ctx, "vector", "ac", a5)
    sch.on("vector", lambda v, o=SAt[0:16, :], a=CF[0:16, :], b=SAt[0:16, :]: v.tensor_mul(o, a, b))
    sch.on("vector", lambda v, o=SBt[0:16, :], a=CF[0:16, :], b=SBt[0:16, :]: v.tensor_mul(o, a, b))
    for dstt in (SAt, SBt):
        for j in range(1, 8):
            sch.on("vector", lambda v, o=dstt[0:16, 0:128], a=dstt[0:16, 0:128],
                   b=dstt[0:16, j * 128:(j + 1) * 128]: v.tensor_add(o, a, b))
    e8 = sch.cur("ev")
    for srcr, dstf in ((SAt, ctx.cafm), (SBt, ctx.cbfm)):
        hi2 = ctx.hilo[0:16, 128:256]
        lo2 = ctx.hilo[0:16, 256:384]
        W(ctx, "scalar", "mm", sch.cur("mm"))
        W(ctx, "vector", "mm", sch.cur("mm"))
        W(ctx, "scalar", "ev", e8)
        a6 = sch.cur("ac") + 1
        sch.on("scalar", lambda s, o=hi2, i=srcr[0:16, 0:128]: s.copy(o, i))
        W(ctx, "vector", "ac", a6)
        e9 = sch.cur("ev") + 1
        sch.on("vector", lambda v, o=lo2, a=srcr[0:16, 0:128], b=hi2:
               v.tensor_sub(o, a, b))
        W(ctx, "tensor", "ev", e9)
        bank_wait(ctx, 6)
        sch.on("tensor", lambda t, o=ctx.PB[6][:, 0:16], l=hi2, r=ctx.identb[0:16, 0:16]:
               t.matmul(o, l, r, start=True, stop=False))
        sch.inc("mm")
        m4 = sch.cur("mm")
        sch.on("tensor", lambda t, o=ctx.PB[6][:, 0:16], l=lo2, r=ctx.identb[0:16, 0:16], s=sem["mm"]:
               t.matmul(o, l, r, start=False, stop=True).then_inc(s, 1))
        W(ctx, "vector", "mm", m4)
        sch.on("vector", lambda v, o=dstf, i=ctx.PB[6][:, 0:16]:
               v.tensor_copy(o, i))
        bank_free(ctx, "vector", 6)
    f32a.pop()

    f32a.push()
    nrc = [f32a.alloc(1024, "nrc") for _ in range(2)]
    drc = [f32a.alloc(1024, "drc") for _ in range(2)]
    ALPH = f32a.alloc(1024, "ALPH")
    store_barrier(ctx, "sync")
    comb_ev = sch.cur("ev")
    comb_ac = sch.cur("ac")
    Wcur(ctx, "vector", ["pl"])
    for wi in range(8):
        slot, buf, qt = sr_qts[wi]
        W(ctx, "tensor", slot, qt)
        b3 = buf.rearrange("p (kc m) -> p kc m", kc=NCC)
        for mi in range(2):
            mc = wi * 2 + mi
            for g in range(2):
                b = ctx.banks.acquire()
                for kc in range(NCC):
                    last = (kc == NCC - 1)
                    if last:
                        sch.inc("mm")
                    sch.on("tensor", lambda t, o=ctx.PB[b][:, 0:512],
                           l=b3[:, kc, mi * 128:(mi + 1) * 128],
                           r=xr3[:, kc, g * 512:(g + 1) * 512],
                           st=(kc == 0), sp=last, fin=last, s=sem["mm"]:
                           (t.matmul(o, l, r, start=st, stop=sp).then_inc(s, 1) if fin
                            else t.matmul(o, l, r, start=st, stop=sp)))
                mt = sch.cur("mm")
                W(ctx, "scalar", "mm", mt)
                sch.on("scalar", lambda s, o=SR[:, mc * T + g * 512: mc * T + (g + 1) * 512],
                       i=ctx.PB[b][:, 0:512]:
                       s.activation(o, i, AF.Sigmoid))
                ctx.banks.free("scalar", b)
        slab_mark(ctx, slot)
        if wi + 3 < 8:
            sr_issue(wi + 3)
        for dc in (wi * 2, wi * 2 + 1):
            W(ctx, "sync", "ev", sch.cur("ev"))
            W(ctx, "sync", "qS", sch.cur("qS"))
            dma(ctx, "sync", "qS", nrc[dc % 2], dr["numsp"].rearrange("(dc p) t -> p dc t", p=P)[:, dc])
            dma(ctx, "sync", "qS", drc[dc % 2], dr["densp"].rearrange("(dc p) t -> p dc t", p=P)[:, dc])
            qn = sch.cur("qS")
            W(ctx, "vector", "qS", qn)
            if dc == 0:
                W(ctx, "vector", "ev", comb_ev)
                W(ctx, "vector", "ac", comb_ac)
            sch.on("vector", lambda v, o=ALPH, i=ctx.ramp, s1=ctx.DV["w"][:, dc:dc + 1],
                   s2=ctx.mall[:, dc:dc + 1]: v.tensor_scalar(o, i, s1, s2, ALU.mult, ALU.subtract))
            e10 = sch.cur("ev")
            W(ctx, "scalar", "ev", e10)
            a7 = sch.cur("ac") + 1
            sch.on("scalar", lambda s, o=ALPH, i=ALPH: s.activation(o, i, AF.Exp))
            W(ctx, "vector", "ac", a7)
            sch.on("vector", lambda v, o=nrc[dc % 2], i=ALPH, s1=ctx.cafm[:, dc:dc + 1], x=nrc[dc % 2]:
                   v.scalar_tensor_tensor(o, i, s1, x, ALU.mult, ALU.add))
            sch.on("vector", lambda v, o=drc[dc % 2], i=ALPH, s1=ctx.cbfm[:, dc:dc + 1], x=drc[dc % 2]:
                   v.scalar_tensor_tensor(o, i, s1, x, ALU.mult, ALU.add))
            sch.on("vector", lambda v, o=drc[dc % 2], i=drc[dc % 2]: v.reciprocal(o, i))
            sch.on("vector", lambda v, o=nrc[dc % 2], a=nrc[dc % 2], b=drc[dc % 2]: v.tensor_mul(o, a, b))
            sch.on("vector", lambda v, o=SR[:, dc * T:(dc + 1) * T], a=nrc[dc % 2],
                   b=SR[:, dc * T:(dc + 1) * T]: v.tensor_mul(o, a, b))
    rwkv_done = sch.cur("ev")
    f32a.pop()

    f32a.push()
    arena_fence(ctx)
    rw3 = SR.rearrange("p (dc t) -> p dc t", dc=NCC)
    z1t = f32a.alloc(512, "z1t")
    x1q = f32a.alloc(512, "x1q")
    W(ctx, "tensor", "ev", rwkv_done)

    def wo_emit(mc, g, b, mt):
        cs = slice(g * 512, (g + 1) * 512)
        Wcur(ctx, "sync", ["ev"])
        W(ctx, "sync", "qS", sch.cur("qS"))
        qx = dma(ctx, "sync", "qS", x1q, fmv(dr["x1sp"], mc, cs))
        W(ctx, "vector", "qS", qx)
        W(ctx, "vector", "mm", mt)
        W(ctx, "vector", "sC", sch.cur("sC"))
        sch.on("vector", lambda v, o=z1t, a=x1q, b=mu1[:, cs]: v.tensor_sub(o, a, b))
        sch.on("vector", lambda v, o=z1t, a=z1t, b=r1b[:, cs]: v.tensor_mul(o, a, b))
        sch.on("vector", lambda v, o=z1t, a=z1t, b=ctx.PB[b][:, 0:512]:
               v.tensor_add(o, a, b))
        ctx.banks.free("vector", b)
        W(ctx, "sync", "ev", sch.cur("ev"))
        dma(ctx, "sync", "sC", fmv(dr["z1sp"], mc, cs), z1t)
        if debug_y:
            dma(ctx, "sync", "sC", fmv(dr["y"], mc, cs), z1t)

    proj_stage(ctx, dr["woT"], rw3, wo_emit, windows=8, win=256, sfn="sf3")
    f32a.pop()
    bfa.pop()
    bfa.pop()

    f32a.push(); bfa.push()
    arena_fence(ctx)
    ctx.rows = f32a.alloc(4 * 1024, "rows")
    colb = f32a.alloc(16, "colb2")
    lring = Ring(ctx, [f32a.alloc(512, "z1re") for _ in range(4)], ["qR0", "qR1", "qR2", "qR3"])
    store_barrier(ctx, "sync")
    lpend = [None]

    def ln2_xin(cc, g):
        if lpend[0] is not None:
            lring.consumed(lpend[0])
        buf, j = lring.load(fmv(dr["z1sp"], cc, slice(g * 512, (g + 1) * 512)),
                            engs=("vector", "scalar"))
        lpend[0] = j
        if g == 1:
            sch.on("vector", lambda v, o=colb[:, cc:cc + 1], i=buf[:, 511:512]:
                   v.tensor_copy(o, i))
        return buf

    for g in range(2):
        ln_stats(ctx, ln2_xin, ctx.bcA, halves=(g,), bcast=(g == 1))
        lring.consumed(lpend[0])
        lpend[0] = None
        if g == 1:
            send_col_round(ctx, colb, "pay2", "gat2", "r2")
    f32a.pop(); bfa.pop()
    f32a.pop()



def chmix_stage(ctx, debug_y):
    sch, sem, dr = ctx.sch, ctx.sem, ctx.dr
    f32a, bfa = ctx.f32, ctx.bf
    f32a.push(); bfa.push()
    XK2 = bfa.alloc(NCC * T, "XK2")
    xk23 = XK2.rearrange("p (cc t) -> p cc t", cc=NCC)
    bfa.push()
    XR2 = bfa.alloc(NCC * T, "XR2")
    xr23 = XR2.rearrange("p (cc t) -> p cc t", cc=NCC)

    mu2 = ctx.bcA[:, 0:T]
    r2b = ctx.bcA[:, T:2 * T]

    bfa.push()
    Gb = bfa.alloc(NCC * 1025, "G")
    g3 = Gb.rearrange("p (cc t) -> p cc t", cc=NCC)
    f32a.push()
    arena_fence(ctx)
    zring = Ring(ctx, [f32a.alloc(512, "z1h") for _ in range(4)], ["qR0", "qR1", "qR2", "qR3"])
    tmp = f32a.alloc(1024, "tmpC")
    store_barrier(ctx, "sync")
    for g in range(2):
        cs = slice(g * 512, (g + 1) * 512)
        for cc in range(NCC):
            zb_, jz = zring.load(fmv(dr["z1sp"], cc, cs))
            sch.on("vector", lambda v, o=tmp[:, 0:512], a=zb_,
                   b=mu2[:, cs]: v.tensor_sub(o, a, b))
            sch.on("vector", lambda v, o=tmp[:, 0:512], a=tmp[:, 0:512], b=r2b[:, cs]:
                   v.tensor_mul(o, a, b))
            sch.on("vector", lambda v, o=g3[:, cc, 1 + g * 512:1 + (g + 1) * 512], i=tmp[:, 0:512],
                   s1=ctx.VB("ln2w")[:, cc:cc + 1], s2=ctx.VB("ln2b")[:, cc:cc + 1]:
                   v.tensor_scalar(o, i, s1, s2, ALU.mult, ALU.add))
            zring.consumed(jz)
    zb = f32a.alloc(16, "zb")
    gb = f32a.alloc(16, "gb")
    recv_col(ctx, "gat2", "r2", zb, tmp[:, 0:16])
    boundary_ln(ctx, zb, gb, "ln2w", "ln2b", double_ln=False)
    W(ctx, "vector", "ev", sch.cur("ev"))
    for cc in range(NCC):
        sch.on("vector", lambda v, o=g3[:, cc, 0:1], i=gb[:, cc:cc + 1]:
               v.tensor_copy(o, i))
    dlt = ctx.hilo[:, 0:1024]
    Wcur(ctx, "vector", ["ev"])
    for cc in range(NCC):
        for buf3, vec in ((xk23, "cmk"), (xr23, "cmr")):
            sch.on("vector", lambda v, o=dlt, a=g3[:, cc, 1:1025], b=g3[:, cc, 0:1024]:
                   v.tensor_sub(o, a, b))
            sch.on("vector", lambda v, o=dlt, i=dlt, s1=ctx.VB(vec)[:, cc:cc + 1]:
                   v.tensor_scalar(o, i, s1, None, ALU.mult))
            sch.on("vector", lambda v, o=buf3[:, cc], a=dlt, b=g3[:, cc, 0:1024]:
                   v.tensor_add(o, a, b))
    lerp2 = sch.cur("ev")
    f32a.pop()
    bfa.pop()

    bfa.push()
    arena_fence(ctx)
    sgt = bfa.alloc(512, "sigt")
    W(ctx, "tensor", "ev", lerp2)

    def sig_emit(mc, g, b, mt):
        cs = slice(g * 512, (g + 1) * 512)
        W(ctx, "scalar", "mm", mt)
        W(ctx, "scalar", "sC", sch.cur("sC"))
        sch.on("scalar", lambda s, o=sgt, i=ctx.PB[b][:, 0:512]:
               s.activation(o, i, AF.Sigmoid))
        ctx.banks.free("scalar", b)
        W(ctx, "sync", "ac", sch.cur("ac"))
        dma(ctx, "sync", "sC", dr["sigsp"].rearrange("(cc p) t -> p cc t", p=P)[:, mc, cs], sgt)

    proj_stage(ctx, dr["wcrT"], xr23, sig_emit, windows=8, win=256, sfn="sf3")
    bfa.pop()
    bfa.pop()

    bfa.push()
    arena_fence(ctx)
    KF = bfa.alloc(NFC * HALF, "KF")
    kf3 = KF.rearrange("p (fc t) -> p fc t", fc=NFC)
    rel = ctx.f32.alloc(512, "relscr")
    sgr = Ring(ctx, [bfa.alloc(HALF, "sgq") for _ in range(2)], ["qR6", "qR7"])
    f32a.push()
    rx1 = Ring(ctx, [f32a.alloc(HALF, "x1q") for _ in range(3)], ["qR0", "qR1", "qR2"])
    rz1 = Ring(ctx, [f32a.alloc(HALF, "z1q") for _ in range(3)], ["qR3", "qR4", "qR5"])
    x2t = [f32a.alloc(HALF, "x2t") for _ in range(2)]
    W(ctx, "sync", "sC", sch.cur("sC"))

    for h in range(2):
        hs = slice(h * HALF, (h + 1) * HALF)

        def kf_emit(mc, g_unused, b, mt, h=h):
            W(ctx, "scalar", "mm", mt)
            W(ctx, "scalar", "ev", sch.cur("ev"))
            a1 = sch.cur("ac") + 1
            sch.on("scalar", lambda s, o=rel, i=ctx.PB[b][:, 0:512]:
                   s.activation(o, i, AF.Relu))
            ctx.banks.free("scalar", b)
            W(ctx, "vector", "ac", a1)
            sch.on("vector", lambda v, o=kf3[:, mc, 0:HALF], a=rel, b=rel:
                   v.tensor_mul(o, a, b))

        xk_h = XK2.rearrange("p (cc t) -> p cc t", cc=NCC)

        def kf_rhs3(_):
            return None
        for fw in range(NFC):
            slot = ("qA", "qB", "qC")[fw % 3]
            buf = ctx.pool[slot][:, 0:NCC * 128]
            qt = slab_dma(ctx, slot,
                          buf.rearrange("p (kc m) -> p kc m", kc=NCC), dr["wckT"][fw])
            W(ctx, "tensor", slot, qt)
            b3 = buf.rearrange("p (kc m) -> p kc m", kc=NCC)
            for fi in range(1):
                fc = fw
                b = ctx.banks.acquire()
                for kc in range(NCC):
                    last = (kc == NCC - 1)
                    if last:
                        sch.inc("mm")
                    sch.on("tensor", lambda t, o=ctx.PB[b][:, 0:512],
                           l=b3[:, kc, 0:128],
                           r=xk_h[:, kc, hs], st=(kc == 0), sp=last, fin=last, s=sem["mm"]:
                           (t.matmul(o, l, r, start=st, stop=sp).then_inc(s, 1) if fin
                            else t.matmul(o, l, r, start=st, stop=sp)))
                kf_emit(fc, 0, b, sch.cur("mm"))
            slab_mark(ctx, slot)
        kf_done = sch.cur("ev")

        for cg in range(2):
            ccs = range(cg * 8, (cg + 1) * 8)
            W(ctx, "tensor", "ev", kf_done)
            for fc in range(NFC):
                slot = ("qD", "qE", "qF")[fc % 3]
                buf = ctx.pool[slot][:, 0:1024]
                qt = slab_dma(ctx, slot, buf,
                              fmv(dr["wcvT"], fc)[:, cg * 1024:(cg + 1) * 1024])
                W(ctx, "tensor", slot, qt)
                for cc in ccs:
                    pb = ctx.PB[cc % 8][:, 0:512]
                    last = (cc == ccs[-1])
                    if fc == 0:
                        bank_wait(ctx, cc % 8)
                    if last:
                        sch.inc("mm")
                    sch.on("tensor", lambda t, o=pb,
                           l=buf[:, (cc - cg * 8) * 128:(cc - cg * 8 + 1) * 128],
                           r=kf3[:, fc, 0:HALF], st=(fc == 0), sp=(fc == NFC - 1), fin=last, s=sem["mm"]:
                           (t.matmul(o, l, r, start=st, stop=sp).then_inc(s, 1) if fin
                            else t.matmul(o, l, r, start=st, stop=sp)))
                slab_mark(ctx, slot)
            mdone = sch.cur("mm")
            for cc in ccs:
                x1b, j1 = rx1.load(fmv(dr["x1sp"], cc, hs))
                z1b, j2 = rz1.load(fmv(dr["z1sp"], cc, hs))
                sgb, j3 = sgr.load(dr["sigsp"].rearrange("(cc p) t -> p cc t", p=P)[:, cc, hs])
                W(ctx, "vector", "mm", mdone)
                sslot = "sA" if cc % 2 == 0 else "sB"
                W(ctx, "vector", sslot, sch.cur(sslot))
                pb = ctx.PB[cc % 8][:, 0:512]
                ot = x2t[cc % 2]
                sch.on("vector", lambda v, o=ot, a=sgb, b=pb: v.tensor_mul(o, a, b))
                bank_free(ctx, "vector", cc % 8)
                sch.on("vector", lambda v, o=ot, a=ot, b=x1b: v.tensor_add(o, a, b))
                sch.on("vector", lambda v, o=ot, a=ot, b=z1b: v.tensor_add(o, a, b))
                for j, rr in ((j1, rx1), (j2, rz1), (j3, sgr)):
                    rr.consumed(j)
                W(ctx, "sync", "ev", sch.cur("ev"))
                dma(ctx, "sync", sslot, fmv(dr["x2sp"], cc, hs), ot)
                if debug_y:
                    dma(ctx, "sync", sslot, fmv(dr["y"], cc, hs), ot)
    f32a.pop()
    bfa.pop()
    bfa.pop()
    f32a.pop()



def kernel(**inputs):
    upto = inputs.pop("_upto", UPTO)
    d = inputs
    xT = np.ascontiguousarray(np.asarray(d["L0_attn_x"])[0].T.astype(np.float32))

    def tr(a):
        return np.ascontiguousarray(np.asarray(a, dtype=np.float32).T)

    def tile4(w2d, win):
        m = w2d.shape[1]
        return np.ascontiguousarray(
            w2d.reshape(NCC, P, m // win, win).transpose(2, 1, 0, 3))

    wkT = tile4(tr(d["L0_attn_Wk"]), 256)
    wvT = tile4(tr(d["L0_attn_Wv"]), 256)
    wrT = tile4(tr(d["L0_attn_Wr"]), 256)
    woT = tile4(tr(d["L0_attn_Wo"]), 256)
    wcrT = tile4(tr(d["L0_attn_Wcr"]), 256)
    wckT = tile4(tr(d["L0_attn_Wck"]), 128)
    wcvT = tr(d["L0_attn_Wcv"])
    wgate = tile4(np.ascontiguousarray(np.asarray(d["L0_ffn_w_gate"], np.int8)), 128)
    wup = tile4(np.ascontiguousarray(np.asarray(d["L0_ffn_w_up"], np.int8)), 128)
    wdown = np.ascontiguousarray(np.asarray(d["L0_ffn_w_down"], np.int8))

    def fm(v):
        return np.ascontiguousarray(np.asarray(v, np.float32).reshape(16, 128).T)

    vecs = [d["L0_attn_ln1_w"], d["L0_attn_ln1_b"], d["L0_attn_ln2_w"], d["L0_attn_ln2_b"],
            np.asarray(d["L0_attn_tm_k"]).reshape(-1), np.asarray(d["L0_attn_tm_v"]).reshape(-1),
            np.asarray(d["L0_attn_tm_r"]).reshape(-1),
            np.asarray(d["L0_attn_cm_k"]).reshape(-1), np.asarray(d["L0_attn_cm_r"]).reshape(-1),
            d["L0_attn_time_decay"], d["L0_attn_time_first"], d["L0_ffn_s_down"]]
    vecbank = np.hstack([fm(v) for v in vecs]).astype(np.float32)
    fbank = np.hstack([np.ascontiguousarray(np.asarray(v, np.float32).reshape(64, 128).T)
                       for v in (d["L0_ffn_s_gate"], d["L0_ffn_s_up"])]).astype(np.float32)
    ident = np.eye(P, dtype=np.float32)

    in_maps = []
    for c in range(NCORES):
        lag = np.full((16, NCORES * P), 1e6, dtype=np.float32)
        for j in range(c):
            lag[:, j * P:(j + 1) * P] = float(c - 1 - j)
        bm = np.full((P, 1), 0.0 if c == 0 else 1.0, dtype=np.float32)
        pm = np.zeros((P, 8), dtype=np.float32)
        if c > 0:
            pm[:, c - 1] = 1.0
        m = {"x": np.ascontiguousarray(xT[:, c * T:(c + 1) * T]),
             "vecbank": vecbank, "fbank": fbank, "ident": ident,
             "lag": lag, "bmask": bm, "pmask": pm,
             "wgate": wgate, "wup": wup, "wdown": wdown,
             "wkT": wkT, "wvT": wvT, "wrT": wrT, "woT": woT, "wcrT": wcrT,
             "wckT": wckT, "wcvT": wcvT}
        in_maps.append(m)

    nc = build_nc(upto=upto)
    res = run_bass_kernel_spmd(nc, in_maps, core_ids=list(range(NCORES)))
    out = np.empty((8192, 2048), dtype=np.float32)
    for c in range(NCORES):
        out[c * T:(c + 1) * T, :] = res.results[c]["y"].T
    return out.reshape(1, 8192, 2048)
